# version 14
# baseline (speedup 1.0000x reference)
import contextlib
import math
import os
import numpy as np
import ml_dtypes
import concourse.bass as bass
import concourse.mybir as mybir
from concourse.bass_utils import run_bass_kernel_spmd
from concourse.bass import IndirectOffsetOnAxis

F32 = mybir.dt.float32
BF16 = mybir.dt.bfloat16
I32 = mybir.dt.int32
AF = mybir.ActivationFunctionType
ALU = mybir.AluOpType
AX = mybir.AxisListType

N = 2048
C = 1024
NH = 16
HD = 64
NE = 8
NI = 2048
KSEL = 1024
NT = N // 128
CC = C // 128
TWO_PI = 2.0 * math.pi
TWO_PI_HI = float(np.float32(TWO_PI))
TWO_PI_LO = float(np.float64(TWO_PI) - np.float64(np.float32(TWO_PI)))
INV_2PI = 1.0 / TWO_PI
HALF_PI = math.pi / 2.0
SLOPES = [2.0 ** (-0.5 * (i + 1)) for i in range(NH)]
SCL2 = 1.0 / 2048.0
LN1E4 = math.log(10000.0)

ENGINES = ("tensor", "vector", "scalar", "gpsimd", "sync")
DMA_QUEUES = ("sync", "scalar", "gpsimd")


class Prog:
    DMA_POOL = 8

    def __init__(self):
        self.queues = {e: [] for e in ENGINES}
        self.count = {}
        self.sem_names = []
        for e in ENGINES:
            self._add_sem(e)
        for e in DMA_QUEUES:
            for i in range(self.DMA_POOL):
                self._add_sem(f"{e}_dma{i}")
        self.dma_rr = {e: 0 for e in DMA_QUEUES}
        self.waited = {e: {} for e in ENGINES}
        self.writer = {}
        self.readers = {}

    def _add_sem(self, name):
        self.count[name] = 0
        self.sem_names.append(name)

    def _dep(self, deps, sem, cnt):
        if cnt > deps.get(sem, 0):
            deps[sem] = cnt

    def emit(self, eng, fn, reads=(), writes=(), dma=False):
        if dma:
            i = self.dma_rr[eng]
            self.dma_rr[eng] = (i + 1) % self.DMA_POOL
            sem = f"{eng}_dma{i}"
            inc = 16
        else:
            sem = eng
            inc = 1
        deps = {}
        if dma and self.count[sem] > 0:
            self._dep(deps, sem, self.count[sem])
        for k in reads:
            w = self.writer.get(k)
            if w is not None:
                self._dep(deps, *w)
        for k in writes:
            w = self.writer.get(k)
            if w is not None:
                self._dep(deps, *w)
            for r in self.readers.get(k, ()):
                self._dep(deps, *r)
        waits = []
        wt = self.waited[eng]
        for s, c in deps.items():
            if s == "tensor" and eng == "tensor":
                continue
            if wt.get(s, 0) >= c:
                continue
            waits.append((s, c))
            wt[s] = c
        self.count[sem] += inc
        cnt_after = self.count[sem]
        self.queues[eng].append((waits, fn, sem, inc))
        for k in writes:
            self.writer[k] = (sem, cnt_after)
            self.readers[k] = []
        for k in reads:
            self.readers.setdefault(k, []).append((sem, cnt_after))
        return (sem, cnt_after)

    def barrier_wait(self, eng, sems_counts):
        wt = self.waited[eng]
        waits = []
        for s, c in sems_counts.items():
            if wt.get(s, 0) >= c:
                continue
            waits.append((s, c))
            wt[s] = c
        if waits:
            self.queues[eng].append((waits, None, None, 0))

    def full_barrier(self):
        snap = {s: c for s, c in self.count.items() if c > 0}
        for e in ENGINES:
            self.barrier_wait(e, snap)

    def final_wait_all(self, eng="sync"):
        self.barrier_wait(eng, {s: c for s, c in self.count.items() if c > 0})

    def build_into(self, block, sems):
        def run_queue(engine_obj, items):
            for waits, fn, sem, inc in items:
                for (s, c) in waits:
                    engine_obj.wait_ge(sems[s], c)
                if fn is not None:
                    fn(engine_obj).then_inc(sems[sem], inc)

        @block.sync
        def _(eng):
            run_queue(eng, self.queues["sync"])

        @block.vector
        def _(eng):
            run_queue(eng, self.queues["vector"])

        @block.scalar
        def _(eng):
            run_queue(eng, self.queues["scalar"])

        @block.tensor
        def _(eng):
            run_queue(eng, self.queues["tensor"])

        @block.gpsimd
        def _(eng):
            run_queue(eng, self.queues["gpsimd"])


class Ctx:
    def __init__(self, nc, p, ec):
        self.nc = nc
        self.p = p
        self.ec = ec
        self.ec0 = ec
        self.sb = {}
        self.dbg_names = set()
        self.scopes = []

    def sbuf(self, name, shape, dtype, outer=False):
        ec = self.ec0 if outer else self.ec
        kw = {"side": "right"} if outer else {}
        t = ec(self.nc.sbuf_tensor(name, list(shape), dtype,
                                   allow_name_mangling=True, **kw))
        self.sb[name] = t
        return t

    def psum(self, name, shape=(128, 512), dtype=F32):
        t = self.ec(self.nc.psum_tensor(name, list(shape), dtype,
                                        allow_name_mangling=True))
        self.sb[name] = t
        return t

    def scope(self):
        st = contextlib.ExitStack()
        self.scopes.append((st, self.ec))
        self.ec = st.enter_context
        return st

    def close_scope(self):
        st, outer_ec = self.scopes.pop()
        self.p.full_barrier()
        st.close()
        self.ec = outer_ec

    def dbg_dump(self, name, ap, shape, dtype=F32):
        if name not in self.dbg_names:
            return
        t = self.nc.dram_tensor("dbg_" + name, list(shape), dtype,
                                kind="ExternalOutput")
        self.p.full_barrier()
        self.p.emit("sync", lambda e: e.dma_start(out=t[:, :] if len(shape) == 2 else t[:], in_=ap),
                    reads=[], writes=["dbg_" + name], dma=True)





def transpose_via_ptr(cx, items, identity="identb"):
    p = cx.p
    ptr = cx.sb["ptr"]
    flip = getattr(cx, "_ptr_flip", 0)
    i = 0
    while i < len(items):
        grp = items[i:i + 4]
        bank = flip % 2
        bkey = "ptrA" if bank == 0 else "ptrB"
        for s, (in_ap, in_keys, _, _) in enumerate(grp):
            off = 1024 * bank + 128 * s
            p.emit("tensor", lambda e, in_ap=in_ap, off=off: e.transpose(
                ptr[:, off:off + 128], in_ap, cx.sb[identity][:, :]),
                   reads=list(in_keys) + [identity], writes=[bkey])
        for s, (_, _, evac_fn, ew) in enumerate(grp):
            off = 1024 * bank + 128 * s
            p.emit("scalar", lambda e, evac_fn=evac_fn, off=off:
                   evac_fn(e, ptr[:, off:off + 128]),
                   reads=[bkey], writes=list(ew))
        flip += 1
        i += 4
    cx._ptr_flip = flip


def emit_consts(cx):
    nc, p = cx.nc, cx.p
    E = p.emit
    KC = int(os.environ.get("KCONST", "9"))
    cx.sbuf("identf", [128, 128], F32)
    cx.sbuf("identb", [128, 128], BF16)
    cx.sbuf("onesf_row", [1, 128], F32)
    cx.sbuf("onesb_row", [1, 128], BF16)
    cx.sbuf("onesb_col", [128, 1], BF16)
    cx.sbuf("iota_col", [128, 1], F32)
    cx.sbuf("halfpi", [128, 1], F32)
    cx.sbuf("pos_cols", [128, 16], F32)
    cx.sbuf("cmask", [128, 4 * 512], BF16)
    for nm in ("identf", "identb"):
        t = cx.sb[nm]
        E("gpsimd", lambda e, t=t: e.memset(t[:, :], 0.0), writes=[nm])
        E("gpsimd", lambda e, t=t: e.affine_select(
            out=t[:, :], in_=t[:, :], compare_op=ALU.not_equal, fill=1.0,
            base=0, pattern=[[-1, 128]], channel_multiplier=1),
          reads=[nm], writes=[nm])
    if KC < 2:
        return
    for nm, v in (("onesf_row", 1.0), ("onesb_row", 1.0), ("onesb_col", 1.0),
                  ("halfpi", HALF_PI)):
        t = cx.sb[nm]
        E("gpsimd", lambda e, t=t, v=v: e.memset(t[:, :], v), writes=[nm])
    if KC < 3:
        return
    E("gpsimd", lambda e: e.iota(cx.sb["iota_col"][:, :], pattern=[[0, 1]],
                                 base=0, channel_multiplier=1,
                                 allow_small_or_imprecise_dtypes=True),
      writes=["iota_col"])
    E("gpsimd", lambda e: e.iota(cx.sb["pos_cols"][:, :], pattern=[[128, 16]],
                                 base=0, channel_multiplier=1,
                                 allow_small_or_imprecise_dtypes=True),
      writes=["pos_cols"])
    if KC < 4:
        return
    E("gpsimd", lambda e: e.memset(cx.sb["cmask"][:, :], 1.0), writes=["cmask"])
    for k in range(4):
        E("gpsimd", lambda e, k=k: e.affine_select(
            out=cx.sb["cmask"][:, 512 * k:512 * (k + 1)],
            in_=cx.sb["cmask"][:, 512 * k:512 * (k + 1)],
            compare_op=ALU.is_ge, fill=0.0, base=-(128 * k),
            pattern=[[1, 512]], channel_multiplier=-1),
          reads=["cmask"], writes=["cmask"])


def bcast_row(cx, dst_name, dst_off, src_ap, width, read_keys, write_key=None):
    p = cx.p
    dst = cx.sb[dst_name]
    pb = cx.sb["pbc"]
    wk = write_key or dst_name
    for s in range(0, width, 512):
        w = min(512, width - s)
        p.emit("tensor", lambda e, s=s, w=w: e.matmul(
            pb[:, 0:w], cx.sb["onesf_row"][0:1, :],
            src_ap[0:1, s:s + w], start=True, stop=True),
               reads=list(read_keys) + ["onesf_row"], writes=["pbc"])
        p.emit("vector", lambda e, s=s, w=w: e.tensor_copy(
            out=dst[:, dst_off + s:dst_off + s + w], in_=pb[:, 0:w]),
               reads=["pbc"], writes=[wk])


def emit_ln_to_xnf(cx, x_dram, pfx, eps=1e-5):
    p = cx.p
    E = p.emit
    xt = cx.sb["xt2"]
    xnb = cx.sb["xnb2"]
    scr = cx.sb["scr1k"]
    st = cx.sb["st8"]
    mst, rst, mrst = cx.sb["mst"], cx.sb["rst"], cx.sb["mrst"]
    ptr = cx.sb["ptr"]
    xnf = [cx.sb[f"xnf{c}"] for c in range(CC)]
    mk, rk, mrk = pfx + "mst", pfx + "rst", pfx + "mrst"

    for nt in range(NT):
        b = nt % 2
        st = cx.sb["st8"] if b == 0 else cx.sb["st8b"]
        sp = "st8" if b == 0 else "st8b"
        xk = f"xt2:{b}"
        E("sync", lambda e, nt=nt, b=b: e.dma_start(
            out=xt[:, 1024 * b:1024 * (b + 1)],
            in_=x_dram[128 * nt:128 * (nt + 1), :]),
          writes=[xk], dma=True)
        xv = xt[:, 1024 * b:1024 * (b + 1)]
        E("vector", lambda e, xv=xv, st=st: e.reduce_sum(
            out=st[:, 0:1], in_=xv, axis=AX.X), reads=[xk],
          writes=[sp + ":s"])
        E("scalar", lambda e, xv=xv: e.activation(
            out=scr[:, :], in_=xv, func=AF.Square, accum_out=st[:, 1:2]),
          reads=[xk], writes=["scr1k", "st8:q"])
        E("vector", lambda e, nt=nt: e.tensor_scalar(
            out=mst[:, nt:nt + 1], in0=st[:, 0:1], scalar1=1.0 / 1024.0,
            scalar2=None, op0=ALU.mult),
          reads=["st8:s"], writes=[mk])
        E("scalar", lambda e, nt=nt: e.activation(
            out=st[:, 2:3], in_=mst[:, nt:nt + 1], func=AF.Square),
          reads=[mk], writes=["st8:m2"])
        E("vector", lambda e: e.tensor_scalar(
            out=st[:, 3:4], in0=st[:, 1:2], scalar1=1.0 / 1024.0,
            scalar2=float(eps), op0=ALU.mult, op1=ALU.add),
          reads=["st8:q"], writes=["st8:v"])
        E("vector", lambda e: e.tensor_tensor(
            out=st[:, 3:4], in0=st[:, 3:4], in1=st[:, 2:3], op=ALU.subtract),
          reads=["st8:v", "st8:m2"], writes=["st8:v"])
        E("scalar", lambda e: e.activation(
            out=st[:, 4:5], in_=st[:, 3:4], func=AF.Sqrt),
          reads=["st8:v"], writes=["st8:sd"])
        E("vector", lambda e, nt=nt: e.reciprocal(
            out=rst[:, nt:nt + 1], in_=st[:, 4:5]),
          reads=["st8:sd"], writes=[rk])
        E("vector", lambda e, nt=nt: e.tensor_tensor(
            out=mrst[:, nt:nt + 1], in0=mst[:, nt:nt + 1],
            in1=rst[:, nt:nt + 1], op=ALU.mult),
          reads=[mk, rk], writes=[mrk])
        E("vector", lambda e, xv=xv, nt=nt, b=b: e.tensor_scalar(
            out=xnb[:, 1024 * b:1024 * (b + 1)], in0=xv,
            scalar1=mst[:, nt:nt + 1], scalar2=rst[:, nt:nt + 1],
            op0=ALU.subtract, op1=ALU.mult),
          reads=[xk, mk, rk], writes=[f"xnb2:{b}"])
        for cc in range(CC):
            E("tensor", lambda e, cc=cc, b=b: e.transpose(
                ptr[:, 128 * (cc % 4):128 * (cc % 4 + 1)],
                xnb[:, 1024 * b + 128 * cc:1024 * b + 128 * (cc + 1)],
                cx.sb["identb"][:, :]),
              reads=[f"xnb2:{b}", "identb"], writes=[f"ptr:{cc % 4}"])
            if cc % 4 == 3:
                for c2 in range(cc - 3, cc + 1):
                    E("scalar", lambda e, c2=c2, nt=nt: e.activation(
                        out=xnf[c2][:, 128 * nt:128 * (nt + 1)],
                        in_=ptr[:, 128 * (c2 % 4):128 * (c2 % 4 + 1)],
                        func=AF.Copy),
                      reads=[f"ptr:{c2 % 4}"], writes=[f"xnf{c2}:{nt}"])


def mm_chain(cx, psum_ap, pk, pairs, extra_reads=()):
    p = cx.p
    n = len(pairs)
    for i, (lh, rh, rd) in enumerate(pairs):
        p.emit("tensor", lambda e, lh=lh, rh=rh, i=i: e.matmul(
            psum_ap, lh, rh, start=(i == 0), stop=(i == n - 1)),
               reads=list(rd) + list(extra_reads), writes=[pk])



def emit_ln(cx, x_dram, pfx, with_lo=False, eps=1e-5, x_dram_xnt=None):
    p = cx.p
    E = p.emit
    xt = cx.sb["xt2"]
    xnb = cx.sb[pfx + "xnb"]
    scr = cx.sb["scr1k"]
    res = cx.sb["res1k"]
    mst, rst, mrst = cx.sb["mst"], cx.sb["rst"], cx.sb["mrst"]
    ptr = cx.sb["ptr"]
    xnf = [cx.sb[f"{pfx}xnf{c}"] for c in range(CC)]
    xnlo = [cx.sb[f"{pfx}xnlo{c}"] for c in range(CC)] if with_lo else None
    mk, rk, mrk = pfx + "mst", pfx + "rst", pfx + "mrst"

    for nt in range(NT):
        b = nt % 2
        st = cx.sb["st8"] if b == 0 else cx.sb["st8b"]
        sp = "st8" if b == 0 else "st8b"
        xk = f"xt2:{b}"
        E("sync", lambda e, nt=nt, b=b: e.dma_start(
            out=xt[:, 1024 * b:1024 * (b + 1)],
            in_=x_dram[128 * nt:128 * (nt + 1), :]),
          writes=[xk], dma=True)
        xv = xt[:, 1024 * b:1024 * (b + 1)]
        E("vector", lambda e, xv=xv, st=st: e.reduce_sum(
            out=st[:, 0:1], in_=xv, axis=AX.X), reads=[xk],
          writes=[sp + ":s"])
        E("scalar", lambda e, xv=xv, st=st: e.activation(
            out=scr[:, :], in_=xv, func=AF.Square, accum_out=st[:, 1:2]),
          reads=[xk], writes=["scr1k", sp + ":q"])
        E("vector", lambda e, nt=nt, st=st: e.tensor_scalar(
            out=mst[:, nt:nt + 1], in0=st[:, 0:1], scalar1=1.0 / 1024.0,
            scalar2=None, op0=ALU.mult),
          reads=[sp + ":s"], writes=[mk])
        E("scalar", lambda e, nt=nt, st=st: e.activation(
            out=st[:, 2:3], in_=mst[:, nt:nt + 1], func=AF.Square),
          reads=[mk], writes=[sp + ":m2"])
        E("vector", lambda e, st=st: e.tensor_scalar(
            out=st[:, 3:4], in0=st[:, 1:2], scalar1=1.0 / 1024.0,
            scalar2=float(eps), op0=ALU.mult, op1=ALU.add),
          reads=[sp + ":q"], writes=[sp + ":v"])
        E("vector", lambda e, st=st: e.tensor_tensor(
            out=st[:, 3:4], in0=st[:, 3:4], in1=st[:, 2:3], op=ALU.subtract),
          reads=[sp + ":v", sp + ":m2"], writes=[sp + ":v"])
        E("scalar", lambda e, st=st: e.activation(
            out=st[:, 4:5], in_=st[:, 3:4], func=AF.Sqrt),
          reads=[sp + ":v"], writes=[sp + ":sd"])
        E("vector", lambda e, nt=nt, st=st: e.reciprocal(
            out=rst[:, nt:nt + 1], in_=st[:, 4:5]),
          reads=[sp + ":sd"], writes=[rk])
        E("vector", lambda e, nt=nt: e.tensor_tensor(
            out=mrst[:, nt:nt + 1], in0=mst[:, nt:nt + 1],
            in1=rst[:, nt:nt + 1], op=ALU.mult),
          reads=[mk, rk], writes=[mrk])
        store_xnt = (pfx == "l0")
        if with_lo:
            E("vector", lambda e, xv=xv, nt=nt: e.tensor_scalar(
                out=res[:, :], in0=xv, scalar1=mst[:, nt:nt + 1],
                scalar2=rst[:, nt:nt + 1], op0=ALU.subtract, op1=ALU.mult),
              reads=[xk, mk, rk], writes=["res1k"])
            E("vector", lambda e, b=b: e.tensor_copy(
                out=xnb[:, 1024 * b:1024 * (b + 1)], in_=res[:, :]),
              reads=["res1k"], writes=[f"{pfx}xnb:{b}"])
            E("vector", lambda e, b=b: e.tensor_tensor(
                out=xnb[:, 2048:3072],
                in0=res[:, :], in1=xnb[:, 1024 * b:1024 * (b + 1)],
                op=ALU.subtract),
              reads=["res1k", f"{pfx}xnb:{b}"], writes=[f"{pfx}xnlo:x"])
            if store_xnt:
                E("scalar", lambda e, nt=nt, b=b: e.dma_start(
                    out=x_dram_xnt[128 * nt:128 * (nt + 1), :],
                    in_=xnb[:, 1024 * b:1024 * (b + 1)]),
                  reads=[f"{pfx}xnb:{b}"], writes=[f"xntd:{nt}"], dma=True)
        else:
            E("vector", lambda e, xv=xv, nt=nt, b=b: e.tensor_scalar(
                out=xnb[:, 1024 * b:1024 * (b + 1)], in0=xv,
                scalar1=mst[:, nt:nt + 1], scalar2=rst[:, nt:nt + 1],
                op0=ALU.subtract, op1=ALU.mult),
              reads=[xk, mk, rk], writes=[f"{pfx}xnb:{b}"])
        items = []
        for cc in range(CC):
            def mk_evac(c2, nt):
                def f(e, ps):
                    return e.activation(
                        out=xnf[c2][:, 128 * nt:128 * (nt + 1)], in_=ps,
                        func=AF.Copy)
                return f
            items.append((
                xnb[:, 1024 * b + 128 * cc:1024 * b + 128 * (cc + 1)],
                [f"{pfx}xnb:{b}"], mk_evac(cc, nt), [f"{pfx}xnf{cc}:{nt}"]))
        transpose_via_ptr(cx, items)
        if with_lo:
            items = []
            for cc in range(CC):
                def mk_evac2(c2, nt):
                    def f(e, ps):
                        return e.activation(
                            out=xnlo[c2][:, 128 * nt:128 * (nt + 1)], in_=ps,
                            func=AF.Copy)
                    return f
                items.append((
                    xnb[:, 2048 + 128 * cc:2048 + 128 * (cc + 1)],
                    [f"{pfx}xnlo:x"], mk_evac2(cc, nt),
                    [f"{pfx}xnlo{cc}:{nt}"]))
            transpose_via_ptr(cx, items)



def emit_l0_gate_and_bisect(cx, dr):
    p = cx.p
    E = p.emit
    xnf = [cx.sb[f"l0xnf{c}"] for c in range(CC)]
    xnlo = [cx.sb[f"l0xnlo{c}"] for c in range(CC)]
    gate32 = cx.sb["gate32"]
    ghi, glo = cx.sb["ghi"], cx.sb["glo"]
    aff_t = cx.sb["aff_t"]
    aff_T = cx.sb["aff_T"]
    aff_pack = cx.sb["aff_pack"]
    pA, pB = cx.sb["pA"], cx.sb["pB"]
    st = cx.sb["st8"]
    scr = cx.sb["scr1k"]
    E("sync", lambda e: e.dma_start(
        out=gate32[:, :].rearrange("p (c e) -> p c e", e=8),
        in_=dr["gate_w"][:, :].rearrange("(c p) e -> p c e", p=128)),
      writes=["gate32"], dma=True)
    E("vector", lambda e: e.tensor_copy(out=ghi[:, :], in_=gate32[:, :]),
      reads=["gate32"], writes=["ghi"])
    E("vector", lambda e: e.tensor_tensor(
        out=glo[:, :], in0=gate32[:, :], in1=ghi[:, :], op=ALU.subtract),
      reads=["gate32", "ghi"], writes=["glo"])
    for nt in range(NT):
        pairs = []
        for cc in range(CC):
            pairs.append((xnf[cc][:, 128 * nt:128 * (nt + 1)],
                          ghi[:, 8 * cc:8 * (cc + 1)],
                          [f"l0xnf{cc}:{nt}", "ghi"]))
        for cc in range(CC):
            pairs.append((xnf[cc][:, 128 * nt:128 * (nt + 1)],
                          glo[:, 8 * cc:8 * (cc + 1)],
                          [f"l0xnf{cc}:{nt}", "glo"]))
        for cc in range(CC):
            pairs.append((xnlo[cc][:, 128 * nt:128 * (nt + 1)],
                          ghi[:, 8 * cc:8 * (cc + 1)],
                          [f"l0xnlo{cc}:{nt}", "ghi"]))
        mm_chain(cx, pA[:, 0:8], "pA", pairs)
        E("vector", lambda e: e.reduce_max(
            out=st[:, 0:1], in_=pA[:, 0:8], axis=AX.X, negate=True),
          reads=["pA"], writes=["st8:nm"])
        E("scalar", lambda e: e.activation(
            out=scr[:, 0:8], in_=pA[:, 0:8], func=AF.Exp,
            bias=st[:, 0:1], accum_out=st[:, 1:2]),
          reads=["pA", "st8:nm"], writes=["scr1k", "st8:es"])
        E("vector", lambda e: e.reciprocal(out=st[:, 2:3], in_=st[:, 1:2]),
          reads=["st8:es"], writes=["st8:rs"])
        E("vector", lambda e, nt=nt: e.tensor_scalar(
            out=aff_t[:, 8 * nt:8 * (nt + 1)], in0=scr[:, 0:8],
            scalar1=st[:, 2:3], scalar2=None, op0=ALU.mult),
          reads=["scr1k", "st8:rs"], writes=[f"aff_t:{nt}"])
        E("tensor", lambda e, nt=nt: e.transpose(
            pB[0:8, 0:128], aff_t[:, 8 * nt:8 * (nt + 1)],
            cx.sb["identf"][:, :]),
          reads=[f"aff_t:{nt}", "identf"], writes=["pB"])
        E("vector", lambda e, nt=nt: e.tensor_copy(
            out=aff_T[0:8, 128 * nt:128 * (nt + 1)], in_=pB[0:8, 0:128]),
          reads=["pB"], writes=[f"aff_T:{nt}"])
    E("sync", lambda e: e.dma_start(
        out=aff_pack[:, :],
        in_=aff_T[:, :].rearrange("e (t n) -> e t n", n=128)),
      reads=[f"aff_T:{nt}" for nt in range(NT)], writes=["aff_pack"], dma=True)

    th = cx.sb["th"]
    s8, s8t = cx.sb["s8"], cx.sb["s8t"]
    thb, cnt_col = cx.sb["thb"], cx.sb["cnt_col"]
    ind_sb, cgt = cx.sb["ind_sb"], cx.sb["cgt"]
    iby = cx.sb["iby16"]
    ibyi = cx.sb["iby16i"]
    iot8, i8b = cx.sb["iot8"], cx.sb["i8b"]
    E("gpsimd", lambda e: e.memset(s8t[:, :], 1.0), writes=["s8t"])
    E("gpsimd", lambda e: e.affine_select(
        out=s8t[:, :], in_=s8t[:, :], compare_op=ALU.is_ge, fill=0.0,
        base=0, pattern=[[-16, 8]], channel_multiplier=1),
      reads=["s8t"], writes=["s8t"])
    E("gpsimd", lambda e: e.affine_select(
        out=s8t[:, :], in_=s8t[:, :], compare_op=ALU.is_ge, fill=0.0,
        base=15, pattern=[[16, 8]], channel_multiplier=-1),
      reads=["s8t"], writes=["s8t"])
    E("tensor", lambda e: e.transpose(
        cx.sb["pbc"][0:8, 0:128], s8t[:, :], cx.sb["identf"][:, :]),
      reads=["s8t", "identf"], writes=["pbc"])
    E("vector", lambda e: e.tensor_copy(out=s8[0:8, :],
                                        in_=cx.sb["pbc"][0:8, 0:128]),
      reads=["pbc"], writes=["s8"])
    E("gpsimd", lambda e: e.memset(th[0:8, 0:1], 0.0), writes=["th:lo"])
    E("gpsimd", lambda e: e.memset(th[0:8, 1:2], 1.0), writes=["th:hi"])
    E("gpsimd", lambda e: e.memset(th[0:8, 2:3], 0.5), writes=["th:mid"])

    def bisect_iter():
        pbc = cx.sb["pbc"]
        E("tensor", lambda e: e.matmul(
            pbc[0:128, 0:1], s8[0:8, :], th[0:8, 2:3], start=True, stop=True),
          reads=["s8", "th:mid"], writes=["pbc"])
        E("vector", lambda e: e.tensor_copy(out=thb[:, 0:1],
                                            in_=pbc[0:128, 0:1]),
          reads=["pbc"], writes=["thb"])
        E("vector", lambda e: e.tensor_scalar(
            out=ind_sb[:, :], in0=cx.sb["aff_pack"][:, :],
            scalar1=thb[:, 0:1], scalar2=0.0, op0=ALU.is_ge, op1=ALU.add,
            accum_out=cnt_col[:, 0:1]),
          reads=["aff_pack", "thb"], writes=["ind_sb", "cnt_col"])
        E("tensor", lambda e: e.matmul(
            pbc[0:8, 0:1], s8t[:, :], cnt_col[:, 0:1], start=True, stop=True),
          reads=["s8t", "cnt_col"], writes=["pbc"])
        E("vector", lambda e: e.tensor_scalar(
            out=cgt[0:8, 0:1], in0=pbc[0:8, 0:1], scalar1=float(KSEL),
            scalar2=None, op0=ALU.is_gt),
          reads=["pbc"], writes=["cgt"])
        E("vector", lambda e: e.tensor_tensor(
            out=th[0:8, 3:4], in0=th[0:8, 2:3], in1=th[0:8, 0:1],
            op=ALU.subtract),
          reads=["th:mid", "th:lo"], writes=["th:s"])
        E("vector", lambda e: e.tensor_tensor(
            out=th[0:8, 3:4], in0=th[0:8, 3:4], in1=cgt[0:8, 0:1],
            op=ALU.mult),
          reads=["th:s", "cgt"], writes=["th:s"])
        E("vector", lambda e: e.tensor_tensor(
            out=th[0:8, 0:1], in0=th[0:8, 0:1], in1=th[0:8, 3:4], op=ALU.add),
          reads=["th:lo", "th:s"], writes=["th:lo"])
        E("vector", lambda e: e.tensor_tensor(
            out=th[0:8, 3:4], in0=th[0:8, 2:3], in1=th[0:8, 1:2],
            op=ALU.subtract),
          reads=["th:mid", "th:hi"], writes=["th:s"])
        E("vector", lambda e: e.tensor_tensor(
            out=th[0:8, 3:4], in0=th[0:8, 3:4], in1=cgt[0:8, 0:1],
            op=ALU.mult),
          reads=["th:s", "cgt"], writes=["th:s"])
        E("vector", lambda e: e.tensor_tensor(
            out=th[0:8, 1:2], in0=th[0:8, 2:3], in1=th[0:8, 3:4],
            op=ALU.subtract),
          reads=["th:mid", "th:s"], writes=["th:hi"])
        E("vector", lambda e: e.tensor_tensor(
            out=th[0:8, 3:4], in0=th[0:8, 0:1], in1=th[0:8, 1:2], op=ALU.add),
          reads=["th:lo", "th:hi"], writes=["th:s"])
        E("vector", lambda e: e.tensor_scalar(
            out=th[0:8, 2:3], in0=th[0:8, 3:4], scalar1=0.5, scalar2=None,
            op0=ALU.mult),
          reads=["th:s"], writes=["th:mid"])

    return bisect_iter


def emit_l0_gating_final(cx):
    p = cx.p
    E = p.emit
    th = cx.sb["th"]
    hi_row = cx.sb["hi_row"]
    thb_t = cx.sb["thb_t"]
    w_t = cx.sb["w_t"]
    aff_t = cx.sb["aff_t"]
    scr = cx.sb["scr1k"]
    E("tensor", lambda e: e.transpose(
        cx.sb["pbc"][0:1, 0:8], th[0:8, 1:2], cx.sb["identf"][0:8, 0:8]),
      reads=["th:hi", "identf"], writes=["pbc"])
    E("vector", lambda e: e.tensor_copy(out=hi_row[0:1, 0:8],
                                        in_=cx.sb["pbc"][0:1, 0:8]),
      reads=["pbc"], writes=["hi_row"])
    bcast_row(cx, "thb_t", 0, hi_row[0:1, 0:8], 8, ["hi_row"])
    for nt in range(NT):
        E("vector", lambda e, nt=nt: e.tensor_tensor(
            out=scr[:, 0:8], in0=aff_t[:, 8 * nt:8 * (nt + 1)],
            in1=thb_t[:, 0:8], op=ALU.is_ge),
          reads=[f"aff_t:{nt}", "thb_t"], writes=["scr1k"])
        E("vector", lambda e, nt=nt: e.tensor_tensor(
            out=w_t[:, 8 * nt:8 * (nt + 1)], in0=scr[:, 0:8],
            in1=aff_t[:, 8 * nt:8 * (nt + 1)], op=ALU.mult),
          reads=["scr1k", f"aff_t:{nt}"], writes=[f"w_t:{nt}"])


def emit_rope_tables(cx, nt, ss=0):
    p = cx.p
    E = p.emit
    sx = "" if ss == 0 else "2"
    ivb = cx.sb["ivf_b"]
    tA, tB, kf = (cx.sb["scr512a" + sx], cx.sb["scr512b" + sx],
                  cx.sb["scr512c" + sx])
    ki = cx.sb["kint" + sx]
    pos = cx.sb["pos_cols"]
    E("vector", lambda e, nt=nt: e.tensor_scalar(
        out=tA[:, :], in0=ivb[:, :], scalar1=pos[:, nt:nt + 1], scalar2=None,
        op0=ALU.mult),
      reads=["ivf_b", "pos_cols"], writes=["scr512a" + sx])
    for ph, extra in ((0, None), (1, 0.25)):
        E("vector", lambda e, extra=extra: e.tensor_scalar(
            out=ki[:, :], in0=tA[:, :], scalar1=INV_2PI, scalar2=extra,
            op0=ALU.mult, op1=(ALU.add if extra is not None else ALU.bypass)),
          reads=["scr512a" + sx], writes=["kint" + sx])
        E("vector", lambda e: e.tensor_copy(out=kf[:, :], in_=ki[:, :]),
          reads=["kint" + sx], writes=["scr512c" + sx])
        E("vector", lambda e: e.scalar_tensor_tensor(
            out=tB[:, :], in0=kf[:, :], scalar=-TWO_PI_HI, in1=tA[:, :],
            op0=ALU.mult, op1=ALU.add),
          reads=["scr512c" + sx, "scr512a" + sx], writes=["scr512b" + sx])
        E("vector", lambda e: e.scalar_tensor_tensor(
            out=tB[:, :], in0=kf[:, :], scalar=-TWO_PI_LO, in1=tB[:, :],
            op0=ALU.mult, op1=ALU.add),
          reads=["scr512c" + sx, "scr512b" + sx], writes=["scr512b" + sx])
        thr = float(np.float32(math.pi if ph == 0 else math.pi / 2))
        E("vector", lambda e, thr=thr: e.tensor_scalar(
            out=kf[:, :], in0=tB[:, :], scalar1=thr,
            scalar2=None, op0=ALU.is_gt),
          reads=["scr512b" + sx], writes=["scr512c" + sx])
        E("vector", lambda e: e.scalar_tensor_tensor(
            out=tB[:, :], in0=kf[:, :], scalar=-TWO_PI_HI, in1=tB[:, :],
            op0=ALU.mult, op1=ALU.add),
          reads=["scr512c" + sx, "scr512b" + sx], writes=["scr512b" + sx])
        if ph == 0:
            E("scalar", lambda e, nt=nt: e.activation(
                out=cx.sb["sin_all"][:, 512 * nt:512 * (nt + 1)],
                in_=tB[:, :], func=AF.Sin),
              reads=["scr512b" + sx], writes=[f"sin_all:{nt}"])
        else:
            E("scalar", lambda e, nt=nt: e.activation(
                out=cx.sb["cos_all"][:, 512 * nt:512 * (nt + 1)],
                in_=tB[:, :], func=AF.Sin, bias=cx.sb["halfpi"][:, 0:1]),
              reads=["scr512b" + sx, "halfpi"], writes=[f"cos_all:{nt}"])


def emit_rope_apply(cx, x_ap, xkey, nt, ss=0):
    p = cx.p
    E = p.emit
    sx = "" if ss == 0 else "2"
    cos = cx.sb["cos_all"][:, 512 * nt:512 * (nt + 1)]
    sin = cx.sb["sin_all"][:, 512 * nt:512 * (nt + 1)]
    ck, sk = f"cos_all:{nt}", f"sin_all:{nt}"
    xe = x_ap.rearrange("p (i two) -> p i two", two=2)[:, :, 0]
    xo = x_ap.rearrange("p (i two) -> p i two", two=2)[:, :, 1]
    a, b, c, d = (cx.sb["scr512a" + sx], cx.sb["scr512b" + sx],
                  cx.sb["scr512c" + sx], cx.sb["scr512d" + sx])
    E("vector", lambda e: e.tensor_tensor(out=a[:, :], in0=xe, in1=cos,
                                          op=ALU.mult),
      reads=[xkey, ck], writes=["scr512a" + sx])
    E("vector", lambda e: e.tensor_tensor(out=b[:, :], in0=xo, in1=sin,
                                          op=ALU.mult),
      reads=[xkey, sk], writes=["scr512b" + sx])
    E("gpsimd", lambda e: e.tensor_tensor(out=c[:, :], in0=xo, in1=cos,
                                          op=ALU.mult),
      reads=[xkey, ck], writes=["scr512c" + sx])
    E("gpsimd", lambda e: e.tensor_tensor(out=d[:, :], in0=xe, in1=sin,
                                          op=ALU.mult),
      reads=[xkey, sk], writes=["scr512d" + sx])
    E("vector", lambda e: e.tensor_tensor(out=xe, in0=a[:, :], in1=b[:, :],
                                          op=ALU.subtract),
      reads=["scr512a" + sx, "scr512b" + sx], writes=[xkey])
    E("gpsimd", lambda e: e.tensor_tensor(out=xo, in0=c[:, :], in1=d[:, :],
                                          op=ALU.add),
      reads=["scr512c" + sx, "scr512d" + sx], writes=[xkey])


def load_wT(cx, w_dram, dst, dstname):
    p = cx.p
    E = p.emit
    wtmp = cx.sb["wtmp"]
    for jc in range(CC):
        b = jc % 2
        wk = f"wtmp:{b}"
        E("gpsimd", lambda e, jc=jc, b=b, w_dram=w_dram: e.dma_start(
            out=wtmp[:, 1024 * b:1024 * (b + 1)],
            in_=w_dram[128 * jc:128 * (jc + 1), :]),
          writes=[wk], dma=True)
        items = []
        for cc in range(CC):
            def mk_evac(c2, jc):
                def f(e, ps):
                    return e.activation(
                        out=dst[c2][:, 128 * jc:128 * (jc + 1)], in_=ps,
                        func=AF.Copy)
                return f
            items.append((
                wtmp[:, 1024 * b + 128 * cc:1024 * b + 128 * (cc + 1)],
                [wk], mk_evac(cc, jc), [f"{dstname}{cc}"]))
        transpose_via_ptr(cx, items)


def emit_layer0(cx, dr, pump):
    p = cx.p
    E = p.emit
    xnf = [cx.sb[f"l0xnf{c}"] for c in range(CC)]
    cx.scope()
    qt = cx.sbuf("qt", [128, NT * 1024], BF16)
    cx.sbuf("cos_all", [128, NT * 512], BF16)
    cx.sbuf("sin_all", [128, NT * 512], BF16)
    cx.sbuf("scr512a2", [128, 512], F32)
    cx.sbuf("scr512b2", [128, 512], F32)
    cx.sbuf("scr512c2", [128, 512], F32)
    cx.sbuf("scr512d2", [128, 512], F32)

    cx.scope()
    wtq = [cx.sbuf(f"wtq{c}", [128, 1024], BF16) for c in range(CC)]
    cx.sbuf("wtmp", [128, 2048], BF16)
    brow1 = cx.sbuf("brow1", [1, 1024], F32)
    bqb = cx.sbuf("bqb", [128, 1024], BF16)
    load_wT(cx, dr["Wq"], wtq, "wtq")
    E("sync", lambda e: e.dma_start(out=brow1[0:1, :], in_=dr["bq"][:]),
      writes=["brow1"], dma=True)
    bcast_row(cx, "bqb", 0, brow1[0:1, :], 1024, ["brow1"])
    scrA, scrB = cx.sb["scr512a"], cx.sb["scr512b"]
    for nt in range(NT):
        pump(); pump()
        for jh in range(2):
            ps = cx.sb["pA"] if jh == 0 else cx.sb["pB"]
            pk = "pA" if jh == 0 else "pB"
            mm_chain(cx, ps[:, :], pk,
                     [(xnf[cc][:, 128 * nt:128 * (nt + 1)],
                       wtq[cc][:, 512 * jh:512 * (jh + 1)],
                       [f"l0xnf{cc}:{nt}", f"wtq{cc}"]) for cc in range(CC)])
            E("vector", lambda e, ps=ps, pk=pk, jh=jh: e.tensor_tensor(
                out=scrA[:, :], in0=ps[:, :],
                in1=bqb[:, 512 * jh:512 * (jh + 1)], op=ALU.add),
              reads=[pk, "bqb"], writes=["scr512a"])
            E("scalar", lambda e: e.activation(
                out=scrB[:, :], in_=scrA[:, :], func=AF.Gelu),
              reads=["scr512a"], writes=["scr512b"])
            E("vector", lambda e, nt=nt, jh=jh: e.tensor_scalar(
                out=qt[:, 1024 * nt + 512 * jh:1024 * nt + 512 * (jh + 1)],
                in0=scrB[:, :], scalar1=0.21, scalar2=None, op0=ALU.add),
              reads=["scr512b"], writes=[f"qt:{nt}"])
    cx.close_scope()

    if KSTOP == "b1a":
        cx.close_scope()
        return
    cx.scope()
    cx.psum("pD", (1, 1024), F32)
    cx.psum("pkv", (128, 512), F32)
    wtk = [cx.sbuf(f"wtk{c}", [128, 1024], BF16) for c in range(CC)]
    wtv = [cx.sbuf(f"wtv{c}", [128, 1024], BF16) for c in range(CC)]
    cx.sbuf("wtmp", [128, 2048], BF16)
    brow2 = cx.sbuf("brow2", [1, 1024], F32)
    bkb = cx.sbuf("bkb", [128, 1024], BF16)
    bvb2 = cx.sbuf("bvb2", [128, 1024], F32)
    ktb2 = cx.sbuf("ktb", [128, 2048], BF16)
    vtb2 = cx.sbuf("vtb", [128, 2048], BF16)
    cx.sbuf("kint", [128, 512], I32)
    cx.sbuf("kint2", [128, 512], I32)
    ivr = cx.sbuf("ivf_row", [1, 512], F32)
    cx.sbuf("ivf_b", [128, 512], F32)
    load_wT(cx, dr["Wk"], wtk, "wtk")
    load_wT(cx, dr["Wv"], wtv, "wtv")
    E("sync", lambda e: e.dma_start(out=brow2[0:1, :], in_=dr["bk"][:]),
      writes=["brow2:k"], dma=True)
    bcast_row(cx, "bkb", 0, brow2[0:1, :], 1024, ["brow2:k"])
    E("sync", lambda e: e.dma_start(out=brow2[0:1, :], in_=dr["bv"][:]),
      reads=["bkb"], writes=["brow2:v"], dma=True)
    bcast_row(cx, "bvb2", 0, brow2[0:1, :], 1024, ["brow2:v"])
    E("vector", lambda e: e.tensor_scalar(
        out=cx.sb["bvb2"][:, :], in0=cx.sb["bvb2"][:, :], scalar1=SCL2,
        scalar2=None, op0=ALU.mult),
      reads=["bvb2"], writes=["bvb2"])
    E("gpsimd", lambda e: e.iota(ivr[:, :], pattern=[[1, 512]], base=0,
                                 channel_multiplier=0,
                                 allow_small_or_imprecise_dtypes=True),
      writes=["ivf_row"])
    E("scalar", lambda e: e.activation(
        out=ivr[:, :], in_=ivr[:, :], func=AF.Exp, scale=-LN1E4 / 512.0),
      reads=["ivf_row"], writes=["ivf_row"])
    bcast_row(cx, "ivf_b", 0, ivr[0:1, :], 512, ["ivf_row"])

    for nt in range(NT):
        kb = nt % 2
        ktb = ktb2[:, 1024 * kb:1024 * (kb + 1)]
        vtb = vtb2[:, 1024 * kb:1024 * (kb + 1)]
        kkey, vkey = f"ktb:{kb}", f"vtb:{kb}"
        pump(); pump()
        emit_rope_tables(cx, nt, ss=kb)
        for pi, (wt, wname) in enumerate(((wtk, "wtk"), (wtv, "wtv"))):
            for jh in range(2):
                ps = cx.sb["pA"] if jh == 0 else cx.sb["pB"]
                pk = "pA" if jh == 0 else "pB"
                mm_chain(cx, ps[:, :], pk,
                         [(xnf[cc][:, 128 * nt:128 * (nt + 1)],
                           wt[cc][:, 512 * jh:512 * (jh + 1)],
                           [f"l0xnf{cc}:{nt}", f"{wname}{cc}"])
                          for cc in range(CC)])
                if pi == 0:
                    E("vector", lambda e, ps=ps, pk=pk, jh=jh: e.tensor_tensor(
                        out=scrA[:, :], in0=ps[:, :],
                        in1=bkb[:, 512 * jh:512 * (jh + 1)], op=ALU.add),
                      reads=[pk, "bkb"], writes=["scr512a"])
                    E("scalar", lambda e: e.activation(
                        out=scrB[:, :], in_=scrA[:, :], func=AF.Gelu),
                      reads=["scr512a"], writes=["scr512b"])
                    E("vector", lambda e, jh=jh, ktb=ktb: e.tensor_scalar(
                        out=ktb[:, 512 * jh:512 * (jh + 1)],
                        in0=scrB[:, :], scalar1=0.21, scalar2=None,
                        op0=ALU.add),
                      reads=["scr512b"], writes=[kkey])
                else:
                    E("vector", lambda e, ps=ps, pk=pk, jh=jh, vtb=vtb:
                      e.scalar_tensor_tensor(
                          out=vtb[:, 512 * jh:512 * (jh + 1)], in0=ps[:, :],
                          scalar=SCL2, in1=bvb2[:, 512 * jh:512 * (jh + 1)],
                          op0=ALU.mult, op1=ALU.add),
                      reads=[pk, "bvb2"], writes=[vkey])
        for half in range(2):
            E("tensor", lambda e, nt=nt, half=half, ktb=ktb: e.matmul(
                cx.sb["pD"][0:1, 512 * half:512 * (half + 1)],
                cx.sb["onesb_col"][:, 0:1],
                ktb[:, 512 * half:512 * (half + 1)],
                start=(nt == 0), stop=(nt == NT - 1), skip_group_check=True),
              reads=["onesb_col", kkey], writes=["pD"])
        emit_rope_apply(cx, ktb[:, :], kkey, nt, ss=kb)
        for h in range(NH):
            pair, odd = h // 2, h % 2
            E("tensor", lambda e, nt=nt, h=h, pair=pair, odd=odd, ktb=ktb,
              vtb=vtb: e.matmul(
                cx.sb["pkv"][64 * odd:64 * (odd + 1),
                             64 * pair:64 * (pair + 1)],
                ktb[:, 64 * h:64 * (h + 1)], vtb[:, 64 * h:64 * (h + 1)],
                start=(nt == 0 and h <= 1),
                stop=(nt == NT - 1 and h >= NH - 2),
                skip_group_check=True, tile_position=(0, 64 * odd)),
              reads=[kkey, vkey], writes=["pkv"])
    kmrow = cx.sb["kmrow"]
    E("vector", lambda e: e.tensor_scalar(
        out=kmrow[0:1, :], in0=cx.sb["pD"][0:1, 0:1024], scalar1=1.0 / 2048.0,
        scalar2=None, op0=ALU.mult),
      reads=["pD"], writes=["kmrow"])
    bcast_row(cx, "kmean_b", 0, kmrow[0:1, 0:1024], 1024, ["kmrow"])
    kvbd = cx.sb["kv_bd"]
    E("gpsimd", lambda e: e.memset(kvbd[:, :], 0.0), writes=["kv_bd"])
    E("scalar", lambda e: e.activation(
        out=kvbd[0:64, :].rearrange("p (c x) -> p c x", x=128)[:, :, 0:64],
        in_=cx.sb["pkv"][0:64, :].rearrange("p (c x) -> p c x", x=64),
        func=AF.Copy),
      reads=["pkv", "kv_bd"], writes=["kv_bd"])
    E("scalar", lambda e: e.activation(
        out=kvbd[64:128, :].rearrange("p (c x) -> p c x", x=128)[:, :, 64:128],
        in_=cx.sb["pkv"][64:128, :].rearrange("p (c x) -> p c x", x=64),
        func=AF.Copy),
      reads=["pkv", "kv_bd"], writes=["kv_bd"])
    cx.close_scope()

    if KSTOP == "b1b":
        cx.close_scope()
        return
    cx.scope()
    cx.psum("pout", (128, 1024), F32)
    qrT = cx.sbuf("qrT", [128, 2048], BF16)
    z_all = cx.sbuf("z_all", [128, 256], F32)
    gbrow = cx.sbuf("gbrow", [1, 1024], F32)
    gammab = cx.sbuf("gammab", [128, 1024], F32)
    betab = cx.sbuf("betab", [128, 1024], F32)
    E("sync", lambda e: e.dma_start(out=gbrow[0:1, :], in_=dr["gamma"][:]),
      writes=["gbrow:g"], dma=True)
    bcast_row(cx, "gammab", 0, gbrow[0:1, :], 1024, ["gbrow:g"])
    E("sync", lambda e: e.dma_start(out=gbrow[0:1, :], in_=dr["beta"][:]),
      reads=["gammab"], writes=["gbrow:b"], dma=True)
    bcast_row(cx, "betab", 0, gbrow[0:1, :], 1024, ["gbrow:b"])
    kv_bd = cx.sb["kv_bd"]
    kmean_b = cx.sb["kmean_b"]
    xt2 = cx.sb["xt2"]
    st = cx.sb["st8"]
    resA, scrA1 = cx.sb["res1k"], cx.sb["scr1k"]
    resB = cx.sbuf("res1kB", [128, 1024], F32)
    scrB1 = cx.sbuf("scr1kB", [128, 1024], F32)
    zscr = cx.sbuf("zscr", [128, 1024], F32)
    zTs = cx.sbuf("zTs", [16, 128], F32)
    zbS = cx.sbuf("zbS", [128, 1024], F32)
    cx.sbuf("i16f", [16, 1024], F32)
    E("gpsimd", lambda e: e.memset(cx.sb["i16f"][:, :], 1.0), writes=["i16f"])
    E("gpsimd", lambda e: e.affine_select(
        out=cx.sb["i16f"][:, :], in_=cx.sb["i16f"][:, :],
        compare_op=ALU.is_ge, fill=0.0,
        base=0, pattern=[[1, 1024]], channel_multiplier=-64),
      reads=["i16f"], writes=["i16f"])
    E("gpsimd", lambda e: e.affine_select(
        out=cx.sb["i16f"][:, :], in_=cx.sb["i16f"][:, :],
        compare_op=ALU.is_ge, fill=0.0,
        base=63, pattern=[[-1, 1024]], channel_multiplier=64),
      reads=["i16f"], writes=["i16f"])
    def b2_stageA(nt):
        b = nt % 2
        scr = scrA1 if b == 0 else scrB1
        sk = "scr1k" if b == 0 else "scr1kB"
        xk = f"xt2:{b}"
        E("sync", lambda e, nt=nt, b=b: e.dma_start(
            out=xt2[:, 1024 * b:1024 * (b + 1)],
            in_=dr["x"][128 * nt:128 * (nt + 1), :]),
          writes=[xk], dma=True)
        E("gpsimd", lambda e, nt=nt: e.tensor_tensor(
            out=zscr[:, :], in0=qt[:, 1024 * nt:1024 * (nt + 1)],
            in1=kmean_b[:, :], op=ALU.mult),
          reads=[f"qt:{nt}", "kmean_b"], writes=["zscr"])
        E("vector", lambda e, nt=nt: e.reduce_sum(
            out=z_all[:, 16 * nt:16 * (nt + 1)],
            in_=zscr[:, :].rearrange("p (h d) -> p h d", d=64), axis=AX.X),
          reads=["zscr"], writes=[f"z_all:{nt}"])
        E("vector", lambda e, nt=nt: e.tensor_scalar(
            out=z_all[:, 16 * nt:16 * (nt + 1)],
            in0=z_all[:, 16 * nt:16 * (nt + 1)], scalar1=1e-6, scalar2=None,
            op0=ALU.add),
          reads=[f"z_all:{nt}"], writes=[f"z_all:{nt}"])
        E("vector", lambda e, nt=nt: e.reciprocal(
            out=z_all[:, 16 * nt:16 * (nt + 1)],
            in_=z_all[:, 16 * nt:16 * (nt + 1)]),
          reads=[f"z_all:{nt}"], writes=[f"z_all:{nt}"])
        emit_rope_apply(cx, qt[:, 1024 * nt:1024 * (nt + 1)], f"qt:{nt}", nt,
                        ss=b)
        items = []
        for cc in range(CC):
            def mk_evac(c2, b2_=b):
                def f(e, ps):
                    return e.activation(
                        out=qrT[:, 1024 * b2_ + 128 * c2:
                                1024 * b2_ + 128 * (c2 + 1)], in_=ps,
                        func=AF.Copy)
                return f
            items.append((
                qt[:, 1024 * nt + 128 * cc:1024 * nt + 128 * (cc + 1)],
                [f"qt:{nt}"], mk_evac(cc), [f"qrT:{b}"]))
        transpose_via_ptr(cx, items)

    def b2_stageB(nt):
        b = nt % 2
        res = resA if b == 0 else resB
        scr = scrA1 if b == 0 else scrB1
        rk = "res1k" if b == 0 else "res1kB"
        sk = "scr1k" if b == 0 else "scr1kB"
        xk = f"xt2:{b}"
        for pr in range(8):
            E("tensor", lambda e, pr=pr, b=b: e.matmul(
                cx.sb["pout"][:, 128 * pr:128 * (pr + 1)],
                qrT[:, 1024 * b + 128 * pr:1024 * b + 128 * (pr + 1)],
                kv_bd[:, 128 * pr:128 * (pr + 1)],
                start=True, stop=True, skip_group_check=True),
              reads=[f"qrT:{b}", "kv_bd"], writes=["pout"])
        E("tensor", lambda e, nt=nt: e.transpose(
            cx.sb["pbc"][0:16, 0:128], z_all[:, 16 * nt:16 * (nt + 1)],
            cx.sb["identf"][:, :]),
          reads=[f"z_all:{nt}", "identf"], writes=["pbc"])
        E("vector", lambda e: e.tensor_copy(out=zTs[:, :],
                                            in_=cx.sb["pbc"][0:16, 0:128]),
          reads=["pbc"], writes=["zTs"])
        for dh in range(2):
            E("tensor", lambda e, dh=dh: e.matmul(
                cx.sb["pA" if dh == 0 else "pB"][:, :], zTs[0:16, :],
                cx.sb["i16f"][0:16, 512 * dh:512 * (dh + 1)],
                start=True, stop=True),
              reads=["zTs", "i16f"], writes=["pA" if dh == 0 else "pB"])
        E("vector", lambda e, nt=nt, b=b, scr=scr: e.tensor_scalar(
            out=scr[:, :], in0=xt2[:, 1024 * b:1024 * (b + 1)],
            scalar1=cx.sb["rst"][:, nt:nt + 1],
            scalar2=cx.sb["mrst"][:, nt:nt + 1],
            op0=ALU.mult, op1=ALU.subtract),
          reads=[xk, "l0rst", "l0mrst"], writes=[sk])
        for dh in range(2):
            E("vector", lambda e, dh=dh: e.tensor_copy(
                out=zbS[:, 512 * dh:512 * (dh + 1)],
                in_=cx.sb["pA" if dh == 0 else "pB"][:, :]),
              reads=["pA" if dh == 0 else "pB"], writes=["zbS"])
        E("vector", lambda e, res=res: e.tensor_tensor(
            out=res[:, :], in0=cx.sb["pout"][:, :], in1=zbS[:, :],
            op=ALU.mult),
          reads=["pout", "zbS"], writes=[rk])
        E("vector", lambda e, res=res, scr=scr: e.tensor_tensor(
            out=res[:, :], in0=res[:, :], in1=scr[:, :], op=ALU.add),
          reads=[rk, sk], writes=[rk])
        E("vector", lambda e, res=res: e.reduce_sum(
            out=st[:, 0:1], in_=res[:, :], axis=AX.X),
          reads=[rk], writes=["st8:s"])
        E("scalar", lambda e, res=res, scr=scr: e.activation(
            out=scr[:, :], in_=res[:, :], func=AF.Square,
            accum_out=st[:, 1:2]),
          reads=[rk], writes=[sk, "st8:q"])
        E("vector", lambda e: e.tensor_scalar(
            out=st[:, 2:3], in0=st[:, 0:1], scalar1=1.0 / 1024.0,
            scalar2=None, op0=ALU.mult),
          reads=["st8:s"], writes=["st8:m"])
        E("scalar", lambda e: e.activation(out=st[:, 3:4], in_=st[:, 2:3],
                                           func=AF.Square),
          reads=["st8:m"], writes=["st8:m2"])
        E("vector", lambda e: e.tensor_scalar(
            out=st[:, 4:5], in0=st[:, 1:2], scalar1=1.0 / 1024.0,
            scalar2=1e-12, op0=ALU.mult, op1=ALU.add),
          reads=["st8:q"], writes=["st8:v"])
        E("vector", lambda e: e.tensor_tensor(
            out=st[:, 4:5], in0=st[:, 4:5], in1=st[:, 3:4], op=ALU.subtract),
          reads=["st8:v", "st8:m2"], writes=["st8:v"])
        E("scalar", lambda e: e.activation(out=st[:, 5:6], in_=st[:, 4:5],
                                           func=AF.Sqrt),
          reads=["st8:v"], writes=["st8:sd"])
        E("vector", lambda e: e.reciprocal(out=st[:, 6:7], in_=st[:, 5:6]),
          reads=["st8:sd"], writes=["st8:r"])
        E("vector", lambda e: e.tensor_tensor(
            out=st[:, 7:8], in0=st[:, 2:3], in1=st[:, 6:7], op=ALU.mult),
          reads=["st8:m", "st8:r"], writes=["st8:mr"])
        E("vector", lambda e, res=res, scr=scr: e.tensor_scalar(
            out=scr[:, :], in0=res[:, :], scalar1=st[:, 6:7],
            scalar2=st[:, 7:8], op0=ALU.mult, op1=ALU.subtract),
          reads=[rk, "st8:r", "st8:mr"], writes=[sk])
        E("vector", lambda e, res=res, scr=scr: e.tensor_tensor(
            out=res[:, :], in0=scr[:, :], in1=gammab[:, :], op=ALU.mult),
          reads=[sk, "gammab"], writes=[rk])
        E("vector", lambda e, res=res, scr=scr: e.tensor_tensor(
            out=scr[:, :], in0=res[:, :], in1=betab[:, :], op=ALU.add),
          reads=[rk, "betab"], writes=[sk])
        E("scalar", lambda e, nt=nt, scr=scr: e.dma_start(
            out=dr["s0"][128 * nt:128 * (nt + 1), :], in_=scr[:, :]),
          reads=[sk], writes=[f"s0d:{nt}"], dma=True)

    b2_stageA(0)
    b2_stageA(1)
    for nt in range(NT):
        b2_stageB(nt)
        if nt + 2 < NT:
            b2_stageA(nt + 2)
    cx.close_scope()
    cx.close_scope()

    if KSTOP == "b2":
        return
    cx.close_scope()

    for _ in range(40):
        pump()
    emit_l0_gating_final(cx)
    BIG = 1024.0
    slot_tf = cx.sbuf("slot_tf", [128, 128], F32)
    slot_ti = cx.sbuf("slot_ti", [128, 128], I32)
    cx.scope()
    cx.psum("pE", (128, 512), F32)
    cx.psum("pF", (128, 512), F32)
    cx.scope()
    mpk = cx.sbuf("mpk", [128, 128], F32)
    ltr = cx.sbuf("ltr", [128, 128], BF16)
    bsm = cx.sbuf("bsm", [128, 128], BF16)
    cntb = cx.sbuf("cntb", [128, 1], BF16)
    offs = cx.sbuf("offs", [128, 1], F32)
    th = cx.sb["th"]
    s8, s8t = cx.sb["s8"], cx.sb["s8t"]
    thb, cnt_col, ind_sb = cx.sb["thb"], cx.sb["cnt_col"], cx.sb["ind_sb"]
    pbc = cx.sb["pbc"]
    E("tensor", lambda e: e.matmul(
        pbc[0:128, 0:1], s8[0:8, :], th[0:8, 1:2], start=True, stop=True),
      reads=["s8", "th:hi"], writes=["pbc"])
    E("vector", lambda e: e.tensor_copy(out=thb[:, 0:1], in_=pbc[0:128, 0:1]),
      reads=["pbc"], writes=["thb"])
    E("vector", lambda e: e.tensor_scalar(
        out=ind_sb[:, :], in0=cx.sb["aff_pack"][:, :], scalar1=thb[:, 0:1],
        scalar2=0.0, op0=ALU.is_ge, op1=ALU.add, accum_out=cnt_col[:, 0:1]),
      reads=["aff_pack", "thb"], writes=["ind_f", "cnt_col"])
    E("vector", lambda e: e.tensor_copy(out=mpk[:, :], in_=ind_sb[:, :]),
      reads=["ind_f"], writes=["mpk"])
    E("vector", lambda e: e.tensor_copy(out=cntb[:, 0:1], in_=cnt_col[:, 0:1]),
      reads=["cnt_col"], writes=["cntb"])
    E("gpsimd", lambda e: e.memset(ltr[:, :], 1.0), writes=["ltr"])
    E("gpsimd", lambda e: e.affine_select(
        out=ltr[:, :], in_=ltr[:, :], compare_op=ALU.is_ge, fill=0.0,
        base=-1, pattern=[[1, 128]], channel_multiplier=-1),
      reads=["ltr"], writes=["ltr"])
    s8b = cx.sbuf("s8b", [8, 128], BF16)
    E("vector", lambda e: e.tensor_copy(out=s8b[0:8, :], in_=s8[0:8, :]),
      reads=["s8"], writes=["s8b"])
    E("tensor", lambda e: e.matmul(
        pbc[0:128, 0:128], s8b[0:8, :], s8b[0:8, :], start=True, stop=True),
      reads=["s8b"], writes=["pbc"])
    E("vector", lambda e: e.tensor_copy(out=bsm[:, :], in_=pbc[0:128, 0:128]),
      reads=["pbc"], writes=["bsm"])
    E("vector", lambda e: e.tensor_tensor(
        out=ltr[:, :], in0=ltr[:, :], in1=bsm[:, :], op=ALU.mult),
      reads=["ltr", "bsm"], writes=["ltr"])
    E("tensor", lambda e: e.matmul(
        pbc[0:128, 0:1], ltr[:, :], cntb[:, 0:1], start=True, stop=True),
      reads=["ltr", "cntb"], writes=["pbc"])
    E("vector", lambda e: e.tensor_copy(out=offs[:, 0:1], in_=pbc[0:128, 0:1]),
      reads=["pbc"], writes=["offs"])
    ev8i = cx.sbuf("ev8i", [8, 1], I32)
    ev8 = cx.sbuf("ev8", [8, 1], F32)
    bigcol = cx.sbuf("bigcol", [128, 1], F32)
    offsb = cx.sbuf("offsb", [128, 1], F32)
    E("gpsimd", lambda e: e.iota(ev8i[:, :], pattern=[[0, 1]], base=0,
                                 channel_multiplier=1),
      writes=["ev8i"])
    E("vector", lambda e: e.tensor_copy(out=ev8[:, :], in_=ev8i[:, :]),
      reads=["ev8i"], writes=["ev8"])
    E("tensor", lambda e: e.matmul(
        pbc[0:128, 4:5], s8[0:8, :], ev8[0:8, 0:1], start=True, stop=True),
      reads=["s8", "ev8"], writes=["pbc"])
    E("vector", lambda e: e.scalar_tensor_tensor(
        out=bigcol[:, 0:1], in0=pbc[0:128, 4:5], scalar=-16.0,
        in1=cx.sb["iota_col"][:, 0:1], op0=ALU.mult, op1=ALU.add),
      reads=["pbc", "iota_col"], writes=["ntcol"])
    E("vector", lambda e: e.tensor_tensor(
        out=offsb[:, 0:1], in0=offs[:, 0:1], in1=bigcol[:, 0:1],
        op=ALU.subtract),
      reads=["offs", "ntcol"], writes=["offsb"])
    E("vector", lambda e: e.tensor_scalar(
        out=offsb[:, 0:1], in0=offsb[:, 0:1], scalar1=-1.0 - BIG,
        scalar2=None, op0=ALU.add),
      reads=["offsb"], writes=["offsb"])
    E("vector", lambda e: e.tensor_scalar(
        out=bigcol[:, 0:1], in0=bigcol[:, 0:1], scalar1=BIG,
        scalar2=None, op0=ALU.add),
      reads=["ntcol"], writes=["bigcol"])
    E("vector", lambda e: e.tensor_tensor_scan(
        out=mpk[:, :], data0=mpk[:, :], data1=mpk[:, :], initial=0.0,
        op0=ALU.add, op1=ALU.bypass),
      reads=["mpk"], writes=["mpk2"])
    E("vector", lambda e: e.tensor_scalar(
        out=mpk[:, :], in0=mpk[:, :], scalar1=offsb[:, 0:1],
        scalar2=None, op0=ALU.add),
      reads=["mpk2", "offsb"], writes=["mpk3"])
    E("vector", lambda e: e.tensor_tensor(
        out=mpk[:, :], in0=mpk[:, :],
        in1=ind_sb[:, :], op=ALU.mult),
      reads=["mpk3", "ind_f"], writes=["mpk4"])
    E("vector", lambda e: e.tensor_scalar(
        out=mpk[:, :], in0=mpk[:, :], scalar1=bigcol[:, 0:1],
        scalar2=None, op0=ALU.add),
      reads=["mpk4", "bigcol"], writes=["mpk5"])
    E("tensor", lambda e: e.transpose(
        pbc[0:128, 0:128], mpk[:, :], cx.sb["identf"][:, :]),
      reads=["mpk5", "identf"], writes=["pbc"])
    E("vector", lambda e: e.tensor_copy(out=slot_tf[:, :],
                                        in_=pbc[0:128, 0:128]),
      reads=["pbc"], writes=["slot_tf"])
    E("vector", lambda e: e.tensor_copy(out=slot_ti[:, :], in_=slot_tf[:, :]),
      reads=["slot_tf"], writes=["slot_ti"])
    cx.close_scope()

    zb128 = cx.sbuf("zb128", [128, 1040], BF16)
    z128f = cx.sbuf("z128f", [128, 1024], F32)
    E("gpsimd", lambda e: e.memset(zb128[:, :], 0.0), writes=["zb128"])
    E("gpsimd", lambda e: e.memset(z128f[:, :], 0.0), writes=["z128f"])
    for nt_z in range(NT):
        E("sync", lambda e, nt_z=nt_z: e.dma_start(
            out=dr["g1d"][128 * nt_z:128 * (nt_z + 1), :], in_=z128f[:, :]),
          reads=["z128f"], writes=[f"g1z:{nt_z}"], dma=True)
    xg_keys = {xi: [f"xg:{xi}:{nt}" for nt in range(NT)] for xi in range(2)}
    for xi in range(2):
        for r0 in range(0, 1040, 128):
            rn = min(128, 1040 - r0)
            E("scalar", lambda e, xi=xi, r0=r0, rn=rn: e.dma_start(
                out=dr[f"xg{xi}"][r0:r0 + rn, :], in_=zb128[0:rn, :]),
              reads=["zb128"], writes=xg_keys[xi], dma=True)
    meta4 = cx.sbuf("meta4", [128, 512], F32)
    wb_b = cx.sbuf("wb_b", [128, 128], BF16)
    wlo_t = cx.sbuf("wlo_t", [128, 128], F32)
    iota8i = cx.sbuf("iota8i", [128, 8], I32)
    iota8 = cx.sbuf("iota8", [128, 8], F32)
    w_t = cx.sb["w_t"]
    wtk = [f"w_t:{nt}" for nt in range(NT)]
    E("vector", lambda e: e.tensor_copy(out=wb_b[:, :], in_=w_t[:, :]),
      reads=wtk, writes=["wb_b"])
    E("vector", lambda e: e.tensor_tensor(
        out=wlo_t[:, :], in0=w_t[:, :], in1=wb_b[:, :], op=ALU.subtract),
      reads=wtk + ["wb_b"], writes=["wlo_t"])
    E("gpsimd", lambda e: e.iota(iota8i[:, :], pattern=[[0, 8]], base=0,
                                 channel_multiplier=1),
      writes=["iota8i"])
    E("vector", lambda e: e.tensor_copy(out=iota8[:, :], in_=iota8i[:, :]),
      reads=["iota8i"], writes=["iota8"])
    m4v = meta4[:, :].rearrange("p (e nt four) -> p e nt four", e=8, four=4)
    for e_i in range(NE):
        wv = w_t[:, :].rearrange("p (nt e) -> p nt e", e=8)[:, :, e_i]
        lv = wlo_t[:, :].rearrange("p (nt e) -> p nt e", e=8)[:, :, e_i]
        E("vector", lambda e, e_i=e_i, wv=wv: e.tensor_copy(
            out=m4v[:, e_i, :, 0], in_=wv),
          reads=wtk, writes=[f"m4w:{e_i}"])
        E("vector", lambda e, e_i=e_i, lv=lv: e.tensor_copy(
            out=m4v[:, e_i, :, 1], in_=lv),
          reads=["wlo_t"], writes=[f"m4l:{e_i}"])
    for nt in range(NT):
        E("vector", lambda e, nt=nt: e.tensor_scalar(
            out=m4v[:, :, nt, 2], in0=iota8[:, :], scalar1=0.0,
            scalar2=float(nt // 2), op0=ALU.mult, op1=ALU.add),
          reads=["iota8"], writes=[f"m4h:{nt}"])
        E("vector", lambda e, nt=nt: e.tensor_scalar(
            out=m4v[:, :, nt, 3], in0=iota8[:, :],
            scalar1=float(128 * (nt & 1)), scalar2=None, op0=ALU.add),
          reads=["iota8"], writes=[f"m4o:{nt}"])
    xgT = [cx.sbuf(f"xgT{c}", [128, 1040], BF16) for c in range(CC)]
    mcolb = cx.sbuf("mcolb", [128, 64], BF16)
    mcol = cx.sbuf("mcol", [128, 48], F32)
    wcol2 = cx.sbuf("wcol2", [128, 16], F32)
    idxi2 = cx.sbuf("idxi2", [128, 16], I32)
    xbuf = cx.sbuf("xbuf", [128, 2080], BF16)
    h_all = cx.sbuf("h_all", [128, 16 * 1024], BF16)
    wgu = cx.sbuf("wgu", [128, 16384], BF16)
    wd2 = cx.sbuf("wd2", [128, 32768], BF16)
    w_t = cx.sb["w_t"]
    scrA = cx.sb["scr512a"]
    def load_wd2(e_i):
        wdb = e_i % 2
        for k4 in range(4):
            E("gpsimd", lambda eng, e_i=e_i, k4=k4, wdb=wdb: eng.dma_start(
                out=wd2[:, 16384 * wdb + 4096 * k4:
                        16384 * wdb + 4096 * (k4 + 1)].rearrange(
                    "p (c j) -> p c j", j=1024),
                in_=dr["Wd"][e_i, 512 * k4:512 * (k4 + 1), :].rearrange(
                    "(c p) j -> p c j", p=128)),
              writes=[f"wd2:{wdb}:{ic}" for ic in range(4 * k4, 4 * k4 + 4)],
              dma=True)

    def prep_scat(e_i):
        xgb = e_i % 2
        for nt in range(NT):
            xb = nt % 2
            E("sync", lambda eng, nt=nt, xb=xb: eng.dma_start(
                out=xbuf[:, 1040 * xb:1040 * xb + 1024],
                in_=dr["xnt"][128 * nt:128 * (nt + 1), :]),
              writes=[f"xbuf:{xb}"], dma=True)
            E("scalar", lambda eng, e_i=e_i, nt=nt, xb=xb: eng.activation(
                out=xbuf[:, 1040 * xb + 1024:1040 * xb + 1028],
                in_=meta4[:, 4 * (16 * e_i + nt):4 * (16 * e_i + nt) + 4],
                func=AF.Copy),
              reads=[f"m4w:{e_i}", f"m4l:{e_i}", f"m4h:{nt}", f"m4o:{nt}"],
              writes=[f"xbufm:{xb}"])
            E("gpsimd", lambda eng, e_i=e_i, nt=nt, xb=xb, xgb=xgb:
              eng.indirect_dma_start(
                  out=dr[f"xg{xgb}"][:, :],
                  out_offset=IndirectOffsetOnAxis(
                      slot_ti[:, 16 * e_i + nt:16 * e_i + nt + 1], 0),
                  in_=xbuf[:, 1040 * xb:1040 * (xb + 1)], in_offset=None),
              reads=[f"xbuf:{xb}", f"xbufm:{xb}", "slot_ti"],
              writes=[f"xg:{xgb}:{nt}"], dma=True)

    def prep_xgt(e_i):
        xgb = e_i % 2
        for cc in range(CC):
            E("scalar",
              lambda eng, cc=cc, xgb=xgb: eng.dma_start(
                out=xgT[cc][:, :],
                in_=dr[f"xg{xgb}"][:, 128 * cc:128 * (cc + 1)],
                transpose=True),
              reads=xg_keys[xgb], writes=[f"xgT{cc}"], dma=True)
        E("scalar", lambda eng, xgb=xgb: eng.dma_start(
            out=mcolb[:, 32 * xgb:32 * (xgb + 1)].rearrange(
                "p (g c) -> p g c", c=4),
            in_=dr[f"xg{xgb}"][0:1024, 1024:1028].rearrange(
                "(g p) c -> p g c", p=128)),
          reads=xg_keys[xgb], writes=[f"xmT:{xgb}"], dma=True)
        for zg in range(8):
            E("scalar", lambda eng, xgb=xgb, zg=zg: eng.dma_start(
                out=dr[f"xg{xgb}"][128 * zg:128 * (zg + 1), 1024:1040],
                in_=zb128[:, 0:16]),
              reads=["zb128", f"xmT:{xgb}"], writes=xg_keys[xgb], dma=True)
        E("scalar", lambda eng, xgb=xgb: eng.activation(
            out=mcol[:, 0:32], in_=mcolb[:, 32 * xgb:32 * (xgb + 1)],
            func=AF.Copy),
          reads=[f"xmT:{xgb}"], writes=["mcol"])
        mv = mcol[:, 0:32].rearrange("p (g four) -> p g four", four=4)
        E("vector", lambda eng, xgb=xgb: eng.tensor_tensor(
            out=wcol2[:, 8 * xgb:8 * (xgb + 1)], in0=mv[:, :, 0],
            in1=mv[:, :, 1], op=ALU.add),
          reads=["mcol"], writes=[f"wcol:{xgb}"])
        E("vector", lambda eng: eng.tensor_scalar(
            out=mcol[:, 40:48], in0=mv[:, :, 2], scalar1=256.0,
            scalar2=None, op0=ALU.mult),
          reads=["mcol"], writes=["mcolx"])
        E("vector", lambda eng: eng.tensor_tensor(
            out=mcol[:, 40:48], in0=mcol[:, 40:48], in1=mv[:, :, 3],
            op=ALU.add),
          reads=["mcolx", "mcol"], writes=["mcolx"])
        E("vector", lambda eng, xgb=xgb: eng.tensor_copy(
            out=idxi2[:, 8 * xgb:8 * (xgb + 1)], in_=mcol[:, 40:48]),
          reads=["mcolx"], writes=[f"idxi:{xgb}"])

    def wgu_load(e_i, ig):
        gb = ig % 2
        E("gpsimd", lambda eng, e_i=e_i, ig=ig, gb=gb: eng.dma_start(
            out=wgu[:, 8192 * gb:8192 * gb + 4096].rearrange(
                "p (c i) -> p c i", i=512),
            in_=dr["Wg"][e_i, :, 512 * ig:512 * (ig + 1)].rearrange(
                "(c p) i -> p c i", p=128)),
          writes=[f"wgu:g{gb}"], dma=True)
        E("gpsimd", lambda eng, e_i=e_i, ig=ig, gb=gb: eng.dma_start(
            out=wgu[:, 8192 * gb + 4096:8192 * gb + 8192].rearrange(
                "p (c i) -> p c i", i=512),
            in_=dr["Wu"][e_i, :, 512 * ig:512 * (ig + 1)].rearrange(
                "(c p) i -> p c i", p=128)),
          writes=[f"wgu:u{gb}"], dma=True)

    def phase1(e_i, skip01=False):
        for ig in range(4):
            gb = ig % 2
            if ig >= 2 or not skip01:
                wgu_load(e_i, ig)
            for sub in range(4):
                ic = 4 * ig + sub
                for ks in range(2):
                    par = (2 * sub + ks) % 2
                    pg, pgk = (("pA", "pA") if par == 0 else ("pE", "pE"))
                    pu, puk = (("pB", "pB") if par == 0 else ("pF", "pF"))
                    sA, sAk = (("scr512a", "scr512a") if par == 0
                               else ("scr512b", "scr512b"))
                    mm_chain(cx, cx.sb[pg][:, :], pgk,
                             [(wgu[:, 8192 * gb + 512 * cc + 128 * sub:
                                   8192 * gb + 512 * cc + 128 * (sub + 1)],
                               xgT[cc][:, 512 * ks:512 * (ks + 1)],
                               [f"wgu:g{gb}", f"xgT{cc}"])
                              for cc in range(CC)])
                    mm_chain(cx, cx.sb[pu][:, :], puk,
                             [(wgu[:, 8192 * gb + 4096 + 512 * cc + 128 * sub:
                                   8192 * gb + 4096 + 512 * cc + 128 * (sub + 1)],
                               xgT[cc][:, 512 * ks:512 * (ks + 1)],
                               [f"wgu:u{gb}", f"xgT{cc}"])
                              for cc in range(CC)])
                    E("scalar", lambda eng, pg=pg, sA=sA: eng.activation(
                        out=cx.sb[sA][:, :], in_=cx.sb[pg][:, :],
                        func=AF.Silu),
                      reads=[pgk], writes=[sAk])
                    E("vector", lambda eng, ic=ic, ks=ks, pu=pu, sA=sA:
                      eng.tensor_tensor(
                        out=h_all[:, 1024 * ic + 512 * ks:
                                  1024 * ic + 512 * (ks + 1)],
                        in0=cx.sb[sA][:, :], in1=cx.sb[pu][:, :],
                        op=ALU.mult),
                      reads=[sAk, puk], writes=[f"h:{ic}:{ks}"])

    def phase2(e_i):
        wdb = e_i % 2
        xgb = e_i % 2
        for kt in range(8):
            cb = kt % 2
            ctb = cx.sb["res1k"] if cb == 0 else cx.sb["scr1k"]
            ck = "res1k" if cb == 0 else "scr1k"
            for dh in range(2):
                ps = cx.sb["pE"] if dh == 0 else cx.sb["pF"]
                pk = "pE" if dh == 0 else "pF"
                mm_chain(cx, ps[:, :], pk,
                         [(h_all[:, 1024 * ic + 128 * kt:
                                 1024 * ic + 128 * (kt + 1)],
                           wd2[:, 16384 * wdb + 1024 * ic + 512 * dh:
                               16384 * wdb + 1024 * ic + 512 * (dh + 1)],
                           [f"h:{ic}:{kt // 4}", f"wd2:{wdb}:{ic}"])
                          for ic in range(16)])
                E("vector", lambda eng, ps=ps, pk=pk, dh=dh, ctb=ctb,
                  kt=kt, xgb=xgb: eng.tensor_scalar(
                      out=ctb[:, 512 * dh:512 * (dh + 1)], in0=ps[:, :],
                      scalar1=wcol2[:, 8 * xgb + kt:8 * xgb + kt + 1],
                      scalar2=None, op0=ALU.mult),
                  reads=[pk, f"wcol:{xgb}"], writes=[ck])
            prev = ([f"g1z:{nt_z}" for nt_z in range(NT)] if e_i == 0
                    else [f"g1w:{e_i - 1}:{k}" for k in range(8)])
            E("gpsimd", lambda eng, e_i=e_i, kt=kt, ctb=ctb, xgb=xgb:
              eng.indirect_dma_start(
                  out=dr["g1d"][:, :],
                  out_offset=IndirectOffsetOnAxis(
                      idxi2[:, 8 * xgb + kt:8 * xgb + kt + 1], 0),
                  in_=ctb[:, :], in_offset=None, compute_op=ALU.add),
              reads=[ck, f"idxi:{xgb}"] + prev,
              writes=[f"g1w:{e_i}:{kt}"], dma=True)

    for xb in range(2):
        E("gpsimd", lambda eng, xb=xb: eng.memset(
            xbuf[:, 1040 * xb + 1028:1040 * (xb + 1)], 0.0),
          writes=[f"xbufm:{xb}"])
    load_wd2(0)
    load_wd2(1)
    prep_scat(0)
    prep_scat(1)
    for e_i in range(NE):
        prep_xgt(e_i)
        phase1(e_i, skip01=(e_i > 0))
        if e_i + 1 < NE:
            wgu_load(e_i + 1, 0)
            wgu_load(e_i + 1, 1)
        if e_i + 2 < NE:
            prep_scat(e_i + 2)
        phase2(e_i)
        if e_i + 2 < NE:
            load_wd2(e_i + 2)
    cx.close_scope()

    if KSTOP == "c":
        return
    cx.scope()
    stg2 = cx.sbuf("stg2", [128, 2048], F32)
    acc2 = cx.sbuf("acc2", [128, 2048], F32)
    xt2 = cx.sb["xt2"]
    g1done = [f"g1w:{NE - 1}:{k}" for k in range(8)]
    for nt in range(NT):
        b = nt % 2
        acc = acc2[:, 1024 * b:1024 * (b + 1)]
        ak = f"acc2:{b}"
        E("sync", lambda e, nt=nt, b=b: e.dma_start(
            out=xt2[:, 1024 * b:1024 * (b + 1)],
            in_=dr["x"][128 * nt:128 * (nt + 1), :]),
          writes=[f"xt2:{b}"], dma=True)
        E("scalar", lambda e, nt=nt, b=b: e.dma_start(
            out=stg2[:, 1024 * b:1024 * (b + 1)],
            in_=dr["s0"][128 * nt:128 * (nt + 1), :]),
          reads=[f"s0d:{nt}"], writes=[f"stg2:{b}"], dma=True)
        E("sync", lambda e, nt=nt, acc=acc: e.dma_start(
            out=acc[:, :], in_=dr["g1d"][128 * nt:128 * (nt + 1), :]),
          reads=g1done, writes=[ak], dma=True)
        E("scalar", lambda e, acc=acc: e.activation(
            out=acc[:, :], in_=acc[:, :], func=AF.Silu),
          reads=[ak], writes=[ak])
        E("vector", lambda e, acc=acc, b=b: e.tensor_tensor(
            out=acc[:, :], in0=acc[:, :],
            in1=stg2[:, 1024 * b:1024 * (b + 1)], op=ALU.mult),
          reads=[ak, f"stg2:{b}"], writes=[ak])
        E("vector", lambda e, acc=acc, b=b: e.tensor_tensor(
            out=acc[:, :], in0=acc[:, :],
            in1=xt2[:, 1024 * b:1024 * (b + 1)], op=ALU.add),
          reads=[ak, f"xt2:{b}"], writes=[ak])
        E("sync", lambda e, acc=acc, nt=nt: e.dma_start(
            out=dr["x1"][128 * nt:128 * (nt + 1), :], in_=acc[:, :]),
          reads=[ak], writes=[f"x1d:{nt}"], dma=True)
    cx.close_scope()


def emit_l1_routing(cx, dr):
    p = cx.p
    E = p.emit
    xnf = [cx.sb[f"l1xnf{c}"] for c in range(CC)]
    xnlo = [cx.sb[f"l1xnlo{c}"] for c in range(CC)]
    tsgb = cx.sb["tsgb"]
    sgn_f = cx.sb["sgn_f"]
    o_all = cx.sb["o_all"]
    o_T = cx.sb["o_T"]
    st = cx.sb["st8"]
    E("gpsimd", lambda e: e.dma_start(out=tsgb[:, :], in_=dr["tsig"][:, :]),
      writes=["tsgb"], dma=True)
    E("scalar", lambda e: e.activation(out=tsgb[:, :], in_=tsgb[:, :],
                                       func=AF.Sign),
      reads=["tsgb"], writes=["tsgb"])
    ptr = cx.sb["ptr"]
    for cc in range(CC):
        bank = cc % 2
        bkey = "ptrA" if bank == 0 else "ptrB"
        E("tensor", lambda e, cc=cc, bank=bank: e.transpose(
            ptr[:, 1024 * bank:1024 * bank + 8],
            tsgb[0:8, 128 * cc:128 * (cc + 1)], cx.sb["identb"][0:8, 0:8]),
          reads=["tsgb", "identb"], writes=[bkey])
        E("scalar", lambda e, cc=cc, bank=bank: e.activation(
            out=sgn_f[:, 8 * cc:8 * (cc + 1)],
            in_=ptr[:, 1024 * bank:1024 * bank + 8],
            func=AF.Copy),
          reads=[bkey], writes=["sgn_f"])
    pA = cx.sb["pA"]
    for nt in range(NT):
        pairs = []
        for cc in range(CC):
            pairs.append((xnf[cc][:, 128 * nt:128 * (nt + 1)],
                          sgn_f[:, 8 * cc:8 * (cc + 1)],
                          [f"l1xnf{cc}:{nt}", "sgn_f"]))
        for cc in range(CC):
            pairs.append((xnlo[cc][:, 128 * nt:128 * (nt + 1)],
                          sgn_f[:, 8 * cc:8 * (cc + 1)],
                          [f"l1xnlo{cc}:{nt}", "sgn_f"]))
        mm_chain(cx, pA[:, 0:8], "pA", pairs)
        E("vector", lambda e: e.reduce_max(out=st[:, 0:1], in_=pA[:, 0:8],
                                           axis=AX.X),
          reads=["pA"], writes=["st8:mx"])
        E("vector", lambda e, nt=nt: e.tensor_scalar(
            out=o_all[:, 8 * nt:8 * (nt + 1)], in0=pA[:, 0:8],
            scalar1=st[:, 0:1], scalar2=None, op0=ALU.is_ge),
          reads=["pA", "st8:mx"], writes=[f"o_all:{nt}"])
        E("tensor", lambda e, nt=nt: e.transpose(
            cx.sb["pbc"][0:8, 0:128],
            o_all[:, 8 * nt:8 * (nt + 1)], cx.sb["identf"][:, :]),
          reads=[f"o_all:{nt}", "identf"], writes=["pbc"])
        E("scalar", lambda e, nt=nt: e.activation(
            out=o_T[0:8, 128 * nt:128 * (nt + 1)],
            in_=cx.sb["pbc"][0:8, 0:128],
            func=AF.Copy),
          reads=["pbc"], writes=[f"o_T:{nt}"])


def emit_l1_ffn(cx, dr):
    p = cx.p
    E = p.emit
    xnf = [cx.sb[f"l1xnf{c}"] for c in range(CC)]
    w1b = cx.sb["w1b"]
    b1sb = cx.sb["b1sb"]
    acc1sb = cx.sb["acc1sb"]
    o_all = cx.sb["o_all"]
    o_T = cx.sb["o_T"]
    res1k = cx.sb["res1k"]
    E("gpsimd", lambda e: e.dma_start(out=b1sb[:, :], in_=dr["b1"][:, :]),
      writes=["b1sb"], dma=True)
    for t in range(8):
        wb = t % 2
        for cc in range(CC):
            E("gpsimd", lambda e, t=t, cc=cc, wb=wb: e.dma_start(
                out=w1b[:, 8192 * wb + 1024 * cc:8192 * wb + 1024 * (cc + 1)],
                in_=dr["W1"][t, 128 * cc:128 * (cc + 1), :]),
              writes=[f"w1b:{wb}:{cc}"], dma=True)
        for nt in range(NT):
            cb = nt % 2
            for dh in range(2):
                ps = cx.sb["pA"] if dh == 0 else cx.sb["pB"]
                pk = "pA" if dh == 0 else "pB"
                mm_chain(cx, ps[:, :], pk,
                         [(xnf[cc][:, 128 * nt:128 * (nt + 1)],
                           w1b[:, 8192 * wb + 1024 * cc + 512 * dh:
                               8192 * wb + 1024 * cc + 512 * (dh + 1)],
                           [f"l1xnf{cc}:{nt}", f"w1b:{wb}:{cc}"])
                          for cc in range(CC)])
                if t == 0:
                    E("vector", lambda e, ps=ps, pk=pk, nt=nt, dh=dh, t=t:
                      e.tensor_scalar(
                          out=acc1sb[:, 1024 * nt + 512 * dh:
                                     1024 * nt + 512 * (dh + 1)],
                          in0=ps[:, :],
                          scalar1=o_all[:, 8 * nt + t:8 * nt + t + 1],
                          scalar2=None, op0=ALU.mult),
                      reads=[pk, f"o_all:{nt}"], writes=[f"a1s:{nt}"])
                else:
                    E("vector", lambda e, ps=ps, pk=pk, nt=nt, dh=dh, t=t:
                      e.scalar_tensor_tensor(
                          out=acc1sb[:, 1024 * nt + 512 * dh:
                                     1024 * nt + 512 * (dh + 1)],
                          in0=ps[:, :],
                          scalar=o_all[:, 8 * nt + t:8 * nt + t + 1],
                          in1=acc1sb[:, 1024 * nt + 512 * dh:
                                     1024 * nt + 512 * (dh + 1)],
                          op0=ALU.mult, op1=ALU.add),
                      reads=[pk, f"o_all:{nt}", f"a1s:{nt}"],
                      writes=[f"a1s:{nt}"])
    for nt in range(NT):
        for dh in range(2):
            ps = cx.sb["pA"] if dh == 0 else cx.sb["pB"]
            pk = "pA" if dh == 0 else "pB"
            E("tensor", lambda e, nt=nt, dh=dh, ps=ps: e.matmul(
                ps[:, :], o_T[0:8, 128 * nt:128 * (nt + 1)],
                b1sb[0:8, 512 * dh:512 * (dh + 1)], start=True, stop=True),
              reads=[f"o_T:{nt}", "b1sb"], writes=[pk])
            E("vector", lambda e, ps=ps, pk=pk, nt=nt, dh=dh: e.tensor_tensor(
                out=acc1sb[:, 1024 * nt + 512 * dh:1024 * nt + 512 * (dh + 1)],
                in0=acc1sb[:, 1024 * nt + 512 * dh:1024 * nt + 512 * (dh + 1)],
                in1=ps[:, :], op=ALU.add),
              reads=[pk, f"a1s:{nt}"], writes=[f"a1s:{nt}"])
        E("sync", lambda e, nt=nt: e.dma_start(
            out=dr["acc1"][128 * nt:128 * (nt + 1), :],
            in_=acc1sb[:, 1024 * nt:1024 * (nt + 1)]),
          reads=[f"a1s:{nt}"], writes=[f"acc1d:{nt}"], dma=True)


def emit_l1_attention(cx, dr):
    p = cx.p
    E = p.emit
    xnf = [cx.sb[f"l1xnf{c}"] for c in range(CC)]
    q_f = [cx.sb[f"q_f{c}"] for c in range(CC)]
    k_f = [cx.sb[f"k_f{c}"] for c in range(CC)]
    vt1 = cx.sb["vt1"]

    cx.scope()
    qwb = [cx.sbuf(f"qwb{c}", [128, 1024], BF16) for c in range(CC)]
    kwb = [cx.sbuf(f"kwb{c}", [128, 1024], BF16) for c in range(CC)]
    for cc in range(CC):
        E("gpsimd", lambda e, cc=cc: e.dma_start(
            out=qwb[cc][:, :], in_=dr["qw"][128 * cc:128 * (cc + 1), :]),
          writes=[f"qwb{cc}"], dma=True)
        E("gpsimd", lambda e, cc=cc: e.dma_start(
            out=kwb[cc][:, :], in_=dr["kw"][128 * cc:128 * (cc + 1), :]),
          writes=[f"kwb{cc}"], dma=True)
    for jc in range(CC):
        for ns in range(4):
            for which in ("q", "k"):
                wsrc = qwb if which == "q" else kwb
                dst = q_f if which == "q" else k_f
                dn = "q_f" if which == "q" else "k_f"
                wn = "qwb" if which == "q" else "kwb"
                ps = cx.sb["pA"] if which == "q" else cx.sb["pB"]
                pk = "pA" if which == "q" else "pB"
                mm_chain(cx, ps[:, :], pk,
                         [(wsrc[cc][:, 128 * jc:128 * (jc + 1)],
                           xnf[cc][:, 512 * ns:512 * (ns + 1)],
                           [f"{wn}{cc}"] +
                           [f"l1xnf{cc}:{4 * ns + u}" for u in range(4)])
                          for cc in range(CC)])
                E("scalar", lambda e, ps=ps, pk=pk, jc=jc, ns=ns, dst=dst,
                  dn=dn: e.activation(
                      out=dst[jc][:, 512 * ns:512 * (ns + 1)], in_=ps[:, :],
                      func=AF.Copy),
                  reads=[pk], writes=[f"{dn}{jc}"])
    cx.close_scope()
    cx.scope()
    vwb = [cx.sbuf(f"vwb{c}", [128, 1024], BF16) for c in range(CC)]
    for cc in range(CC):
        E("gpsimd", lambda e, cc=cc: e.dma_start(
            out=vwb[cc][:, :], in_=dr["vw"][128 * cc:128 * (cc + 1), :]),
          writes=[f"vwb{cc}"], dma=True)
    for nt in range(NT):
        for jh in range(2):
            ps = cx.sb["pA"] if jh == 0 else cx.sb["pB"]
            pk = "pA" if jh == 0 else "pB"
            mm_chain(cx, ps[:, :], pk,
                     [(xnf[cc][:, 128 * nt:128 * (nt + 1)],
                       vwb[cc][:, 512 * jh:512 * (jh + 1)],
                       [f"vwb{cc}", f"l1xnf{cc}:{nt}"]) for cc in range(CC)])
            E("vector", lambda e, ps=ps, pk=pk, nt=nt, jh=jh: e.tensor_copy(
                out=vt1[:, 1040 * nt + 520 * jh:1040 * nt + 520 * (jh + 1)]
                .rearrange("p (h x) -> p h x", x=65)[:, :, 0:64],
                in_=ps[:, :].rearrange("p (h x) -> p h x", x=64)),
              reads=[pk], writes=[f"vt1:{nt}"])
        E("vector", lambda e, nt=nt: e.memset(
            vt1[:, 1040 * nt:1040 * (nt + 1)]
            .rearrange("p (h x) -> p h x", x=65)[:, :, 64:65], 1.0),
          reads=[], writes=[f"vt1:{nt}"])
    cx.close_scope()


def emit_l1_attention_core(cx, dr):
    p = cx.p
    E = p.emit
    q_f = [cx.sb[f"q_f{c}"] for c in range(CC)]
    k_f = [cx.sb[f"k_f{c}"] for c in range(CC)]
    vt1 = cx.sb["vt1"]
    aof = [cx.sb[f"aof{c}"] for c in range(CC)]
    tmp_o = cx.sb["tmp_o"]
    ptb = cx.sb["ptb"]
    bcol = cx.sb["bcol"]
    slot_ctr = [0]
    zrow = cx.sb["zrow"]
    zrow0 = cx.sb["zrow0"]

    def wchunk(h):
        return 128 if h < 2 else (256 if h == 2 else 512)

    for h in range(NH):
        sg = SLOPES[h]
        pair, odd = h // 2, h % 2
        w = wchunk(h)
        skip_d = 80.0 / sg
        for isup in range(4):
            i0 = 512 * isup
            jmax = min(NT, (i0 + 512) // 128)
            live = []
            for jt in range(jmax):
                j0 = 128 * jt
                chunks = []
                any_live = False
                for kq in range(512 // w):
                    ick = i0 + w * kq
                    if j0 - ick >= w:
                        st_ = "invalid"
                    elif ick - (j0 + 127) > skip_d:
                        st_ = "skip"
                    elif j0 - ick >= 0:
                        st_ = "partial"
                    else:
                        st_ = "full"
                    chunks.append((kq, ick, st_))
                    if st_ in ("partial", "full"):
                        any_live = True
                if any_live:
                    live.append((jt, chunks))

            def emit_qk_exp(idx, h=h, pair=pair, odd=odd, sg=sg, w=w,
                            isup=isup):
                jt, chunks = live[idx]
                j0 = 128 * jt
                pb = idx % 2
                if odd == 0:
                    ps = cx.sb["pS"] if pb == 0 else cx.sb["pA"]
                    pk = "pS" if pb == 0 else "pA"
                else:
                    ps = cx.sb["pS2"] if pb == 0 else cx.sb["pB"]
                    pk = "pS2" if pb == 0 else "pB"
                pb4 = idx % 4
                ptk = f"ptb:{pb4}"
                pt = ptb[:, 512 * pb4:512 * (pb4 + 1)]
                E("tensor", lambda e, ps=ps, jt=jt, pair=pair, odd=odd,
                  isup=isup: e.matmul(
                      ps[:, :],
                      k_f[pair][64 * odd:64 * (odd + 1),
                                128 * jt:128 * (jt + 1)],
                      q_f[pair][64 * odd:64 * (odd + 1),
                                512 * isup:512 * (isup + 1)],
                      start=True, stop=True),
                  reads=[f"k_f{pair}", f"q_f{pair}"], writes=[pk])
                for kq, ick, st_ in chunks:
                    if st_ in ("invalid", "skip"):
                        E("vector", lambda e, pt=pt, kq=kq, w=w: e.memset(
                            pt[:, w * kq:w * (kq + 1)], 0.0),
                          reads=[], writes=[ptk])
                        continue
                    sl = slot_ctr[0] % 16
                    slot_ctr[0] += 1
                    E("vector", lambda e, j0=j0, ick=ick, sl=sl, sg=sg,
                      w=w: e.tensor_scalar(
                          out=bcol[:, sl:sl + 1],
                          in0=cx.sb["iota_col"][:, 0:1],
                          scalar1=sg, scalar2=sg * (j0 - (ick + w / 2.0)),
                          op0=ALU.mult, op1=ALU.add),
                      reads=["iota_col"], writes=[f"bcol:{sl}"])
                    E("scalar", lambda e, pt=pt, ps=ps, kq=kq, sl=sl, w=w:
                      e.activation(
                          out=pt[:, w * kq:w * (kq + 1)],
                          in_=ps[:, w * kq:w * (kq + 1)], func=AF.Exp,
                          scale=0.125, bias=bcol[:, sl:sl + 1]),
                      reads=[pk, f"bcol:{sl}"], writes=[ptk])
                    if st_ == "partial":
                        dlt = (j0 - ick) // 128
                        E("vector", lambda e, pt=pt, kq=kq, dlt=dlt, w=w:
                          e.tensor_tensor(
                              out=pt[:, w * kq:w * (kq + 1)],
                              in0=pt[:, w * kq:w * (kq + 1)],
                              in1=cx.sb["cmask"][:, 512 * dlt:512 * dlt + w],
                              op=ALU.mult),
                          reads=[ptk, "cmask"], writes=[ptk])

            def emit_pv(idx, h=h):
                jt, _ = live[idx]
                pb4 = idx % 4
                ptk = f"ptb:{pb4}"
                pt = ptb[:, 512 * pb4:512 * (pb4 + 1)]
                E("tensor", lambda e, pt=pt, jt=jt, h=h, fp=(idx == 0):
                  e.matmul(
                    cx.sb["pO"][0:65, :],
                    vt1[:, 1040 * jt + 65 * h:1040 * jt + 65 * (h + 1)],
                    pt, start=fp, stop=False, skip_group_check=True),
                  reads=[ptk, f"vt1:{jt}"], writes=["pO"])

            LAG = 2
            for idx in range(len(live)):
                emit_qk_exp(idx)
                if idx >= LAG:
                    emit_pv(idx - LAG)
            for idx in range(max(0, len(live) - LAG), len(live)):
                emit_pv(idx)
            if odd == 0:
                E("scalar", lambda e, pair=pair, isup=isup: e.activation(
                    out=aof[pair][0:64, 512 * isup:512 * (isup + 1)],
                    in_=cx.sb["pO"][0:64, :], func=AF.Copy),
                  reads=["pO"], writes=[f"aof{pair}:{isup}"])
            else:
                E("scalar", lambda e, isup=isup: e.activation(
                    out=tmp_o[0:64, 512 * isup:512 * (isup + 1)],
                    in_=cx.sb["pO"][0:64, :], func=AF.Copy),
                  reads=["pO"], writes=[f"tmp_o:{isup}"])
            E("vector", lambda e, isup=isup: e.tensor_copy(
                out=zrow[64:65, 512 * isup:512 * (isup + 1)],
                in_=cx.sb["pO"][64:65, :]),
              reads=["pO"], writes=[f"zrow:{isup}"])
        E("sync", lambda e: e.dma_start(out=zrow0[0:1, :], in_=zrow[64:65, :]),
          reads=[f"zrow:{u}" for u in range(4)], writes=["zrow0"], dma=True)
        E("vector", lambda e: e.reciprocal(out=zrow0[0:1, :],
                                           in_=zrow0[0:1, :]),
          reads=["zrow0"], writes=["zrow0"])
        for isup in range(4):
            E("tensor", lambda e, isup=isup: e.matmul(
                cx.sb["pbc"][0:64, :], cx.sb["onesf_row"][0:1, 0:64],
                zrow0[0:1, 512 * isup:512 * (isup + 1)], start=True,
                stop=True),
              reads=["zrow0", "onesf_row"], writes=["pbc"])
            if odd == 0:
                E("vector", lambda e, pair=pair, isup=isup: e.tensor_tensor(
                    out=aof[pair][0:64, 512 * isup:512 * (isup + 1)],
                    in0=aof[pair][0:64, 512 * isup:512 * (isup + 1)],
                    in1=cx.sb["pbc"][0:64, :], op=ALU.mult),
                  reads=[f"aof{pair}:{isup}", "pbc"],
                  writes=[f"aof{pair}:{isup}"])
            else:
                E("vector", lambda e, isup=isup: e.tensor_tensor(
                    out=tmp_o[0:64, 512 * isup:512 * (isup + 1)],
                    in0=tmp_o[0:64, 512 * isup:512 * (isup + 1)],
                    in1=cx.sb["pbc"][0:64, :], op=ALU.mult),
                  reads=[f"tmp_o:{isup}", "pbc"], writes=[f"tmp_o:{isup}"])
        if odd == 1:
            E("sync", lambda e, pair=pair: e.dma_start(
                out=aof[pair][64:128, :], in_=tmp_o[0:64, :]),
              reads=[f"tmp_o:{u}" for u in range(4)],
              writes=[f"aof{pair}:odd"], dma=True)


def emit_l1_oproj_combine(cx, dr):
    p = cx.p
    E = p.emit
    aof = [cx.sb[f"aof{c}"] for c in range(CC)]
    owb = [cx.sb[f"owb{c}"] for c in range(CC)]
    xt2 = cx.sb["xt2"]
    res1k, scr1k = cx.sb["res1k"], cx.sb["scr1k"]
    scrA, scrB = cx.sb["scr512a"], cx.sb["scr512b"]
    for cc in range(CC):
        E("gpsimd", lambda e, cc=cc: e.dma_start(
            out=owb[cc][:, :], in_=dr["ow"][128 * cc:128 * (cc + 1), :]),
          writes=[f"owb{cc}"], dma=True)
    aof_keys = [f"aof{cc}:{u}" for cc in range(CC) for u in range(4)] + \
               [f"aof{cc}:odd" for cc in range(CC)]
    for nt in range(NT):
        b = nt % 2
        E("sync", lambda e, nt=nt, b=b: e.dma_start(
            out=xt2[:, 1024 * b:1024 * (b + 1)],
            in_=dr["x1"][128 * nt:128 * (nt + 1), :]),
          reads=[f"x1d:{nt}"], writes=[f"xt2:{b}"], dma=True)
        E("scalar", lambda e, nt=nt: e.dma_start(
            out=scr1k[:, :], in_=dr["acc1"][128 * nt:128 * (nt + 1), :]),
          reads=[f"acc1d:{nt}"], writes=["scr1k"], dma=True)
        for dh in range(2):
            ps = cx.sb["pA"] if dh == 0 else cx.sb["pB"]
            pk = "pA" if dh == 0 else "pB"
            mm_chain(cx, ps[:, :], pk,
                     [(aof[cc][:, 128 * nt:128 * (nt + 1)],
                       owb[cc][:, 512 * dh:512 * (dh + 1)],
                       [f"owb{cc}"] + aof_keys) for cc in range(CC)])
            E("scalar", lambda e, dh=dh: e.activation(
                out=scrA[:, :], in_=scr1k[:, 512 * dh:512 * (dh + 1)],
                func=AF.Silu),
              reads=["scr1k"], writes=["scr512a"])
            E("vector", lambda e, ps=ps, pk=pk: e.tensor_tensor(
                out=scrB[:, :], in0=ps[:, :], in1=scrA[:, :], op=ALU.mult),
              reads=[pk, "scr512a"], writes=["scr512b"])
            E("vector", lambda e, b=b, dh=dh: e.tensor_tensor(
                out=scrB[:, :], in0=scrB[:, :],
                in1=xt2[:, 1024 * b + 512 * dh:1024 * b + 512 * (dh + 1)],
                op=ALU.add),
              reads=["scr512b", f"xt2:{b}"], writes=["scr512b"])
            E("sync", lambda e, nt=nt, dh=dh: e.dma_start(
                out=dr["y"][128 * nt:128 * (nt + 1), 512 * dh:512 * (dh + 1)],
                in_=scrB[:, :]),
              reads=["scr512b"], writes=[f"yd:{nt}:{dh}"], dma=True)



KSTOP = os.environ.get("KSTOP", "")


def build_program(layers=(0, 1), dbg=(), scratch_out=False):
    nc = bass.Bass("TRN2", target_bir_lowering=False)
    dr = {}

    def din(name, shape):
        dr[name] = nc.dram_tensor(name, list(shape), F32, kind="ExternalInput")

    def dscr(name, shape, out=False):
        dr[name] = nc.dram_tensor(
            name, list(shape), F32,
            kind="ExternalOutput" if (out or scratch_out) else "Internal")

    both = len(layers) == 2
    if 0 in layers:
        din("x", [N, C])
        din("Wq", [C, C]); din("bq", [C]); din("Wk", [C, C]); din("bk", [C])
        din("Wv", [C, C]); din("bv", [C])
        din("gamma", [C]); din("beta", [C])
        din("gate_w", [C, NE])
        din("Wg", [NE, C, NI]); din("Wu", [NE, C, NI]); din("Wd", [NE, NI, C])
        dscr("s0", [N, C])
        dr["xnt"] = nc.dram_tensor("xnt", [N, C], BF16, kind="Internal")
        for i in range(2):
            dr[f"xg{i}"] = nc.dram_tensor(f"xg{i}", [1040, 1040], BF16,
                                          kind="Internal")
        dr["g1d"] = nc.dram_tensor("g1d", [N, C], F32, kind="Internal")
        dscr("x1", [N, C], out=not both)
    if 1 in layers:
        if 0 not in layers:
            din("x1", [N, C])
        din("qw", [C, C]); din("kw", [C, C]); din("vw", [C, C])
        din("ow", [C, C])
        din("tsig", [8, C])
        din("W1", [8, C, C]); din("b1", [8, C])
        dscr("acc1", [N, C])
        dscr("y", [N, C], out=True)

    p = Prog()
    with contextlib.ExitStack() as stack:
        ec = stack.enter_context
        cx = Ctx(nc, p, ec)
        cx.dbg_names = set(dbg)
        cx.psum("ptr", (128, 2048), BF16)
        cx.psum("pbc", (128, 512), F32)
        cx.psum("pA", (128, 512), F32)
        cx.psum("pB", (128, 512), F32)
        cx.sbuf("xt2", [128, 2048], F32)
        cx.sbuf("scr1k", [128, 1024], F32)
        cx.sbuf("res1k", [128, 1024], F32)
        cx.sbuf("scr512a", [128, 512], F32)
        cx.sbuf("scr512b", [128, 512], F32)
        cx.sbuf("scr512c", [128, 512], F32)
        cx.sbuf("scr512d", [128, 512], F32)
        cx.sbuf("st8", [128, 8], F32)
        cx.sbuf("st8b", [128, 8], F32)
        cx.sbuf("mst", [128, 16], F32)
        cx.sbuf("rst", [128, 16], F32)
        cx.sbuf("mrst", [128, 16], F32)
        emit_consts(cx)

        if 0 in layers:
            cx.scope()
            cx.sbuf("aff_t", [128, 128], F32)
            cx.sbuf("aff_pack", [128, 128], F32)
            cx.sbuf("w_t", [128, 128], F32)
            cx.sbuf("th", [8, 4], F32)
            cx.sbuf("s8", [8, 128], F32)
            cx.sbuf("s8t", [128, 8], F32)
            cx.sbuf("thb", [128, 1], F32)
            cx.sbuf("cnt_col", [128, 1], F32)
            cx.sbuf("ind_sb", [128, 128], BF16)
            cx.sbuf("cgt", [8, 1], F32)
            cx.sbuf("iby16", [128, 2], F32)
            cx.sbuf("iby16i", [128, 1], I32)
            cx.sbuf("iot8", [1, 8], F32)
            cx.sbuf("i8b", [128, 8], F32)
            cx.sbuf("hi_row", [1, 8], F32)
            cx.sbuf("thb_t", [128, 8], F32)
            cx.sbuf("kmrow", [1, 1024], F32)
            cx.sbuf("kmean_b", [128, 1024], BF16)
            cx.sbuf("kv_bd", [128, 1024], BF16)
            cx.scope()
            for c in range(CC):
                cx.sbuf(f"l0xnf{c}", [128, 2048], BF16)
            cx.scope()
            cx.sbuf("l0xnb", [128, 3072], BF16)
            for c in range(CC):
                cx.sbuf(f"l0xnlo{c}", [128, 2048], BF16)
            cx.sbuf("aff_T", [8, 2048], F32)
            cx.sbuf("gate32", [128, 64], F32)
            cx.sbuf("ghi", [128, 64], BF16)
            cx.sbuf("glo", [128, 64], BF16)
            if KSTOP != "consts":
                emit_ln(cx, dr["x"], "l0", with_lo=True,
                        x_dram_xnt=dr["xnt"])
            if KSTOP not in ("consts", "ln0"):
                pump_raw = emit_l0_gate_and_bisect(cx, dr)
            else:
                pump_raw = lambda: None
            cx.close_scope()
            cnt = [0]

            def pump():
                if cnt[0] < 40:
                    cnt[0] += 1
                    pump_raw()

            if KSTOP not in ("ln", "ln0", "consts"):
                emit_layer0(cx, dr, pump)
            cx.dbg_dump("w_t", cx.sb["w_t"][:, :], (128, 128))
            cx.dbg_dump("th", cx.sb["th"][0:8, 0:4], (8, 4))
            cx.close_scope()

        if 1 in layers:
            cx.scope()
            for c in range(CC):
                cx.sbuf(f"l1xnf{c}", [128, 2048], BF16)
            cx.sbuf("l1xnb", [128, 3072], BF16)
            cx.sbuf("o_all", [128, 128], F32)
            cx.sbuf("o_T", [8, 2048], BF16)
            cx.scope()
            for c in range(CC):
                cx.sbuf(f"l1xnlo{c}", [128, 2048], BF16)
            cx.sbuf("tsgb", [8, 1024], BF16)
            cx.sbuf("sgn_f", [128, 64], BF16)
            emit_ln(cx, dr["x1"], "l1", with_lo=True)
            emit_l1_routing(cx, dr)
            cx.close_scope()
            cx.scope()
            cx.sbuf("w1b", [128, 16384], BF16)
            cx.sbuf("b1sb", [8, 1024], BF16)
            cx.sbuf("acc1sb", [128, 16384], F32)
            emit_l1_ffn(cx, dr)
            cx.close_scope()
            for c in range(CC):
                cx.sbuf(f"q_f{c}", [128, 2048], BF16, outer=True)
                cx.sbuf(f"k_f{c}", [128, 2048], BF16, outer=True)
            cx.sbuf("vt1", [128, NT * 1040], BF16, outer=True)
            emit_l1_attention(cx, dr)
            cx.close_scope()
            cx.scope()
            cx.psum("pS", (128, 512), F32)
            cx.psum("pS2", (128, 512), F32)
            cx.psum("pO", (128, 512), F32)
            for c in range(CC):
                cx.sbuf(f"aof{c}", [128, 2048], BF16)
            cx.sbuf("tmp_o", [64, 2048], BF16)
            cx.sbuf("ptb", [128, 2048], BF16)
            cx.sbuf("bcol", [128, 16], F32)
            cx.sbuf("zrow", [128, 2048], F32)
            cx.sbuf("zrow0", [1, 2048], F32)
            emit_l1_attention_core(cx, dr)
            for c in range(CC):
                cx.sbuf(f"owb{c}", [128, 1024], BF16)
            emit_l1_oproj_combine(cx, dr)
            cx.close_scope()

        p.final_wait_all("sync")
        sems = {s: ec(nc.semaphore(s)) for s in p.sem_names}
        block = ec(nc.Block())
        p.build_into(block, sems)
    return nc, dr


_NC_CACHE = {}


def _get_program(key, layers, dbg=(), scratch_out=False):
    if key not in _NC_CACHE:
        _NC_CACHE[key] = build_program(layers, dbg, scratch_out)
    return _NC_CACHE[key]


def make_in_map(inputs, b, layers=(0, 1)):
    m = {}
    if 0 in layers:
        m.update({
            "x": np.asarray(inputs["L0_attn_x"])[b],
            "Wq": inputs["L0_attn_Wq"], "bq": inputs["L0_attn_bq"],
            "Wk": inputs["L0_attn_Wk"], "bk": inputs["L0_attn_bk"],
            "Wv": inputs["L0_attn_Wv"], "bv": inputs["L0_attn_bv"],
            "gamma": inputs["L0_attn_gamma"], "beta": inputs["L0_attn_beta"],
            "gate_w": inputs["L0_ffn_gate_w"],
            "Wg": inputs["L0_ffn_Wg"], "Wu": inputs["L0_ffn_Wu"],
            "Wd": inputs["L0_ffn_Wd"],
        })
    if 1 in layers:
        if 0 not in layers:
            m["x1"] = inputs["x1"][b]
        m.update({
            "qw": inputs["L1_attn_q_w"], "kw": inputs["L1_attn_k_w"],
            "vw": inputs["L1_attn_v_w"], "ow": inputs["L1_attn_o_w"],
            "tsig": inputs["L1_ffn_tile_sigs"],
            "W1": inputs["L1_ffn_W"], "b1": inputs["L1_ffn_b"],
        })
    return {k: np.ascontiguousarray(np.asarray(v), dtype=np.float32)
            for k, v in m.items()}


def kernel(**inputs):
    nc, _ = _get_program("fused", (0, 1))
    B = 8
    in_maps = [make_in_map(inputs, b) for b in range(B)]
    res = run_bass_kernel_spmd(nc, in_maps, core_ids=list(range(B)))
    out = np.stack([res.results[b]["y"] for b in range(B)], axis=0)
    return out.astype(np.float32)


# version 15
# speedup vs baseline: 1.0251x; 1.0251x over previous
import contextlib
import math
import os
import numpy as np
import ml_dtypes
import concourse.bass as bass
import concourse.mybir as mybir
from concourse.bass_utils import run_bass_kernel_spmd
from concourse.bass import IndirectOffsetOnAxis

F32 = mybir.dt.float32
BF16 = mybir.dt.bfloat16
I32 = mybir.dt.int32
AF = mybir.ActivationFunctionType
ALU = mybir.AluOpType
AX = mybir.AxisListType

N = 2048
C = 1024
NH = 16
HD = 64
NE = 8
NI = 2048
KSEL = 1024
NT = N // 128
CC = C // 128
TWO_PI = 2.0 * math.pi
TWO_PI_HI = float(np.float32(TWO_PI))
TWO_PI_LO = float(np.float64(TWO_PI) - np.float64(np.float32(TWO_PI)))
INV_2PI = 1.0 / TWO_PI
HALF_PI = math.pi / 2.0
SLOPES = [2.0 ** (-0.5 * (i + 1)) for i in range(NH)]
SCL2 = 1.0 / 2048.0
LN1E4 = math.log(10000.0)

ENGINES = ("tensor", "vector", "scalar", "gpsimd", "sync")
DMA_QUEUES = ("sync", "scalar", "gpsimd")


class Prog:
    DMA_POOL = 8

    def __init__(self):
        self.queues = {e: [] for e in ENGINES}
        self.count = {}
        self.sem_names = []
        for e in ENGINES:
            self._add_sem(e)
        for e in DMA_QUEUES:
            for i in range(self.DMA_POOL):
                self._add_sem(f"{e}_dma{i}")
        self.dma_rr = {e: 0 for e in DMA_QUEUES}
        self.waited = {e: {} for e in ENGINES}
        self.writer = {}
        self.readers = {}

    def _add_sem(self, name):
        self.count[name] = 0
        self.sem_names.append(name)

    def _dep(self, deps, sem, cnt):
        if cnt > deps.get(sem, 0):
            deps[sem] = cnt

    def emit(self, eng, fn, reads=(), writes=(), dma=False):
        if dma:
            i = self.dma_rr[eng]
            self.dma_rr[eng] = (i + 1) % self.DMA_POOL
            sem = f"{eng}_dma{i}"
            inc = 16
        else:
            sem = eng
            inc = 1
        deps = {}
        if dma and self.count[sem] > 0:
            self._dep(deps, sem, self.count[sem])
        for k in reads:
            w = self.writer.get(k)
            if w is not None:
                self._dep(deps, *w)
        for k in writes:
            w = self.writer.get(k)
            if w is not None:
                self._dep(deps, *w)
            for r in self.readers.get(k, ()):
                self._dep(deps, *r)
        waits = []
        wt = self.waited[eng]
        for s, c in deps.items():
            if s == "tensor" and eng == "tensor":
                continue
            if wt.get(s, 0) >= c:
                continue
            waits.append((s, c))
            wt[s] = c
        self.count[sem] += inc
        cnt_after = self.count[sem]
        self.queues[eng].append((waits, fn, sem, inc))
        for k in writes:
            self.writer[k] = (sem, cnt_after)
            self.readers[k] = []
        for k in reads:
            self.readers.setdefault(k, []).append((sem, cnt_after))
        return (sem, cnt_after)

    def barrier_wait(self, eng, sems_counts):
        wt = self.waited[eng]
        waits = []
        for s, c in sems_counts.items():
            if wt.get(s, 0) >= c:
                continue
            waits.append((s, c))
            wt[s] = c
        if waits:
            self.queues[eng].append((waits, None, None, 0))

    def full_barrier(self):
        snap = {s: c for s, c in self.count.items() if c > 0}
        for e in ENGINES:
            self.barrier_wait(e, snap)

    def final_wait_all(self, eng="sync"):
        self.barrier_wait(eng, {s: c for s, c in self.count.items() if c > 0})

    def build_into(self, block, sems):
        def run_queue(engine_obj, items):
            for waits, fn, sem, inc in items:
                for (s, c) in waits:
                    engine_obj.wait_ge(sems[s], c)
                if fn is not None:
                    fn(engine_obj).then_inc(sems[sem], inc)

        @block.sync
        def _(eng):
            run_queue(eng, self.queues["sync"])

        @block.vector
        def _(eng):
            run_queue(eng, self.queues["vector"])

        @block.scalar
        def _(eng):
            run_queue(eng, self.queues["scalar"])

        @block.tensor
        def _(eng):
            run_queue(eng, self.queues["tensor"])

        @block.gpsimd
        def _(eng):
            run_queue(eng, self.queues["gpsimd"])


class Ctx:
    def __init__(self, nc, p, ec):
        self.nc = nc
        self.p = p
        self.ec = ec
        self.ec0 = ec
        self.sb = {}
        self.dbg_names = set()
        self.scopes = []

    def sbuf(self, name, shape, dtype, outer=False):
        ec = self.ec0 if outer else self.ec
        kw = {"side": "right"} if outer else {}
        t = ec(self.nc.sbuf_tensor(name, list(shape), dtype,
                                   allow_name_mangling=True, **kw))
        self.sb[name] = t
        return t

    def psum(self, name, shape=(128, 512), dtype=F32):
        t = self.ec(self.nc.psum_tensor(name, list(shape), dtype,
                                        allow_name_mangling=True))
        self.sb[name] = t
        return t

    def scope(self):
        st = contextlib.ExitStack()
        self.scopes.append((st, self.ec))
        self.ec = st.enter_context
        return st

    def close_scope(self):
        st, outer_ec = self.scopes.pop()
        self.p.full_barrier()
        st.close()
        self.ec = outer_ec

    def dbg_dump(self, name, ap, shape, dtype=F32):
        if name not in self.dbg_names:
            return
        t = self.nc.dram_tensor("dbg_" + name, list(shape), dtype,
                                kind="ExternalOutput")
        self.p.full_barrier()
        self.p.emit("sync", lambda e: e.dma_start(out=t[:, :] if len(shape) == 2 else t[:], in_=ap),
                    reads=[], writes=["dbg_" + name], dma=True)





def transpose_via_ptr(cx, items, identity="identb"):
    p = cx.p
    ptr = cx.sb["ptr"]
    flip = getattr(cx, "_ptr_flip", 0)
    i = 0
    while i < len(items):
        grp = items[i:i + 4]
        bank = flip % 2
        bkey = "ptrA" if bank == 0 else "ptrB"
        for s, (in_ap, in_keys, _, _) in enumerate(grp):
            off = 1024 * bank + 128 * s
            p.emit("tensor", lambda e, in_ap=in_ap, off=off: e.transpose(
                ptr[:, off:off + 128], in_ap, cx.sb[identity][:, :]),
                   reads=list(in_keys) + [identity], writes=[bkey])
        for s, (_, _, evac_fn, ew) in enumerate(grp):
            off = 1024 * bank + 128 * s
            p.emit("scalar", lambda e, evac_fn=evac_fn, off=off:
                   evac_fn(e, ptr[:, off:off + 128]),
                   reads=[bkey], writes=list(ew))
        flip += 1
        i += 4
    cx._ptr_flip = flip


def emit_consts(cx):
    nc, p = cx.nc, cx.p
    E = p.emit
    KC = int(os.environ.get("KCONST", "9"))
    cx.sbuf("identf", [128, 128], F32)
    cx.sbuf("identb", [128, 128], BF16)
    cx.sbuf("onesf_row", [1, 128], F32)
    cx.sbuf("onesb_row", [1, 128], BF16)
    cx.sbuf("onesb_col", [128, 1], BF16)
    cx.sbuf("iota_col", [128, 1], F32)
    cx.sbuf("halfpi", [128, 1], F32)
    cx.sbuf("pos_cols", [128, 16], F32)
    cx.sbuf("cmask", [128, 4 * 512], BF16)
    for nm in ("identf", "identb"):
        t = cx.sb[nm]
        E("gpsimd", lambda e, t=t: e.memset(t[:, :], 0.0), writes=[nm])
        E("gpsimd", lambda e, t=t: e.affine_select(
            out=t[:, :], in_=t[:, :], compare_op=ALU.not_equal, fill=1.0,
            base=0, pattern=[[-1, 128]], channel_multiplier=1),
          reads=[nm], writes=[nm])
    if KC < 2:
        return
    for nm, v in (("onesf_row", 1.0), ("onesb_row", 1.0), ("onesb_col", 1.0),
                  ("halfpi", HALF_PI)):
        t = cx.sb[nm]
        E("gpsimd", lambda e, t=t, v=v: e.memset(t[:, :], v), writes=[nm])
    if KC < 3:
        return
    E("gpsimd", lambda e: e.iota(cx.sb["iota_col"][:, :], pattern=[[0, 1]],
                                 base=0, channel_multiplier=1,
                                 allow_small_or_imprecise_dtypes=True),
      writes=["iota_col"])
    E("gpsimd", lambda e: e.iota(cx.sb["pos_cols"][:, :], pattern=[[128, 16]],
                                 base=0, channel_multiplier=1,
                                 allow_small_or_imprecise_dtypes=True),
      writes=["pos_cols"])
    if KC < 4:
        return
    E("gpsimd", lambda e: e.memset(cx.sb["cmask"][:, :], 1.0), writes=["cmask"])
    for k in range(4):
        E("gpsimd", lambda e, k=k: e.affine_select(
            out=cx.sb["cmask"][:, 512 * k:512 * (k + 1)],
            in_=cx.sb["cmask"][:, 512 * k:512 * (k + 1)],
            compare_op=ALU.is_ge, fill=0.0, base=-(128 * k),
            pattern=[[1, 512]], channel_multiplier=-1),
          reads=["cmask"], writes=["cmask"])


def bcast_row(cx, dst_name, dst_off, src_ap, width, read_keys, write_key=None):
    p = cx.p
    dst = cx.sb[dst_name]
    pb = cx.sb["pbc"]
    wk = write_key or dst_name
    for s in range(0, width, 512):
        w = min(512, width - s)
        p.emit("tensor", lambda e, s=s, w=w: e.matmul(
            pb[:, 0:w], cx.sb["onesf_row"][0:1, :],
            src_ap[0:1, s:s + w], start=True, stop=True),
               reads=list(read_keys) + ["onesf_row"], writes=["pbc"])
        p.emit("vector", lambda e, s=s, w=w: e.tensor_copy(
            out=dst[:, dst_off + s:dst_off + s + w], in_=pb[:, 0:w]),
               reads=["pbc"], writes=[wk])


def emit_ln_to_xnf(cx, x_dram, pfx, eps=1e-5):
    p = cx.p
    E = p.emit
    xt = cx.sb["xt2"]
    xnb = cx.sb["xnb2"]
    scr = cx.sb["scr1k"]
    st = cx.sb["st8"]
    mst, rst, mrst = cx.sb["mst"], cx.sb["rst"], cx.sb["mrst"]
    ptr = cx.sb["ptr"]
    xnf = [cx.sb[f"xnf{c}"] for c in range(CC)]
    mk, rk, mrk = pfx + "mst", pfx + "rst", pfx + "mrst"

    for nt in range(NT):
        b = nt % 2
        st = cx.sb["st8"] if b == 0 else cx.sb["st8b"]
        sp = "st8" if b == 0 else "st8b"
        xk = f"xt2:{b}"
        E("sync", lambda e, nt=nt, b=b: e.dma_start(
            out=xt[:, 1024 * b:1024 * (b + 1)],
            in_=x_dram[128 * nt:128 * (nt + 1), :]),
          writes=[xk], dma=True)
        xv = xt[:, 1024 * b:1024 * (b + 1)]
        E("vector", lambda e, xv=xv, st=st: e.reduce_sum(
            out=st[:, 0:1], in_=xv, axis=AX.X), reads=[xk],
          writes=[sp + ":s"])
        E("scalar", lambda e, xv=xv: e.activation(
            out=scr[:, :], in_=xv, func=AF.Square, accum_out=st[:, 1:2]),
          reads=[xk], writes=["scr1k", "st8:q"])
        E("vector", lambda e, nt=nt: e.tensor_scalar(
            out=mst[:, nt:nt + 1], in0=st[:, 0:1], scalar1=1.0 / 1024.0,
            scalar2=None, op0=ALU.mult),
          reads=["st8:s"], writes=[mk])
        E("scalar", lambda e, nt=nt: e.activation(
            out=st[:, 2:3], in_=mst[:, nt:nt + 1], func=AF.Square),
          reads=[mk], writes=["st8:m2"])
        E("vector", lambda e: e.tensor_scalar(
            out=st[:, 3:4], in0=st[:, 1:2], scalar1=1.0 / 1024.0,
            scalar2=float(eps), op0=ALU.mult, op1=ALU.add),
          reads=["st8:q"], writes=["st8:v"])
        E("vector", lambda e: e.tensor_tensor(
            out=st[:, 3:4], in0=st[:, 3:4], in1=st[:, 2:3], op=ALU.subtract),
          reads=["st8:v", "st8:m2"], writes=["st8:v"])
        E("scalar", lambda e: e.activation(
            out=st[:, 4:5], in_=st[:, 3:4], func=AF.Sqrt),
          reads=["st8:v"], writes=["st8:sd"])
        E("vector", lambda e, nt=nt: e.reciprocal(
            out=rst[:, nt:nt + 1], in_=st[:, 4:5]),
          reads=["st8:sd"], writes=[rk])
        E("vector", lambda e, nt=nt: e.tensor_tensor(
            out=mrst[:, nt:nt + 1], in0=mst[:, nt:nt + 1],
            in1=rst[:, nt:nt + 1], op=ALU.mult),
          reads=[mk, rk], writes=[mrk])
        E("vector", lambda e, xv=xv, nt=nt, b=b: e.tensor_scalar(
            out=xnb[:, 1024 * b:1024 * (b + 1)], in0=xv,
            scalar1=mst[:, nt:nt + 1], scalar2=rst[:, nt:nt + 1],
            op0=ALU.subtract, op1=ALU.mult),
          reads=[xk, mk, rk], writes=[f"xnb2:{b}"])
        for cc in range(CC):
            E("tensor", lambda e, cc=cc, b=b: e.transpose(
                ptr[:, 128 * (cc % 4):128 * (cc % 4 + 1)],
                xnb[:, 1024 * b + 128 * cc:1024 * b + 128 * (cc + 1)],
                cx.sb["identb"][:, :]),
              reads=[f"xnb2:{b}", "identb"], writes=[f"ptr:{cc % 4}"])
            if cc % 4 == 3:
                for c2 in range(cc - 3, cc + 1):
                    E("scalar", lambda e, c2=c2, nt=nt: e.activation(
                        out=xnf[c2][:, 128 * nt:128 * (nt + 1)],
                        in_=ptr[:, 128 * (c2 % 4):128 * (c2 % 4 + 1)],
                        func=AF.Copy),
                      reads=[f"ptr:{c2 % 4}"], writes=[f"xnf{c2}:{nt}"])


def mm_chain(cx, psum_ap, pk, pairs, extra_reads=()):
    p = cx.p
    n = len(pairs)
    for i, (lh, rh, rd) in enumerate(pairs):
        p.emit("tensor", lambda e, lh=lh, rh=rh, i=i: e.matmul(
            psum_ap, lh, rh, start=(i == 0), stop=(i == n - 1)),
               reads=list(rd) + list(extra_reads), writes=[pk])



def emit_ln(cx, x_dram, pfx, with_lo=False, eps=1e-5, x_dram_xnt=None):
    p = cx.p
    E = p.emit
    xt = cx.sb["xt2"]
    xnb = cx.sb[pfx + "xnb"]
    scr = cx.sb["scr1k"]
    res = cx.sb["res1k"]
    mst, rst, mrst = cx.sb["mst"], cx.sb["rst"], cx.sb["mrst"]
    ptr = cx.sb["ptr"]
    xnf = [cx.sb[f"{pfx}xnf{c}"] for c in range(CC)]
    xnlo = [cx.sb[f"{pfx}xnlo{c}"] for c in range(CC)] if with_lo else None
    mk, rk, mrk = pfx + "mst", pfx + "rst", pfx + "mrst"

    for nt in range(NT):
        b = nt % 2
        st = cx.sb["st8"] if b == 0 else cx.sb["st8b"]
        sp = "st8" if b == 0 else "st8b"
        xk = f"xt2:{b}"
        E("sync", lambda e, nt=nt, b=b: e.dma_start(
            out=xt[:, 1024 * b:1024 * (b + 1)],
            in_=x_dram[128 * nt:128 * (nt + 1), :]),
          writes=[xk], dma=True)
        xv = xt[:, 1024 * b:1024 * (b + 1)]
        E("vector", lambda e, xv=xv, st=st: e.reduce_sum(
            out=st[:, 0:1], in_=xv, axis=AX.X), reads=[xk],
          writes=[sp + ":s"])
        E("scalar", lambda e, xv=xv, st=st: e.activation(
            out=scr[:, :], in_=xv, func=AF.Square, accum_out=st[:, 1:2]),
          reads=[xk], writes=["scr1k", sp + ":q"])
        E("vector", lambda e, nt=nt, st=st: e.tensor_scalar(
            out=mst[:, nt:nt + 1], in0=st[:, 0:1], scalar1=1.0 / 1024.0,
            scalar2=None, op0=ALU.mult),
          reads=[sp + ":s"], writes=[mk])
        E("scalar", lambda e, nt=nt, st=st: e.activation(
            out=st[:, 2:3], in_=mst[:, nt:nt + 1], func=AF.Square),
          reads=[mk], writes=[sp + ":m2"])
        E("vector", lambda e, st=st: e.tensor_scalar(
            out=st[:, 3:4], in0=st[:, 1:2], scalar1=1.0 / 1024.0,
            scalar2=float(eps), op0=ALU.mult, op1=ALU.add),
          reads=[sp + ":q"], writes=[sp + ":v"])
        E("vector", lambda e, st=st: e.tensor_tensor(
            out=st[:, 3:4], in0=st[:, 3:4], in1=st[:, 2:3], op=ALU.subtract),
          reads=[sp + ":v", sp + ":m2"], writes=[sp + ":v"])
        E("scalar", lambda e, st=st: e.activation(
            out=st[:, 4:5], in_=st[:, 3:4], func=AF.Sqrt),
          reads=[sp + ":v"], writes=[sp + ":sd"])
        E("vector", lambda e, nt=nt, st=st: e.reciprocal(
            out=rst[:, nt:nt + 1], in_=st[:, 4:5]),
          reads=[sp + ":sd"], writes=[rk])
        E("vector", lambda e, nt=nt: e.tensor_tensor(
            out=mrst[:, nt:nt + 1], in0=mst[:, nt:nt + 1],
            in1=rst[:, nt:nt + 1], op=ALU.mult),
          reads=[mk, rk], writes=[mrk])
        store_xnt = (pfx == "l0")
        if with_lo:
            E("vector", lambda e, xv=xv, nt=nt: e.tensor_scalar(
                out=res[:, :], in0=xv, scalar1=mst[:, nt:nt + 1],
                scalar2=rst[:, nt:nt + 1], op0=ALU.subtract, op1=ALU.mult),
              reads=[xk, mk, rk], writes=["res1k"])
            E("vector", lambda e, b=b: e.tensor_copy(
                out=xnb[:, 1024 * b:1024 * (b + 1)], in_=res[:, :]),
              reads=["res1k"], writes=[f"{pfx}xnb:{b}"])
            E("vector", lambda e, b=b: e.tensor_tensor(
                out=xnb[:, 2048:3072],
                in0=res[:, :], in1=xnb[:, 1024 * b:1024 * (b + 1)],
                op=ALU.subtract),
              reads=["res1k", f"{pfx}xnb:{b}"], writes=[f"{pfx}xnlo:x"])
            if store_xnt:
                E("scalar", lambda e, nt=nt, b=b: e.dma_start(
                    out=x_dram_xnt[128 * nt:128 * (nt + 1), :],
                    in_=xnb[:, 1024 * b:1024 * (b + 1)]),
                  reads=[f"{pfx}xnb:{b}"], writes=[f"xntd:{nt}"], dma=True)
        else:
            E("vector", lambda e, xv=xv, nt=nt, b=b: e.tensor_scalar(
                out=xnb[:, 1024 * b:1024 * (b + 1)], in0=xv,
                scalar1=mst[:, nt:nt + 1], scalar2=rst[:, nt:nt + 1],
                op0=ALU.subtract, op1=ALU.mult),
              reads=[xk, mk, rk], writes=[f"{pfx}xnb:{b}"])
        items = []
        for cc in range(CC):
            def mk_evac(c2, nt):
                def f(e, ps):
                    return e.activation(
                        out=xnf[c2][:, 128 * nt:128 * (nt + 1)], in_=ps,
                        func=AF.Copy)
                return f
            items.append((
                xnb[:, 1024 * b + 128 * cc:1024 * b + 128 * (cc + 1)],
                [f"{pfx}xnb:{b}"], mk_evac(cc, nt), [f"{pfx}xnf{cc}:{nt}"]))
        transpose_via_ptr(cx, items)
        if with_lo:
            items = []
            for cc in range(CC):
                def mk_evac2(c2, nt):
                    def f(e, ps):
                        return e.activation(
                            out=xnlo[c2][:, 128 * nt:128 * (nt + 1)], in_=ps,
                            func=AF.Copy)
                    return f
                items.append((
                    xnb[:, 2048 + 128 * cc:2048 + 128 * (cc + 1)],
                    [f"{pfx}xnlo:x"], mk_evac2(cc, nt),
                    [f"{pfx}xnlo{cc}:{nt}"]))
            transpose_via_ptr(cx, items)



def emit_l0_gate_and_bisect(cx, dr):
    p = cx.p
    E = p.emit
    xnf = [cx.sb[f"l0xnf{c}"] for c in range(CC)]
    xnlo = [cx.sb[f"l0xnlo{c}"] for c in range(CC)]
    gate32 = cx.sb["gate32"]
    ghi, glo = cx.sb["ghi"], cx.sb["glo"]
    aff_t = cx.sb["aff_t"]
    aff_T = cx.sb["aff_T"]
    aff_pack = cx.sb["aff_pack"]
    pA, pB = cx.sb["pA"], cx.sb["pB"]
    st = cx.sb["st8"]
    scr = cx.sb["scr1k"]
    E("sync", lambda e: e.dma_start(
        out=gate32[:, :].rearrange("p (c e) -> p c e", e=8),
        in_=dr["gate_w"][:, :].rearrange("(c p) e -> p c e", p=128)),
      writes=["gate32"], dma=True)
    E("vector", lambda e: e.tensor_copy(out=ghi[:, :], in_=gate32[:, :]),
      reads=["gate32"], writes=["ghi"])
    E("vector", lambda e: e.tensor_tensor(
        out=glo[:, :], in0=gate32[:, :], in1=ghi[:, :], op=ALU.subtract),
      reads=["gate32", "ghi"], writes=["glo"])
    for nt in range(NT):
        pairs = []
        for cc in range(CC):
            pairs.append((xnf[cc][:, 128 * nt:128 * (nt + 1)],
                          ghi[:, 8 * cc:8 * (cc + 1)],
                          [f"l0xnf{cc}:{nt}", "ghi"]))
        for cc in range(CC):
            pairs.append((xnf[cc][:, 128 * nt:128 * (nt + 1)],
                          glo[:, 8 * cc:8 * (cc + 1)],
                          [f"l0xnf{cc}:{nt}", "glo"]))
        for cc in range(CC):
            pairs.append((xnlo[cc][:, 128 * nt:128 * (nt + 1)],
                          ghi[:, 8 * cc:8 * (cc + 1)],
                          [f"l0xnlo{cc}:{nt}", "ghi"]))
        mm_chain(cx, pA[:, 0:8], "pA", pairs)
        E("vector", lambda e: e.reduce_max(
            out=st[:, 0:1], in_=pA[:, 0:8], axis=AX.X, negate=True),
          reads=["pA"], writes=["st8:nm"])
        E("scalar", lambda e: e.activation(
            out=scr[:, 0:8], in_=pA[:, 0:8], func=AF.Exp,
            bias=st[:, 0:1], accum_out=st[:, 1:2]),
          reads=["pA", "st8:nm"], writes=["scr1k", "st8:es"])
        E("vector", lambda e: e.reciprocal(out=st[:, 2:3], in_=st[:, 1:2]),
          reads=["st8:es"], writes=["st8:rs"])
        E("vector", lambda e, nt=nt: e.tensor_scalar(
            out=aff_t[:, 8 * nt:8 * (nt + 1)], in0=scr[:, 0:8],
            scalar1=st[:, 2:3], scalar2=None, op0=ALU.mult),
          reads=["scr1k", "st8:rs"], writes=[f"aff_t:{nt}"])
        E("tensor", lambda e, nt=nt: e.transpose(
            pB[0:8, 0:128], aff_t[:, 8 * nt:8 * (nt + 1)],
            cx.sb["identf"][:, :]),
          reads=[f"aff_t:{nt}", "identf"], writes=["pB"])
        E("vector", lambda e, nt=nt: e.tensor_copy(
            out=aff_T[0:8, 128 * nt:128 * (nt + 1)], in_=pB[0:8, 0:128]),
          reads=["pB"], writes=[f"aff_T:{nt}"])
    E("sync", lambda e: e.dma_start(
        out=aff_pack[:, :],
        in_=aff_T[:, :].rearrange("e (t n) -> e t n", n=128)),
      reads=[f"aff_T:{nt}" for nt in range(NT)], writes=["aff_pack"], dma=True)

    th = cx.sb["th"]
    s8, s8t = cx.sb["s8"], cx.sb["s8t"]
    thb, cnt_col = cx.sb["thb"], cx.sb["cnt_col"]
    ind_sb, cgt = cx.sb["ind_sb"], cx.sb["cgt"]
    iby = cx.sb["iby16"]
    ibyi = cx.sb["iby16i"]
    iot8, i8b = cx.sb["iot8"], cx.sb["i8b"]
    E("gpsimd", lambda e: e.memset(s8t[:, :], 1.0), writes=["s8t"])
    E("gpsimd", lambda e: e.affine_select(
        out=s8t[:, :], in_=s8t[:, :], compare_op=ALU.is_ge, fill=0.0,
        base=0, pattern=[[-16, 8]], channel_multiplier=1),
      reads=["s8t"], writes=["s8t"])
    E("gpsimd", lambda e: e.affine_select(
        out=s8t[:, :], in_=s8t[:, :], compare_op=ALU.is_ge, fill=0.0,
        base=15, pattern=[[16, 8]], channel_multiplier=-1),
      reads=["s8t"], writes=["s8t"])
    E("tensor", lambda e: e.transpose(
        cx.sb["pbc"][0:8, 0:128], s8t[:, :], cx.sb["identf"][:, :]),
      reads=["s8t", "identf"], writes=["pbc"])
    E("vector", lambda e: e.tensor_copy(out=s8[0:8, :],
                                        in_=cx.sb["pbc"][0:8, 0:128]),
      reads=["pbc"], writes=["s8"])
    E("gpsimd", lambda e: e.memset(th[0:8, 0:1], 0.0), writes=["th:lo"])
    E("gpsimd", lambda e: e.memset(th[0:8, 1:2], 1.0), writes=["th:hi"])
    E("gpsimd", lambda e: e.memset(th[0:8, 2:3], 0.5), writes=["th:mid"])

    def bisect_iter():
        pbc = cx.sb["pbc"]
        E("tensor", lambda e: e.matmul(
            pbc[0:128, 0:1], s8[0:8, :], th[0:8, 2:3], start=True, stop=True),
          reads=["s8", "th:mid"], writes=["pbc"])
        E("vector", lambda e: e.tensor_copy(out=thb[:, 0:1],
                                            in_=pbc[0:128, 0:1]),
          reads=["pbc"], writes=["thb"])
        E("vector", lambda e: e.tensor_scalar(
            out=ind_sb[:, :], in0=cx.sb["aff_pack"][:, :],
            scalar1=thb[:, 0:1], scalar2=0.0, op0=ALU.is_ge, op1=ALU.add,
            accum_out=cnt_col[:, 0:1]),
          reads=["aff_pack", "thb"], writes=["ind_sb", "cnt_col"])
        E("tensor", lambda e: e.matmul(
            pbc[0:8, 0:1], s8t[:, :], cnt_col[:, 0:1], start=True, stop=True),
          reads=["s8t", "cnt_col"], writes=["pbc"])
        E("vector", lambda e: e.tensor_scalar(
            out=cgt[0:8, 0:1], in0=pbc[0:8, 0:1], scalar1=float(KSEL),
            scalar2=None, op0=ALU.is_gt),
          reads=["pbc"], writes=["cgt"])
        E("vector", lambda e: e.tensor_tensor(
            out=th[0:8, 3:4], in0=th[0:8, 2:3], in1=th[0:8, 0:1],
            op=ALU.subtract),
          reads=["th:mid", "th:lo"], writes=["th:s"])
        E("vector", lambda e: e.tensor_tensor(
            out=th[0:8, 3:4], in0=th[0:8, 3:4], in1=cgt[0:8, 0:1],
            op=ALU.mult),
          reads=["th:s", "cgt"], writes=["th:s"])
        E("vector", lambda e: e.tensor_tensor(
            out=th[0:8, 0:1], in0=th[0:8, 0:1], in1=th[0:8, 3:4], op=ALU.add),
          reads=["th:lo", "th:s"], writes=["th:lo"])
        E("vector", lambda e: e.tensor_tensor(
            out=th[0:8, 3:4], in0=th[0:8, 2:3], in1=th[0:8, 1:2],
            op=ALU.subtract),
          reads=["th:mid", "th:hi"], writes=["th:s"])
        E("vector", lambda e: e.tensor_tensor(
            out=th[0:8, 3:4], in0=th[0:8, 3:4], in1=cgt[0:8, 0:1],
            op=ALU.mult),
          reads=["th:s", "cgt"], writes=["th:s"])
        E("vector", lambda e: e.tensor_tensor(
            out=th[0:8, 1:2], in0=th[0:8, 2:3], in1=th[0:8, 3:4],
            op=ALU.subtract),
          reads=["th:mid", "th:s"], writes=["th:hi"])
        E("vector", lambda e: e.tensor_tensor(
            out=th[0:8, 3:4], in0=th[0:8, 0:1], in1=th[0:8, 1:2], op=ALU.add),
          reads=["th:lo", "th:hi"], writes=["th:s"])
        E("vector", lambda e: e.tensor_scalar(
            out=th[0:8, 2:3], in0=th[0:8, 3:4], scalar1=0.5, scalar2=None,
            op0=ALU.mult),
          reads=["th:s"], writes=["th:mid"])

    return bisect_iter


def emit_l0_gating_final(cx):
    p = cx.p
    E = p.emit
    th = cx.sb["th"]
    hi_row = cx.sb["hi_row"]
    thb_t = cx.sb["thb_t"]
    w_t = cx.sb["w_t"]
    aff_t = cx.sb["aff_t"]
    scr = cx.sb["scr1k"]
    E("tensor", lambda e: e.transpose(
        cx.sb["pbc"][0:1, 0:8], th[0:8, 1:2], cx.sb["identf"][0:8, 0:8]),
      reads=["th:hi", "identf"], writes=["pbc"])
    E("vector", lambda e: e.tensor_copy(out=hi_row[0:1, 0:8],
                                        in_=cx.sb["pbc"][0:1, 0:8]),
      reads=["pbc"], writes=["hi_row"])
    bcast_row(cx, "thb_t", 0, hi_row[0:1, 0:8], 8, ["hi_row"])
    for nt in range(NT):
        E("vector", lambda e, nt=nt: e.tensor_tensor(
            out=scr[:, 0:8], in0=aff_t[:, 8 * nt:8 * (nt + 1)],
            in1=thb_t[:, 0:8], op=ALU.is_ge),
          reads=[f"aff_t:{nt}", "thb_t"], writes=["scr1k"])
        E("vector", lambda e, nt=nt: e.tensor_tensor(
            out=w_t[:, 8 * nt:8 * (nt + 1)], in0=scr[:, 0:8],
            in1=aff_t[:, 8 * nt:8 * (nt + 1)], op=ALU.mult),
          reads=["scr1k", f"aff_t:{nt}"], writes=[f"w_t:{nt}"])


def emit_rope_tables(cx, nt, ss=0):
    p = cx.p
    E = p.emit
    sx = "" if ss == 0 else "2"
    ivb = cx.sb["ivf_b"]
    tA, tB, kf = (cx.sb["scr512a" + sx], cx.sb["scr512b" + sx],
                  cx.sb["scr512c" + sx])
    ki = cx.sb["kint" + sx]
    pos = cx.sb["pos_cols"]
    E("vector", lambda e, nt=nt: e.tensor_scalar(
        out=tA[:, :], in0=ivb[:, :], scalar1=pos[:, nt:nt + 1], scalar2=None,
        op0=ALU.mult),
      reads=["ivf_b", "pos_cols"], writes=["scr512a" + sx])
    for ph, extra in ((0, None), (1, 0.25)):
        E("vector", lambda e, extra=extra: e.tensor_scalar(
            out=ki[:, :], in0=tA[:, :], scalar1=INV_2PI, scalar2=extra,
            op0=ALU.mult, op1=(ALU.add if extra is not None else ALU.bypass)),
          reads=["scr512a" + sx], writes=["kint" + sx])
        E("vector", lambda e: e.tensor_copy(out=kf[:, :], in_=ki[:, :]),
          reads=["kint" + sx], writes=["scr512c" + sx])
        E("vector", lambda e: e.scalar_tensor_tensor(
            out=tB[:, :], in0=kf[:, :], scalar=-TWO_PI_HI, in1=tA[:, :],
            op0=ALU.mult, op1=ALU.add),
          reads=["scr512c" + sx, "scr512a" + sx], writes=["scr512b" + sx])
        E("vector", lambda e: e.scalar_tensor_tensor(
            out=tB[:, :], in0=kf[:, :], scalar=-TWO_PI_LO, in1=tB[:, :],
            op0=ALU.mult, op1=ALU.add),
          reads=["scr512c" + sx, "scr512b" + sx], writes=["scr512b" + sx])
        thr = float(np.float32(math.pi if ph == 0 else math.pi / 2))
        E("vector", lambda e, thr=thr: e.tensor_scalar(
            out=kf[:, :], in0=tB[:, :], scalar1=thr,
            scalar2=None, op0=ALU.is_gt),
          reads=["scr512b" + sx], writes=["scr512c" + sx])
        E("vector", lambda e: e.scalar_tensor_tensor(
            out=tB[:, :], in0=kf[:, :], scalar=-TWO_PI_HI, in1=tB[:, :],
            op0=ALU.mult, op1=ALU.add),
          reads=["scr512c" + sx, "scr512b" + sx], writes=["scr512b" + sx])
        if ph == 0:
            E("scalar", lambda e, nt=nt: e.activation(
                out=cx.sb["sin_all"][:, 512 * nt:512 * (nt + 1)],
                in_=tB[:, :], func=AF.Sin),
              reads=["scr512b" + sx], writes=[f"sin_all:{nt}"])
        else:
            E("scalar", lambda e, nt=nt: e.activation(
                out=cx.sb["cos_all"][:, 512 * nt:512 * (nt + 1)],
                in_=tB[:, :], func=AF.Sin, bias=cx.sb["halfpi"][:, 0:1]),
              reads=["scr512b" + sx, "halfpi"], writes=[f"cos_all:{nt}"])


def emit_rope_apply(cx, x_ap, xkey, nt, ss=0):
    p = cx.p
    E = p.emit
    sx = "" if ss == 0 else "2"
    cos = cx.sb["cos_all"][:, 512 * nt:512 * (nt + 1)]
    sin = cx.sb["sin_all"][:, 512 * nt:512 * (nt + 1)]
    ck, sk = f"cos_all:{nt}", f"sin_all:{nt}"
    xe = x_ap.rearrange("p (i two) -> p i two", two=2)[:, :, 0]
    xo = x_ap.rearrange("p (i two) -> p i two", two=2)[:, :, 1]
    a, b, c, d = (cx.sb["scr512a" + sx], cx.sb["scr512b" + sx],
                  cx.sb["scr512c" + sx], cx.sb["scr512d" + sx])
    E("vector", lambda e: e.tensor_tensor(out=a[:, :], in0=xe, in1=cos,
                                          op=ALU.mult),
      reads=[xkey, ck], writes=["scr512a" + sx])
    E("vector", lambda e: e.tensor_tensor(out=b[:, :], in0=xo, in1=sin,
                                          op=ALU.mult),
      reads=[xkey, sk], writes=["scr512b" + sx])
    E("gpsimd", lambda e: e.tensor_tensor(out=c[:, :], in0=xo, in1=cos,
                                          op=ALU.mult),
      reads=[xkey, ck], writes=["scr512c" + sx])
    E("gpsimd", lambda e: e.tensor_tensor(out=d[:, :], in0=xe, in1=sin,
                                          op=ALU.mult),
      reads=[xkey, sk], writes=["scr512d" + sx])
    E("vector", lambda e: e.tensor_tensor(out=xe, in0=a[:, :], in1=b[:, :],
                                          op=ALU.subtract),
      reads=["scr512a" + sx, "scr512b" + sx], writes=[xkey])
    E("gpsimd", lambda e: e.tensor_tensor(out=xo, in0=c[:, :], in1=d[:, :],
                                          op=ALU.add),
      reads=["scr512c" + sx, "scr512d" + sx], writes=[xkey])


def load_wT(cx, w_dram, dst, dstname):
    p = cx.p
    E = p.emit
    wtmp = cx.sb["wtmp"]
    for jc in range(CC):
        b = jc % 2
        wk = f"wtmp:{b}"
        E("gpsimd", lambda e, jc=jc, b=b, w_dram=w_dram: e.dma_start(
            out=wtmp[:, 1024 * b:1024 * (b + 1)],
            in_=w_dram[128 * jc:128 * (jc + 1), :]),
          writes=[wk], dma=True)
        items = []
        for cc in range(CC):
            def mk_evac(c2, jc):
                def f(e, ps):
                    return e.activation(
                        out=dst[c2][:, 128 * jc:128 * (jc + 1)], in_=ps,
                        func=AF.Copy)
                return f
            items.append((
                wtmp[:, 1024 * b + 128 * cc:1024 * b + 128 * (cc + 1)],
                [wk], mk_evac(cc, jc), [f"{dstname}{cc}"]))
        transpose_via_ptr(cx, items)


def emit_layer0(cx, dr, pump):
    p = cx.p
    E = p.emit
    xnf = [cx.sb[f"l0xnf{c}"] for c in range(CC)]
    cx.scope()
    qt = cx.sbuf("qt", [128, NT * 1024], BF16)
    cx.sbuf("cos_all", [128, NT * 512], BF16)
    cx.sbuf("sin_all", [128, NT * 512], BF16)
    cx.sbuf("scr512a2", [128, 512], F32)
    cx.sbuf("scr512b2", [128, 512], F32)
    cx.sbuf("scr512c2", [128, 512], F32)
    cx.sbuf("scr512d2", [128, 512], F32)

    cx.scope()
    wtq = [cx.sbuf(f"wtq{c}", [128, 1024], BF16) for c in range(CC)]
    cx.sbuf("wtmp", [128, 2048], BF16)
    brow1 = cx.sbuf("brow1", [1, 1024], F32)
    bqb = cx.sbuf("bqb", [128, 1024], BF16)
    load_wT(cx, dr["Wq"], wtq, "wtq")
    E("sync", lambda e: e.dma_start(out=brow1[0:1, :], in_=dr["bq"][:]),
      writes=["brow1"], dma=True)
    bcast_row(cx, "bqb", 0, brow1[0:1, :], 1024, ["brow1"])
    scrA, scrB = cx.sb["scr512a"], cx.sb["scr512b"]
    for nt in range(NT):
        pump(); pump()
        for jh in range(2):
            ps = cx.sb["pA"] if jh == 0 else cx.sb["pB"]
            pk = "pA" if jh == 0 else "pB"
            mm_chain(cx, ps[:, :], pk,
                     [(xnf[cc][:, 128 * nt:128 * (nt + 1)],
                       wtq[cc][:, 512 * jh:512 * (jh + 1)],
                       [f"l0xnf{cc}:{nt}", f"wtq{cc}"]) for cc in range(CC)])
            E("vector", lambda e, ps=ps, pk=pk, jh=jh: e.tensor_tensor(
                out=scrA[:, :], in0=ps[:, :],
                in1=bqb[:, 512 * jh:512 * (jh + 1)], op=ALU.add),
              reads=[pk, "bqb"], writes=["scr512a"])
            E("scalar", lambda e: e.activation(
                out=scrB[:, :], in_=scrA[:, :], func=AF.Gelu),
              reads=["scr512a"], writes=["scr512b"])
            E("vector", lambda e, nt=nt, jh=jh: e.tensor_scalar(
                out=qt[:, 1024 * nt + 512 * jh:1024 * nt + 512 * (jh + 1)],
                in0=scrB[:, :], scalar1=0.21, scalar2=None, op0=ALU.add),
              reads=["scr512b"], writes=[f"qt:{nt}"])
    cx.close_scope()

    if KSTOP == "b1a":
        cx.close_scope()
        return
    cx.scope()
    cx.psum("pD", (1, 1024), F32)
    cx.psum("pkv", (128, 512), F32)
    wtk = [cx.sbuf(f"wtk{c}", [128, 1024], BF16) for c in range(CC)]
    wtv = [cx.sbuf(f"wtv{c}", [128, 1024], BF16) for c in range(CC)]
    cx.sbuf("wtmp", [128, 2048], BF16)
    brow2 = cx.sbuf("brow2", [1, 1024], F32)
    bkb = cx.sbuf("bkb", [128, 1024], BF16)
    bvb2 = cx.sbuf("bvb2", [128, 1024], F32)
    ktb2 = cx.sbuf("ktb", [128, 2048], BF16)
    vtb2 = cx.sbuf("vtb", [128, 2048], BF16)
    cx.sbuf("kint", [128, 512], I32)
    cx.sbuf("kint2", [128, 512], I32)
    ivr = cx.sbuf("ivf_row", [1, 512], F32)
    cx.sbuf("ivf_b", [128, 512], F32)
    load_wT(cx, dr["Wk"], wtk, "wtk")
    load_wT(cx, dr["Wv"], wtv, "wtv")
    E("sync", lambda e: e.dma_start(out=brow2[0:1, :], in_=dr["bk"][:]),
      writes=["brow2:k"], dma=True)
    bcast_row(cx, "bkb", 0, brow2[0:1, :], 1024, ["brow2:k"])
    E("sync", lambda e: e.dma_start(out=brow2[0:1, :], in_=dr["bv"][:]),
      reads=["bkb"], writes=["brow2:v"], dma=True)
    bcast_row(cx, "bvb2", 0, brow2[0:1, :], 1024, ["brow2:v"])
    E("vector", lambda e: e.tensor_scalar(
        out=cx.sb["bvb2"][:, :], in0=cx.sb["bvb2"][:, :], scalar1=SCL2,
        scalar2=None, op0=ALU.mult),
      reads=["bvb2"], writes=["bvb2"])
    E("gpsimd", lambda e: e.iota(ivr[:, :], pattern=[[1, 512]], base=0,
                                 channel_multiplier=0,
                                 allow_small_or_imprecise_dtypes=True),
      writes=["ivf_row"])
    E("scalar", lambda e: e.activation(
        out=ivr[:, :], in_=ivr[:, :], func=AF.Exp, scale=-LN1E4 / 512.0),
      reads=["ivf_row"], writes=["ivf_row"])
    bcast_row(cx, "ivf_b", 0, ivr[0:1, :], 512, ["ivf_row"])

    for nt in range(NT):
        kb = nt % 2
        ktb = ktb2[:, 1024 * kb:1024 * (kb + 1)]
        vtb = vtb2[:, 1024 * kb:1024 * (kb + 1)]
        kkey, vkey = f"ktb:{kb}", f"vtb:{kb}"
        pump(); pump()
        emit_rope_tables(cx, nt, ss=kb)
        for pi, (wt, wname) in enumerate(((wtk, "wtk"), (wtv, "wtv"))):
            for jh in range(2):
                ps = cx.sb["pA"] if jh == 0 else cx.sb["pB"]
                pk = "pA" if jh == 0 else "pB"
                mm_chain(cx, ps[:, :], pk,
                         [(xnf[cc][:, 128 * nt:128 * (nt + 1)],
                           wt[cc][:, 512 * jh:512 * (jh + 1)],
                           [f"l0xnf{cc}:{nt}", f"{wname}{cc}"])
                          for cc in range(CC)])
                if pi == 0:
                    E("vector", lambda e, ps=ps, pk=pk, jh=jh: e.tensor_tensor(
                        out=scrA[:, :], in0=ps[:, :],
                        in1=bkb[:, 512 * jh:512 * (jh + 1)], op=ALU.add),
                      reads=[pk, "bkb"], writes=["scr512a"])
                    E("scalar", lambda e: e.activation(
                        out=scrB[:, :], in_=scrA[:, :], func=AF.Gelu),
                      reads=["scr512a"], writes=["scr512b"])
                    E("vector", lambda e, jh=jh, ktb=ktb: e.tensor_scalar(
                        out=ktb[:, 512 * jh:512 * (jh + 1)],
                        in0=scrB[:, :], scalar1=0.21, scalar2=None,
                        op0=ALU.add),
                      reads=["scr512b"], writes=[kkey])
                else:
                    E("vector", lambda e, ps=ps, pk=pk, jh=jh, vtb=vtb:
                      e.scalar_tensor_tensor(
                          out=vtb[:, 512 * jh:512 * (jh + 1)], in0=ps[:, :],
                          scalar=SCL2, in1=bvb2[:, 512 * jh:512 * (jh + 1)],
                          op0=ALU.mult, op1=ALU.add),
                      reads=[pk, "bvb2"], writes=[vkey])
        for half in range(2):
            E("tensor", lambda e, nt=nt, half=half, ktb=ktb: e.matmul(
                cx.sb["pD"][0:1, 512 * half:512 * (half + 1)],
                cx.sb["onesb_col"][:, 0:1],
                ktb[:, 512 * half:512 * (half + 1)],
                start=(nt == 0), stop=(nt == NT - 1), skip_group_check=True),
              reads=["onesb_col", kkey], writes=["pD"])
        emit_rope_apply(cx, ktb[:, :], kkey, nt, ss=kb)
        for h in range(NH):
            pair, odd = h // 2, h % 2
            E("tensor", lambda e, nt=nt, h=h, pair=pair, odd=odd, ktb=ktb,
              vtb=vtb: e.matmul(
                cx.sb["pkv"][64 * odd:64 * (odd + 1),
                             64 * pair:64 * (pair + 1)],
                ktb[:, 64 * h:64 * (h + 1)], vtb[:, 64 * h:64 * (h + 1)],
                start=(nt == 0 and h <= 1),
                stop=(nt == NT - 1 and h >= NH - 2),
                skip_group_check=True, tile_position=(0, 64 * odd)),
              reads=[kkey, vkey], writes=["pkv"])
    kmrow = cx.sb["kmrow"]
    E("vector", lambda e: e.tensor_scalar(
        out=kmrow[0:1, :], in0=cx.sb["pD"][0:1, 0:1024], scalar1=1.0 / 2048.0,
        scalar2=None, op0=ALU.mult),
      reads=["pD"], writes=["kmrow"])
    bcast_row(cx, "kmean_b", 0, kmrow[0:1, 0:1024], 1024, ["kmrow"])
    kvbd = cx.sb["kv_bd"]
    E("gpsimd", lambda e: e.memset(kvbd[:, :], 0.0), writes=["kv_bd"])
    E("scalar", lambda e: e.activation(
        out=kvbd[0:64, :].rearrange("p (c x) -> p c x", x=128)[:, :, 0:64],
        in_=cx.sb["pkv"][0:64, :].rearrange("p (c x) -> p c x", x=64),
        func=AF.Copy),
      reads=["pkv", "kv_bd"], writes=["kv_bd"])
    E("scalar", lambda e: e.activation(
        out=kvbd[64:128, :].rearrange("p (c x) -> p c x", x=128)[:, :, 64:128],
        in_=cx.sb["pkv"][64:128, :].rearrange("p (c x) -> p c x", x=64),
        func=AF.Copy),
      reads=["pkv", "kv_bd"], writes=["kv_bd"])
    cx.close_scope()

    if KSTOP == "b1b":
        cx.close_scope()
        return
    cx.scope()
    cx.psum("pout", (128, 1024), F32)
    qrT = cx.sbuf("qrT", [128, 2048], BF16)
    z_all = cx.sbuf("z_all", [128, 256], F32)
    gbrow = cx.sbuf("gbrow", [1, 1024], F32)
    gammab = cx.sbuf("gammab", [128, 1024], F32)
    betab = cx.sbuf("betab", [128, 1024], F32)
    E("sync", lambda e: e.dma_start(out=gbrow[0:1, :], in_=dr["gamma"][:]),
      writes=["gbrow:g"], dma=True)
    bcast_row(cx, "gammab", 0, gbrow[0:1, :], 1024, ["gbrow:g"])
    E("sync", lambda e: e.dma_start(out=gbrow[0:1, :], in_=dr["beta"][:]),
      reads=["gammab"], writes=["gbrow:b"], dma=True)
    bcast_row(cx, "betab", 0, gbrow[0:1, :], 1024, ["gbrow:b"])
    kv_bd = cx.sb["kv_bd"]
    kmean_b = cx.sb["kmean_b"]
    xt2 = cx.sb["xt2"]
    st = cx.sb["st8"]
    resA, scrA1 = cx.sb["res1k"], cx.sb["scr1k"]
    resB = cx.sbuf("res1kB", [128, 1024], F32)
    scrB1 = cx.sbuf("scr1kB", [128, 1024], F32)
    zscr = cx.sbuf("zscr", [128, 1024], F32)
    zTs = cx.sbuf("zTs", [16, 128], F32)
    zbS = cx.sbuf("zbS", [128, 1024], F32)
    cx.sbuf("i16f", [16, 1024], F32)
    E("gpsimd", lambda e: e.memset(cx.sb["i16f"][:, :], 1.0), writes=["i16f"])
    E("gpsimd", lambda e: e.affine_select(
        out=cx.sb["i16f"][:, :], in_=cx.sb["i16f"][:, :],
        compare_op=ALU.is_ge, fill=0.0,
        base=0, pattern=[[1, 1024]], channel_multiplier=-64),
      reads=["i16f"], writes=["i16f"])
    E("gpsimd", lambda e: e.affine_select(
        out=cx.sb["i16f"][:, :], in_=cx.sb["i16f"][:, :],
        compare_op=ALU.is_ge, fill=0.0,
        base=63, pattern=[[-1, 1024]], channel_multiplier=64),
      reads=["i16f"], writes=["i16f"])
    def b2_stageA(nt):
        b = nt % 2
        scr = scrA1 if b == 0 else scrB1
        sk = "scr1k" if b == 0 else "scr1kB"
        xk = f"xt2:{b}"
        E("sync", lambda e, nt=nt, b=b: e.dma_start(
            out=xt2[:, 1024 * b:1024 * (b + 1)],
            in_=dr["x"][128 * nt:128 * (nt + 1), :]),
          writes=[xk], dma=True)
        E("gpsimd", lambda e, nt=nt: e.tensor_tensor(
            out=zscr[:, :], in0=qt[:, 1024 * nt:1024 * (nt + 1)],
            in1=kmean_b[:, :], op=ALU.mult),
          reads=[f"qt:{nt}", "kmean_b"], writes=["zscr"])
        E("vector", lambda e, nt=nt: e.reduce_sum(
            out=z_all[:, 16 * nt:16 * (nt + 1)],
            in_=zscr[:, :].rearrange("p (h d) -> p h d", d=64), axis=AX.X),
          reads=["zscr"], writes=[f"z_all:{nt}"])
        E("vector", lambda e, nt=nt: e.tensor_scalar(
            out=z_all[:, 16 * nt:16 * (nt + 1)],
            in0=z_all[:, 16 * nt:16 * (nt + 1)], scalar1=1e-6, scalar2=None,
            op0=ALU.add),
          reads=[f"z_all:{nt}"], writes=[f"z_all:{nt}"])
        E("vector", lambda e, nt=nt: e.reciprocal(
            out=z_all[:, 16 * nt:16 * (nt + 1)],
            in_=z_all[:, 16 * nt:16 * (nt + 1)]),
          reads=[f"z_all:{nt}"], writes=[f"z_all:{nt}"])
        emit_rope_apply(cx, qt[:, 1024 * nt:1024 * (nt + 1)], f"qt:{nt}", nt,
                        ss=b)
        items = []
        for cc in range(CC):
            def mk_evac(c2, b2_=b):
                def f(e, ps):
                    return e.activation(
                        out=qrT[:, 1024 * b2_ + 128 * c2:
                                1024 * b2_ + 128 * (c2 + 1)], in_=ps,
                        func=AF.Copy)
                return f
            items.append((
                qt[:, 1024 * nt + 128 * cc:1024 * nt + 128 * (cc + 1)],
                [f"qt:{nt}"], mk_evac(cc), [f"qrT:{b}"]))
        transpose_via_ptr(cx, items)

    def b2_stageB(nt):
        b = nt % 2
        res = resA if b == 0 else resB
        scr = scrA1 if b == 0 else scrB1
        rk = "res1k" if b == 0 else "res1kB"
        sk = "scr1k" if b == 0 else "scr1kB"
        xk = f"xt2:{b}"
        for pr in range(8):
            E("tensor", lambda e, pr=pr, b=b: e.matmul(
                cx.sb["pout"][:, 128 * pr:128 * (pr + 1)],
                qrT[:, 1024 * b + 128 * pr:1024 * b + 128 * (pr + 1)],
                kv_bd[:, 128 * pr:128 * (pr + 1)],
                start=True, stop=True, skip_group_check=True),
              reads=[f"qrT:{b}", "kv_bd"], writes=["pout"])
        E("tensor", lambda e, nt=nt: e.transpose(
            cx.sb["pbc"][0:16, 0:128], z_all[:, 16 * nt:16 * (nt + 1)],
            cx.sb["identf"][:, :]),
          reads=[f"z_all:{nt}", "identf"], writes=["pbc"])
        E("vector", lambda e: e.tensor_copy(out=zTs[:, :],
                                            in_=cx.sb["pbc"][0:16, 0:128]),
          reads=["pbc"], writes=["zTs"])
        for dh in range(2):
            E("tensor", lambda e, dh=dh: e.matmul(
                cx.sb["pA" if dh == 0 else "pB"][:, :], zTs[0:16, :],
                cx.sb["i16f"][0:16, 512 * dh:512 * (dh + 1)],
                start=True, stop=True),
              reads=["zTs", "i16f"], writes=["pA" if dh == 0 else "pB"])
        E("vector", lambda e, nt=nt, b=b, scr=scr: e.tensor_scalar(
            out=scr[:, :], in0=xt2[:, 1024 * b:1024 * (b + 1)],
            scalar1=cx.sb["rst"][:, nt:nt + 1],
            scalar2=cx.sb["mrst"][:, nt:nt + 1],
            op0=ALU.mult, op1=ALU.subtract),
          reads=[xk, "l0rst", "l0mrst"], writes=[sk])
        for dh in range(2):
            E("vector", lambda e, dh=dh: e.tensor_copy(
                out=zbS[:, 512 * dh:512 * (dh + 1)],
                in_=cx.sb["pA" if dh == 0 else "pB"][:, :]),
              reads=["pA" if dh == 0 else "pB"], writes=["zbS"])
        E("vector", lambda e, res=res: e.tensor_tensor(
            out=res[:, :], in0=cx.sb["pout"][:, :], in1=zbS[:, :],
            op=ALU.mult),
          reads=["pout", "zbS"], writes=[rk])
        E("vector", lambda e, res=res, scr=scr: e.tensor_tensor(
            out=res[:, :], in0=res[:, :], in1=scr[:, :], op=ALU.add),
          reads=[rk, sk], writes=[rk])
        E("vector", lambda e, res=res: e.reduce_sum(
            out=st[:, 0:1], in_=res[:, :], axis=AX.X),
          reads=[rk], writes=["st8:s"])
        E("scalar", lambda e, res=res, scr=scr: e.activation(
            out=scr[:, :], in_=res[:, :], func=AF.Square,
            accum_out=st[:, 1:2]),
          reads=[rk], writes=[sk, "st8:q"])
        E("vector", lambda e: e.tensor_scalar(
            out=st[:, 2:3], in0=st[:, 0:1], scalar1=1.0 / 1024.0,
            scalar2=None, op0=ALU.mult),
          reads=["st8:s"], writes=["st8:m"])
        E("scalar", lambda e: e.activation(out=st[:, 3:4], in_=st[:, 2:3],
                                           func=AF.Square),
          reads=["st8:m"], writes=["st8:m2"])
        E("vector", lambda e: e.tensor_scalar(
            out=st[:, 4:5], in0=st[:, 1:2], scalar1=1.0 / 1024.0,
            scalar2=1e-12, op0=ALU.mult, op1=ALU.add),
          reads=["st8:q"], writes=["st8:v"])
        E("vector", lambda e: e.tensor_tensor(
            out=st[:, 4:5], in0=st[:, 4:5], in1=st[:, 3:4], op=ALU.subtract),
          reads=["st8:v", "st8:m2"], writes=["st8:v"])
        E("scalar", lambda e: e.activation(out=st[:, 5:6], in_=st[:, 4:5],
                                           func=AF.Sqrt),
          reads=["st8:v"], writes=["st8:sd"])
        E("vector", lambda e: e.reciprocal(out=st[:, 6:7], in_=st[:, 5:6]),
          reads=["st8:sd"], writes=["st8:r"])
        E("vector", lambda e: e.tensor_tensor(
            out=st[:, 7:8], in0=st[:, 2:3], in1=st[:, 6:7], op=ALU.mult),
          reads=["st8:m", "st8:r"], writes=["st8:mr"])
        E("vector", lambda e, res=res, scr=scr: e.tensor_scalar(
            out=scr[:, :], in0=res[:, :], scalar1=st[:, 6:7],
            scalar2=st[:, 7:8], op0=ALU.mult, op1=ALU.subtract),
          reads=[rk, "st8:r", "st8:mr"], writes=[sk])
        E("vector", lambda e, res=res, scr=scr: e.tensor_tensor(
            out=res[:, :], in0=scr[:, :], in1=gammab[:, :], op=ALU.mult),
          reads=[sk, "gammab"], writes=[rk])
        E("vector", lambda e, res=res, scr=scr: e.tensor_tensor(
            out=scr[:, :], in0=res[:, :], in1=betab[:, :], op=ALU.add),
          reads=[rk, "betab"], writes=[sk])
        E("scalar", lambda e, nt=nt, scr=scr: e.dma_start(
            out=dr["s0"][128 * nt:128 * (nt + 1), :], in_=scr[:, :]),
          reads=[sk], writes=[f"s0d:{nt}"], dma=True)

    b2_stageA(0)
    b2_stageA(1)
    for nt in range(NT):
        b2_stageB(nt)
        if nt + 2 < NT:
            b2_stageA(nt + 2)
    cx.close_scope()
    cx.close_scope()

    if KSTOP == "b2":
        return
    cx.close_scope()

    for _ in range(40):
        pump()
    emit_l0_gating_final(cx)
    BIG = 1024.0
    slot_tf = cx.sbuf("slot_tf", [128, 128], F32)
    slot_ti = cx.sbuf("slot_ti", [128, 128], I32)
    cx.scope()
    cx.psum("pE", (128, 512), F32)
    cx.psum("pF", (128, 512), F32)
    cx.scope()
    mpk = cx.sbuf("mpk", [128, 128], F32)
    ltr = cx.sbuf("ltr", [128, 128], BF16)
    bsm = cx.sbuf("bsm", [128, 128], BF16)
    cntb = cx.sbuf("cntb", [128, 1], BF16)
    offs = cx.sbuf("offs", [128, 1], F32)
    th = cx.sb["th"]
    s8, s8t = cx.sb["s8"], cx.sb["s8t"]
    thb, cnt_col, ind_sb = cx.sb["thb"], cx.sb["cnt_col"], cx.sb["ind_sb"]
    pbc = cx.sb["pbc"]
    E("tensor", lambda e: e.matmul(
        pbc[0:128, 0:1], s8[0:8, :], th[0:8, 1:2], start=True, stop=True),
      reads=["s8", "th:hi"], writes=["pbc"])
    E("vector", lambda e: e.tensor_copy(out=thb[:, 0:1], in_=pbc[0:128, 0:1]),
      reads=["pbc"], writes=["thb"])
    E("vector", lambda e: e.tensor_scalar(
        out=ind_sb[:, :], in0=cx.sb["aff_pack"][:, :], scalar1=thb[:, 0:1],
        scalar2=0.0, op0=ALU.is_ge, op1=ALU.add, accum_out=cnt_col[:, 0:1]),
      reads=["aff_pack", "thb"], writes=["ind_f", "cnt_col"])
    E("vector", lambda e: e.tensor_copy(out=mpk[:, :], in_=ind_sb[:, :]),
      reads=["ind_f"], writes=["mpk"])
    E("vector", lambda e: e.tensor_copy(out=cntb[:, 0:1], in_=cnt_col[:, 0:1]),
      reads=["cnt_col"], writes=["cntb"])
    E("gpsimd", lambda e: e.memset(ltr[:, :], 1.0), writes=["ltr"])
    E("gpsimd", lambda e: e.affine_select(
        out=ltr[:, :], in_=ltr[:, :], compare_op=ALU.is_ge, fill=0.0,
        base=-1, pattern=[[1, 128]], channel_multiplier=-1),
      reads=["ltr"], writes=["ltr"])
    s8b = cx.sbuf("s8b", [8, 128], BF16)
    E("vector", lambda e: e.tensor_copy(out=s8b[0:8, :], in_=s8[0:8, :]),
      reads=["s8"], writes=["s8b"])
    E("tensor", lambda e: e.matmul(
        pbc[0:128, 0:128], s8b[0:8, :], s8b[0:8, :], start=True, stop=True),
      reads=["s8b"], writes=["pbc"])
    E("vector", lambda e: e.tensor_copy(out=bsm[:, :], in_=pbc[0:128, 0:128]),
      reads=["pbc"], writes=["bsm"])
    E("vector", lambda e: e.tensor_tensor(
        out=ltr[:, :], in0=ltr[:, :], in1=bsm[:, :], op=ALU.mult),
      reads=["ltr", "bsm"], writes=["ltr"])
    E("tensor", lambda e: e.matmul(
        pbc[0:128, 0:1], ltr[:, :], cntb[:, 0:1], start=True, stop=True),
      reads=["ltr", "cntb"], writes=["pbc"])
    E("vector", lambda e: e.tensor_copy(out=offs[:, 0:1], in_=pbc[0:128, 0:1]),
      reads=["pbc"], writes=["offs"])
    ev8i = cx.sbuf("ev8i", [8, 1], I32)
    ev8 = cx.sbuf("ev8", [8, 1], F32)
    bigcol = cx.sbuf("bigcol", [128, 1], F32)
    offsb = cx.sbuf("offsb", [128, 1], F32)
    E("gpsimd", lambda e: e.iota(ev8i[:, :], pattern=[[0, 1]], base=0,
                                 channel_multiplier=1),
      writes=["ev8i"])
    E("vector", lambda e: e.tensor_copy(out=ev8[:, :], in_=ev8i[:, :]),
      reads=["ev8i"], writes=["ev8"])
    E("tensor", lambda e: e.matmul(
        pbc[0:128, 4:5], s8[0:8, :], ev8[0:8, 0:1], start=True, stop=True),
      reads=["s8", "ev8"], writes=["pbc"])
    E("vector", lambda e: e.scalar_tensor_tensor(
        out=bigcol[:, 0:1], in0=pbc[0:128, 4:5], scalar=-16.0,
        in1=cx.sb["iota_col"][:, 0:1], op0=ALU.mult, op1=ALU.add),
      reads=["pbc", "iota_col"], writes=["ntcol"])
    E("vector", lambda e: e.tensor_tensor(
        out=offsb[:, 0:1], in0=offs[:, 0:1], in1=bigcol[:, 0:1],
        op=ALU.subtract),
      reads=["offs", "ntcol"], writes=["offsb"])
    E("vector", lambda e: e.tensor_scalar(
        out=offsb[:, 0:1], in0=offsb[:, 0:1], scalar1=-1.0 - BIG,
        scalar2=None, op0=ALU.add),
      reads=["offsb"], writes=["offsb"])
    E("vector", lambda e: e.tensor_scalar(
        out=bigcol[:, 0:1], in0=bigcol[:, 0:1], scalar1=BIG,
        scalar2=None, op0=ALU.add),
      reads=["ntcol"], writes=["bigcol"])
    E("vector", lambda e: e.tensor_tensor_scan(
        out=mpk[:, :], data0=mpk[:, :], data1=mpk[:, :], initial=0.0,
        op0=ALU.add, op1=ALU.bypass),
      reads=["mpk"], writes=["mpk2"])
    E("vector", lambda e: e.tensor_scalar(
        out=mpk[:, :], in0=mpk[:, :], scalar1=offsb[:, 0:1],
        scalar2=None, op0=ALU.add),
      reads=["mpk2", "offsb"], writes=["mpk3"])
    E("vector", lambda e: e.tensor_tensor(
        out=mpk[:, :], in0=mpk[:, :],
        in1=ind_sb[:, :], op=ALU.mult),
      reads=["mpk3", "ind_f"], writes=["mpk4"])
    E("vector", lambda e: e.tensor_scalar(
        out=mpk[:, :], in0=mpk[:, :], scalar1=bigcol[:, 0:1],
        scalar2=None, op0=ALU.add),
      reads=["mpk4", "bigcol"], writes=["mpk5"])
    E("tensor", lambda e: e.transpose(
        pbc[0:128, 0:128], mpk[:, :], cx.sb["identf"][:, :]),
      reads=["mpk5", "identf"], writes=["pbc"])
    E("vector", lambda e: e.tensor_copy(out=slot_tf[:, :],
                                        in_=pbc[0:128, 0:128]),
      reads=["pbc"], writes=["slot_tf"])
    E("vector", lambda e: e.tensor_copy(out=slot_ti[:, :], in_=slot_tf[:, :]),
      reads=["slot_tf"], writes=["slot_ti"])
    cx.close_scope()

    zb128 = cx.sbuf("zb128", [128, 1040], BF16)
    z128f = cx.sbuf("z128f", [128, 1024], F32)
    E("gpsimd", lambda e: e.memset(zb128[:, :], 0.0), writes=["zb128"])
    E("gpsimd", lambda e: e.memset(z128f[:, :], 0.0), writes=["z128f"])
    for nt_z in range(NT):
        E("sync", lambda e, nt_z=nt_z: e.dma_start(
            out=dr["g1d"][128 * nt_z:128 * (nt_z + 1), :], in_=z128f[:, :]),
          reads=["z128f"], writes=[f"g1z:{nt_z}"], dma=True)
    xg_keys = {xi: [f"xg:{xi}:{nt}" for nt in range(NT)] for xi in range(2)}
    for xi in range(2):
        for r0 in range(0, 1040, 128):
            rn = min(128, 1040 - r0)
            E("scalar", lambda e, xi=xi, r0=r0, rn=rn: e.dma_start(
                out=dr[f"xg{xi}"][r0:r0 + rn, :], in_=zb128[0:rn, :]),
              reads=["zb128"], writes=xg_keys[xi], dma=True)
    meta4 = cx.sbuf("meta4", [128, 512], F32)
    wb_b = cx.sbuf("wb_b", [128, 128], BF16)
    wlo_t = cx.sbuf("wlo_t", [128, 128], F32)
    iota8i = cx.sbuf("iota8i", [128, 8], I32)
    iota8 = cx.sbuf("iota8", [128, 8], F32)
    w_t = cx.sb["w_t"]
    wtk = [f"w_t:{nt}" for nt in range(NT)]
    E("vector", lambda e: e.tensor_copy(out=wb_b[:, :], in_=w_t[:, :]),
      reads=wtk, writes=["wb_b"])
    E("vector", lambda e: e.tensor_tensor(
        out=wlo_t[:, :], in0=w_t[:, :], in1=wb_b[:, :], op=ALU.subtract),
      reads=wtk + ["wb_b"], writes=["wlo_t"])
    E("gpsimd", lambda e: e.iota(iota8i[:, :], pattern=[[0, 8]], base=0,
                                 channel_multiplier=1),
      writes=["iota8i"])
    E("vector", lambda e: e.tensor_copy(out=iota8[:, :], in_=iota8i[:, :]),
      reads=["iota8i"], writes=["iota8"])
    m4v = meta4[:, :].rearrange("p (e nt four) -> p e nt four", e=8, four=4)
    for e_i in range(NE):
        wv = w_t[:, :].rearrange("p (nt e) -> p nt e", e=8)[:, :, e_i]
        lv = wlo_t[:, :].rearrange("p (nt e) -> p nt e", e=8)[:, :, e_i]
        E("vector", lambda e, e_i=e_i, wv=wv: e.tensor_copy(
            out=m4v[:, e_i, :, 0], in_=wv),
          reads=wtk, writes=[f"m4w:{e_i}"])
        E("vector", lambda e, e_i=e_i, lv=lv: e.tensor_copy(
            out=m4v[:, e_i, :, 1], in_=lv),
          reads=["wlo_t"], writes=[f"m4l:{e_i}"])
    for nt in range(NT):
        E("vector", lambda e, nt=nt: e.tensor_scalar(
            out=m4v[:, :, nt, 2], in0=iota8[:, :], scalar1=0.0,
            scalar2=float(nt // 2), op0=ALU.mult, op1=ALU.add),
          reads=["iota8"], writes=[f"m4h:{nt}"])
        E("vector", lambda e, nt=nt: e.tensor_scalar(
            out=m4v[:, :, nt, 3], in0=iota8[:, :],
            scalar1=float(128 * (nt & 1)), scalar2=None, op0=ALU.add),
          reads=["iota8"], writes=[f"m4o:{nt}"])
    xgT = [cx.sbuf(f"xgT{c}", [128, 1040], BF16) for c in range(CC)]
    mcolb = cx.sbuf("mcolb", [128, 64], BF16)
    mcol = cx.sbuf("mcol", [128, 48], F32)
    wcol2 = cx.sbuf("wcol2", [128, 16], F32)
    idxi2 = cx.sbuf("idxi2", [128, 16], I32)
    xbuf = cx.sbuf("xbuf", [128, 2080], BF16)
    h_all = cx.sbuf("h_all", [128, 16 * 1024], BF16)
    wgu = cx.sbuf("wgu", [128, 16384], BF16)
    wd2 = cx.sbuf("wd2", [128, 32768], BF16)
    w_t = cx.sb["w_t"]
    scrA = cx.sb["scr512a"]
    def load_wd2(e_i):
        wdb = e_i % 2
        for k4 in range(4):
            E("gpsimd", lambda eng, e_i=e_i, k4=k4, wdb=wdb: eng.dma_start(
                out=wd2[:, 16384 * wdb + 4096 * k4:
                        16384 * wdb + 4096 * (k4 + 1)].rearrange(
                    "p (c j) -> p c j", j=1024),
                in_=dr["Wd"][e_i, 512 * k4:512 * (k4 + 1), :].rearrange(
                    "(c p) j -> p c j", p=128)),
              writes=[f"wd2:{wdb}:{ic}" for ic in range(4 * k4, 4 * k4 + 4)],
              dma=True)

    def prep_scat(e_i):
        xgb = e_i % 2
        for nt in range(NT):
            xb = nt % 2
            E("sync", lambda eng, nt=nt, xb=xb: eng.dma_start(
                out=xbuf[:, 1040 * xb:1040 * xb + 1024],
                in_=dr["xnt"][128 * nt:128 * (nt + 1), :]),
              writes=[f"xbuf:{xb}"], dma=True)
            E("vector", lambda eng, e_i=e_i, nt=nt, xb=xb: eng.tensor_copy(
                out=xbuf[:, 1040 * xb + 1024:1040 * xb + 1028],
                in_=meta4[:, 4 * (16 * e_i + nt):4 * (16 * e_i + nt) + 4]),
              reads=[f"m4w:{e_i}", f"m4l:{e_i}", f"m4h:{nt}", f"m4o:{nt}"],
              writes=[f"xbufm:{xb}"])
            E("gpsimd", lambda eng, e_i=e_i, nt=nt, xb=xb, xgb=xgb:
              eng.indirect_dma_start(
                  out=dr[f"xg{xgb}"][:, :],
                  out_offset=IndirectOffsetOnAxis(
                      slot_ti[:, 16 * e_i + nt:16 * e_i + nt + 1], 0),
                  in_=xbuf[:, 1040 * xb:1040 * (xb + 1)], in_offset=None),
              reads=[f"xbuf:{xb}", f"xbufm:{xb}", "slot_ti"],
              writes=[f"xg:{xgb}:{nt}"], dma=True)

    def prep_xgt(e_i):
        xgb = e_i % 2
        for cc in range(CC):
            E("scalar",
              lambda eng, cc=cc, xgb=xgb: eng.dma_start(
                out=xgT[cc][:, :],
                in_=dr[f"xg{xgb}"][:, 128 * cc:128 * (cc + 1)],
                transpose=True),
              reads=xg_keys[xgb], writes=[f"xgT{cc}"], dma=True)
        E("scalar", lambda eng, xgb=xgb: eng.dma_start(
            out=mcolb[:, 32 * xgb:32 * (xgb + 1)].rearrange(
                "p (g c) -> p g c", c=4),
            in_=dr[f"xg{xgb}"][0:1024, 1024:1028].rearrange(
                "(g p) c -> p g c", p=128)),
          reads=xg_keys[xgb], writes=[f"xmT:{xgb}"], dma=True)
        for zg in range(8):
            E("scalar", lambda eng, xgb=xgb, zg=zg: eng.dma_start(
                out=dr[f"xg{xgb}"][128 * zg:128 * (zg + 1), 1024:1040],
                in_=zb128[:, 0:16]),
              reads=["zb128", f"xmT:{xgb}"], writes=xg_keys[xgb], dma=True)
        E("scalar", lambda eng, xgb=xgb: eng.activation(
            out=mcol[:, 0:32], in_=mcolb[:, 32 * xgb:32 * (xgb + 1)],
            func=AF.Copy),
          reads=[f"xmT:{xgb}"], writes=["mcol"])
        mv = mcol[:, 0:32].rearrange("p (g four) -> p g four", four=4)
        E("vector", lambda eng, xgb=xgb: eng.tensor_tensor(
            out=wcol2[:, 8 * xgb:8 * (xgb + 1)], in0=mv[:, :, 0],
            in1=mv[:, :, 1], op=ALU.add),
          reads=["mcol"], writes=[f"wcol:{xgb}"])
        E("vector", lambda eng: eng.tensor_scalar(
            out=mcol[:, 40:48], in0=mv[:, :, 2], scalar1=256.0,
            scalar2=None, op0=ALU.mult),
          reads=["mcol"], writes=["mcolx"])
        E("vector", lambda eng: eng.tensor_tensor(
            out=mcol[:, 40:48], in0=mcol[:, 40:48], in1=mv[:, :, 3],
            op=ALU.add),
          reads=["mcolx", "mcol"], writes=["mcolx"])
        E("vector", lambda eng, xgb=xgb: eng.tensor_copy(
            out=idxi2[:, 8 * xgb:8 * (xgb + 1)], in_=mcol[:, 40:48]),
          reads=["mcolx"], writes=[f"idxi:{xgb}"])

    def wgu_load(e_i, ig):
        gb = ig % 2
        E("gpsimd", lambda eng, e_i=e_i, ig=ig, gb=gb: eng.dma_start(
            out=wgu[:, 8192 * gb:8192 * gb + 4096].rearrange(
                "p (c i) -> p c i", i=512),
            in_=dr["Wg"][e_i, :, 512 * ig:512 * (ig + 1)].rearrange(
                "(c p) i -> p c i", p=128)),
          writes=[f"wgu:g{gb}"], dma=True)
        E("gpsimd", lambda eng, e_i=e_i, ig=ig, gb=gb: eng.dma_start(
            out=wgu[:, 8192 * gb + 4096:8192 * gb + 8192].rearrange(
                "p (c i) -> p c i", i=512),
            in_=dr["Wu"][e_i, :, 512 * ig:512 * (ig + 1)].rearrange(
                "(c p) i -> p c i", p=128)),
          writes=[f"wgu:u{gb}"], dma=True)

    def phase1(e_i, skip01=False):
        for ig in range(4):
            gb = ig % 2
            if ig >= 2 or not skip01:
                wgu_load(e_i, ig)
            for sub in range(4):
                ic = 4 * ig + sub
                for ks in range(2):
                    par = (2 * sub + ks) % 2
                    pg, pgk = (("pA", "pA") if par == 0 else ("pE", "pE"))
                    pu, puk = (("pB", "pB") if par == 0 else ("pF", "pF"))
                    sA, sAk = (("scr512a", "scr512a") if par == 0
                               else ("scr512b", "scr512b"))
                    mm_chain(cx, cx.sb[pg][:, :], pgk,
                             [(wgu[:, 8192 * gb + 512 * cc + 128 * sub:
                                   8192 * gb + 512 * cc + 128 * (sub + 1)],
                               xgT[cc][:, 512 * ks:512 * (ks + 1)],
                               [f"wgu:g{gb}", f"xgT{cc}"])
                              for cc in range(CC)])
                    mm_chain(cx, cx.sb[pu][:, :], puk,
                             [(wgu[:, 8192 * gb + 4096 + 512 * cc + 128 * sub:
                                   8192 * gb + 4096 + 512 * cc + 128 * (sub + 1)],
                               xgT[cc][:, 512 * ks:512 * (ks + 1)],
                               [f"wgu:u{gb}", f"xgT{cc}"])
                              for cc in range(CC)])
                    E("scalar", lambda eng, pg=pg, sA=sA: eng.activation(
                        out=cx.sb[sA][:, :], in_=cx.sb[pg][:, :],
                        func=AF.Silu),
                      reads=[pgk], writes=[sAk])
                    E("vector", lambda eng, ic=ic, ks=ks, pu=pu, sA=sA:
                      eng.tensor_tensor(
                        out=h_all[:, 1024 * ic + 512 * ks:
                                  1024 * ic + 512 * (ks + 1)],
                        in0=cx.sb[sA][:, :], in1=cx.sb[pu][:, :],
                        op=ALU.mult),
                      reads=[sAk, puk], writes=[f"h:{ic}:{ks}"])

    def phase2(e_i):
        wdb = e_i % 2
        xgb = e_i % 2
        for kt in range(8):
            cb = kt % 2
            ctb = cx.sb["res1k"] if cb == 0 else cx.sb["scr1k"]
            ck = "res1k" if cb == 0 else "scr1k"
            for dh in range(2):
                ps = cx.sb["pE"] if dh == 0 else cx.sb["pF"]
                pk = "pE" if dh == 0 else "pF"
                mm_chain(cx, ps[:, :], pk,
                         [(h_all[:, 1024 * ic + 128 * kt:
                                 1024 * ic + 128 * (kt + 1)],
                           wd2[:, 16384 * wdb + 1024 * ic + 512 * dh:
                               16384 * wdb + 1024 * ic + 512 * (dh + 1)],
                           [f"h:{ic}:{kt // 4}", f"wd2:{wdb}:{ic}"])
                          for ic in range(16)])
                E("vector", lambda eng, ps=ps, pk=pk, dh=dh, ctb=ctb,
                  kt=kt, xgb=xgb: eng.tensor_scalar(
                      out=ctb[:, 512 * dh:512 * (dh + 1)], in0=ps[:, :],
                      scalar1=wcol2[:, 8 * xgb + kt:8 * xgb + kt + 1],
                      scalar2=None, op0=ALU.mult),
                  reads=[pk, f"wcol:{xgb}"], writes=[ck])
            prev = ([f"g1z:{nt_z}" for nt_z in range(NT)] if e_i == 0
                    else [f"g1w:{e_i - 1}:{k}" for k in range(8)])
            E("gpsimd", lambda eng, e_i=e_i, kt=kt, ctb=ctb, xgb=xgb:
              eng.indirect_dma_start(
                  out=dr["g1d"][:, :],
                  out_offset=IndirectOffsetOnAxis(
                      idxi2[:, 8 * xgb + kt:8 * xgb + kt + 1], 0),
                  in_=ctb[:, :], in_offset=None, compute_op=ALU.add),
              reads=[ck, f"idxi:{xgb}"] + prev,
              writes=[f"g1w:{e_i}:{kt}"], dma=True)

    for xb in range(2):
        E("gpsimd", lambda eng, xb=xb: eng.memset(
            xbuf[:, 1040 * xb + 1028:1040 * (xb + 1)], 0.0),
          writes=[f"xbufm:{xb}"])
    load_wd2(0)
    load_wd2(1)
    prep_scat(0)
    prep_scat(1)
    for e_i in range(NE):
        prep_xgt(e_i)
        phase1(e_i, skip01=(e_i > 0))
        if e_i + 1 < NE:
            wgu_load(e_i + 1, 0)
            wgu_load(e_i + 1, 1)
        phase2(e_i)
        if e_i + 2 < NE:
            prep_scat(e_i + 2)
            load_wd2(e_i + 2)
    cx.close_scope()

    if KSTOP == "c":
        return
    cx.scope()
    stg2 = cx.sbuf("stg2", [128, 2048], F32)
    acc2 = cx.sbuf("acc2", [128, 2048], F32)
    xt2 = cx.sb["xt2"]
    g1done = [f"g1w:{NE - 1}:{k}" for k in range(8)]
    for nt in range(NT):
        b = nt % 2
        acc = acc2[:, 1024 * b:1024 * (b + 1)]
        ak = f"acc2:{b}"
        E("sync", lambda e, nt=nt, b=b: e.dma_start(
            out=xt2[:, 1024 * b:1024 * (b + 1)],
            in_=dr["x"][128 * nt:128 * (nt + 1), :]),
          writes=[f"xt2:{b}"], dma=True)
        E("scalar", lambda e, nt=nt, b=b: e.dma_start(
            out=stg2[:, 1024 * b:1024 * (b + 1)],
            in_=dr["s0"][128 * nt:128 * (nt + 1), :]),
          reads=[f"s0d:{nt}"], writes=[f"stg2:{b}"], dma=True)
        E("sync", lambda e, nt=nt, acc=acc: e.dma_start(
            out=acc[:, :], in_=dr["g1d"][128 * nt:128 * (nt + 1), :]),
          reads=g1done, writes=[ak], dma=True)
        E("scalar", lambda e, acc=acc: e.activation(
            out=acc[:, :], in_=acc[:, :], func=AF.Silu),
          reads=[ak], writes=[ak])
        E("vector", lambda e, acc=acc, b=b: e.tensor_tensor(
            out=acc[:, :], in0=acc[:, :],
            in1=stg2[:, 1024 * b:1024 * (b + 1)], op=ALU.mult),
          reads=[ak, f"stg2:{b}"], writes=[ak])
        E("vector", lambda e, acc=acc, b=b: e.tensor_tensor(
            out=acc[:, :], in0=acc[:, :],
            in1=xt2[:, 1024 * b:1024 * (b + 1)], op=ALU.add),
          reads=[ak, f"xt2:{b}"], writes=[ak])
        E("sync", lambda e, acc=acc, nt=nt: e.dma_start(
            out=dr["x1"][128 * nt:128 * (nt + 1), :], in_=acc[:, :]),
          reads=[ak], writes=[f"x1d:{nt}"], dma=True)
    cx.close_scope()


def emit_l1_routing(cx, dr):
    p = cx.p
    E = p.emit
    xnf = [cx.sb[f"l1xnf{c}"] for c in range(CC)]
    xnlo = [cx.sb[f"l1xnlo{c}"] for c in range(CC)]
    tsgb = cx.sb["tsgb"]
    sgn_f = cx.sb["sgn_f"]
    o_all = cx.sb["o_all"]
    o_T = cx.sb["o_T"]
    st = cx.sb["st8"]
    E("gpsimd", lambda e: e.dma_start(out=tsgb[:, :], in_=dr["tsig"][:, :]),
      writes=["tsgb"], dma=True)
    E("scalar", lambda e: e.activation(out=tsgb[:, :], in_=tsgb[:, :],
                                       func=AF.Sign),
      reads=["tsgb"], writes=["tsgb"])
    ptr = cx.sb["ptr"]
    for cc in range(CC):
        bank = cc % 2
        bkey = "ptrA" if bank == 0 else "ptrB"
        E("tensor", lambda e, cc=cc, bank=bank: e.transpose(
            ptr[:, 1024 * bank:1024 * bank + 8],
            tsgb[0:8, 128 * cc:128 * (cc + 1)], cx.sb["identb"][0:8, 0:8]),
          reads=["tsgb", "identb"], writes=[bkey])
        E("scalar", lambda e, cc=cc, bank=bank: e.activation(
            out=sgn_f[:, 8 * cc:8 * (cc + 1)],
            in_=ptr[:, 1024 * bank:1024 * bank + 8],
            func=AF.Copy),
          reads=[bkey], writes=["sgn_f"])
    pA = cx.sb["pA"]
    for nt in range(NT):
        pairs = []
        for cc in range(CC):
            pairs.append((xnf[cc][:, 128 * nt:128 * (nt + 1)],
                          sgn_f[:, 8 * cc:8 * (cc + 1)],
                          [f"l1xnf{cc}:{nt}", "sgn_f"]))
        for cc in range(CC):
            pairs.append((xnlo[cc][:, 128 * nt:128 * (nt + 1)],
                          sgn_f[:, 8 * cc:8 * (cc + 1)],
                          [f"l1xnlo{cc}:{nt}", "sgn_f"]))
        mm_chain(cx, pA[:, 0:8], "pA", pairs)
        E("vector", lambda e: e.reduce_max(out=st[:, 0:1], in_=pA[:, 0:8],
                                           axis=AX.X),
          reads=["pA"], writes=["st8:mx"])
        E("vector", lambda e, nt=nt: e.tensor_scalar(
            out=o_all[:, 8 * nt:8 * (nt + 1)], in0=pA[:, 0:8],
            scalar1=st[:, 0:1], scalar2=None, op0=ALU.is_ge),
          reads=["pA", "st8:mx"], writes=[f"o_all:{nt}"])
        E("tensor", lambda e, nt=nt: e.transpose(
            cx.sb["pbc"][0:8, 0:128],
            o_all[:, 8 * nt:8 * (nt + 1)], cx.sb["identf"][:, :]),
          reads=[f"o_all:{nt}", "identf"], writes=["pbc"])
        E("scalar", lambda e, nt=nt: e.activation(
            out=o_T[0:8, 128 * nt:128 * (nt + 1)],
            in_=cx.sb["pbc"][0:8, 0:128],
            func=AF.Copy),
          reads=["pbc"], writes=[f"o_T:{nt}"])


def emit_l1_ffn(cx, dr):
    p = cx.p
    E = p.emit
    xnf = [cx.sb[f"l1xnf{c}"] for c in range(CC)]
    w1b = cx.sb["w1b"]
    b1sb = cx.sb["b1sb"]
    acc1sb = cx.sb["acc1sb"]
    o_all = cx.sb["o_all"]
    o_T = cx.sb["o_T"]
    res1k = cx.sb["res1k"]
    E("gpsimd", lambda e: e.dma_start(out=b1sb[:, :], in_=dr["b1"][:, :]),
      writes=["b1sb"], dma=True)
    for t in range(8):
        wb = t % 2
        for cc in range(CC):
            E("gpsimd", lambda e, t=t, cc=cc, wb=wb: e.dma_start(
                out=w1b[:, 8192 * wb + 1024 * cc:8192 * wb + 1024 * (cc + 1)],
                in_=dr["W1"][t, 128 * cc:128 * (cc + 1), :]),
              writes=[f"w1b:{wb}:{cc}"], dma=True)
        for nt in range(NT):
            cb = nt % 2
            for dh in range(2):
                ps = cx.sb["pA"] if dh == 0 else cx.sb["pB"]
                pk = "pA" if dh == 0 else "pB"
                mm_chain(cx, ps[:, :], pk,
                         [(xnf[cc][:, 128 * nt:128 * (nt + 1)],
                           w1b[:, 8192 * wb + 1024 * cc + 512 * dh:
                               8192 * wb + 1024 * cc + 512 * (dh + 1)],
                           [f"l1xnf{cc}:{nt}", f"w1b:{wb}:{cc}"])
                          for cc in range(CC)])
                if t == 0:
                    E("vector", lambda e, ps=ps, pk=pk, nt=nt, dh=dh, t=t:
                      e.tensor_scalar(
                          out=acc1sb[:, 1024 * nt + 512 * dh:
                                     1024 * nt + 512 * (dh + 1)],
                          in0=ps[:, :],
                          scalar1=o_all[:, 8 * nt + t:8 * nt + t + 1],
                          scalar2=None, op0=ALU.mult),
                      reads=[pk, f"o_all:{nt}"], writes=[f"a1s:{nt}"])
                else:
                    E("vector", lambda e, ps=ps, pk=pk, nt=nt, dh=dh, t=t:
                      e.scalar_tensor_tensor(
                          out=acc1sb[:, 1024 * nt + 512 * dh:
                                     1024 * nt + 512 * (dh + 1)],
                          in0=ps[:, :],
                          scalar=o_all[:, 8 * nt + t:8 * nt + t + 1],
                          in1=acc1sb[:, 1024 * nt + 512 * dh:
                                     1024 * nt + 512 * (dh + 1)],
                          op0=ALU.mult, op1=ALU.add),
                      reads=[pk, f"o_all:{nt}", f"a1s:{nt}"],
                      writes=[f"a1s:{nt}"])
    for nt in range(NT):
        for dh in range(2):
            ps = cx.sb["pA"] if dh == 0 else cx.sb["pB"]
            pk = "pA" if dh == 0 else "pB"
            E("tensor", lambda e, nt=nt, dh=dh, ps=ps: e.matmul(
                ps[:, :], o_T[0:8, 128 * nt:128 * (nt + 1)],
                b1sb[0:8, 512 * dh:512 * (dh + 1)], start=True, stop=True),
              reads=[f"o_T:{nt}", "b1sb"], writes=[pk])
            E("vector", lambda e, ps=ps, pk=pk, nt=nt, dh=dh: e.tensor_tensor(
                out=acc1sb[:, 1024 * nt + 512 * dh:1024 * nt + 512 * (dh + 1)],
                in0=acc1sb[:, 1024 * nt + 512 * dh:1024 * nt + 512 * (dh + 1)],
                in1=ps[:, :], op=ALU.add),
              reads=[pk, f"a1s:{nt}"], writes=[f"a1s:{nt}"])
        E("sync", lambda e, nt=nt: e.dma_start(
            out=dr["acc1"][128 * nt:128 * (nt + 1), :],
            in_=acc1sb[:, 1024 * nt:1024 * (nt + 1)]),
          reads=[f"a1s:{nt}"], writes=[f"acc1d:{nt}"], dma=True)


def emit_l1_attention(cx, dr):
    p = cx.p
    E = p.emit
    xnf = [cx.sb[f"l1xnf{c}"] for c in range(CC)]
    q_f = [cx.sb[f"q_f{c}"] for c in range(CC)]
    k_f = [cx.sb[f"k_f{c}"] for c in range(CC)]
    vt1 = cx.sb["vt1"]

    cx.scope()
    qwb = [cx.sbuf(f"qwb{c}", [128, 1024], BF16) for c in range(CC)]
    kwb = [cx.sbuf(f"kwb{c}", [128, 1024], BF16) for c in range(CC)]
    for cc in range(CC):
        E("gpsimd", lambda e, cc=cc: e.dma_start(
            out=qwb[cc][:, :], in_=dr["qw"][128 * cc:128 * (cc + 1), :]),
          writes=[f"qwb{cc}"], dma=True)
        E("gpsimd", lambda e, cc=cc: e.dma_start(
            out=kwb[cc][:, :], in_=dr["kw"][128 * cc:128 * (cc + 1), :]),
          writes=[f"kwb{cc}"], dma=True)
    for jc in range(CC):
        for ns in range(4):
            for which in ("q", "k"):
                wsrc = qwb if which == "q" else kwb
                dst = q_f if which == "q" else k_f
                dn = "q_f" if which == "q" else "k_f"
                wn = "qwb" if which == "q" else "kwb"
                ps = cx.sb["pA"] if which == "q" else cx.sb["pB"]
                pk = "pA" if which == "q" else "pB"
                mm_chain(cx, ps[:, :], pk,
                         [(wsrc[cc][:, 128 * jc:128 * (jc + 1)],
                           xnf[cc][:, 512 * ns:512 * (ns + 1)],
                           [f"{wn}{cc}"] +
                           [f"l1xnf{cc}:{4 * ns + u}" for u in range(4)])
                          for cc in range(CC)])
                E("scalar", lambda e, ps=ps, pk=pk, jc=jc, ns=ns, dst=dst,
                  dn=dn: e.activation(
                      out=dst[jc][:, 512 * ns:512 * (ns + 1)], in_=ps[:, :],
                      func=AF.Copy),
                  reads=[pk], writes=[f"{dn}{jc}"])
    cx.close_scope()
    cx.scope()
    vwb = [cx.sbuf(f"vwb{c}", [128, 1024], BF16) for c in range(CC)]
    for cc in range(CC):
        E("gpsimd", lambda e, cc=cc: e.dma_start(
            out=vwb[cc][:, :], in_=dr["vw"][128 * cc:128 * (cc + 1), :]),
          writes=[f"vwb{cc}"], dma=True)
    for nt in range(NT):
        for jh in range(2):
            ps = cx.sb["pA"] if jh == 0 else cx.sb["pB"]
            pk = "pA" if jh == 0 else "pB"
            mm_chain(cx, ps[:, :], pk,
                     [(xnf[cc][:, 128 * nt:128 * (nt + 1)],
                       vwb[cc][:, 512 * jh:512 * (jh + 1)],
                       [f"vwb{cc}", f"l1xnf{cc}:{nt}"]) for cc in range(CC)])
            E("vector", lambda e, ps=ps, pk=pk, nt=nt, jh=jh: e.tensor_copy(
                out=vt1[:, 1040 * nt + 520 * jh:1040 * nt + 520 * (jh + 1)]
                .rearrange("p (h x) -> p h x", x=65)[:, :, 0:64],
                in_=ps[:, :].rearrange("p (h x) -> p h x", x=64)),
              reads=[pk], writes=[f"vt1:{nt}"])
        E("vector", lambda e, nt=nt: e.memset(
            vt1[:, 1040 * nt:1040 * (nt + 1)]
            .rearrange("p (h x) -> p h x", x=65)[:, :, 64:65], 1.0),
          reads=[], writes=[f"vt1:{nt}"])
    cx.close_scope()


def emit_l1_attention_core(cx, dr):
    p = cx.p
    E = p.emit
    q_f = [cx.sb[f"q_f{c}"] for c in range(CC)]
    k_f = [cx.sb[f"k_f{c}"] for c in range(CC)]
    vt1 = cx.sb["vt1"]
    aof = [cx.sb[f"aof{c}"] for c in range(CC)]
    tmp_o = cx.sb["tmp_o"]
    ptb = cx.sb["ptb"]
    bcol = cx.sb["bcol"]
    slot_ctr = [0]
    zrow = cx.sb["zrow"]
    zrow0 = cx.sb["zrow0"]

    def wchunk(h):
        return 128 if h < 2 else (256 if h == 2 else 512)

    for h in range(NH):
        sg = SLOPES[h]
        pair, odd = h // 2, h % 2
        w = wchunk(h)
        skip_d = 80.0 / sg
        for isup in range(4):
            i0 = 512 * isup
            jmax = min(NT, (i0 + 512) // 128)
            live = []
            for jt in range(jmax):
                j0 = 128 * jt
                chunks = []
                any_live = False
                for kq in range(512 // w):
                    ick = i0 + w * kq
                    if j0 - ick >= w:
                        st_ = "invalid"
                    elif ick - (j0 + 127) > skip_d:
                        st_ = "skip"
                    elif j0 - ick >= 0:
                        st_ = "partial"
                    else:
                        st_ = "full"
                    chunks.append((kq, ick, st_))
                    if st_ in ("partial", "full"):
                        any_live = True
                if any_live:
                    live.append((jt, chunks))

            def emit_qk_exp(idx, h=h, pair=pair, odd=odd, sg=sg, w=w,
                            isup=isup):
                jt, chunks = live[idx]
                j0 = 128 * jt
                pb = idx % 2
                if odd == 0:
                    ps = cx.sb["pS"] if pb == 0 else cx.sb["pA"]
                    pk = "pS" if pb == 0 else "pA"
                else:
                    ps = cx.sb["pS2"] if pb == 0 else cx.sb["pB"]
                    pk = "pS2" if pb == 0 else "pB"
                pb4 = idx % 4
                ptk = f"ptb:{pb4}"
                pt = ptb[:, 512 * pb4:512 * (pb4 + 1)]
                E("tensor", lambda e, ps=ps, jt=jt, pair=pair, odd=odd,
                  isup=isup: e.matmul(
                      ps[:, :],
                      k_f[pair][64 * odd:64 * (odd + 1),
                                128 * jt:128 * (jt + 1)],
                      q_f[pair][64 * odd:64 * (odd + 1),
                                512 * isup:512 * (isup + 1)],
                      start=True, stop=True),
                  reads=[f"k_f{pair}", f"q_f{pair}"], writes=[pk])
                for kq, ick, st_ in chunks:
                    if st_ in ("invalid", "skip"):
                        E("vector", lambda e, pt=pt, kq=kq, w=w: e.memset(
                            pt[:, w * kq:w * (kq + 1)], 0.0),
                          reads=[], writes=[ptk])
                        continue
                    sl = slot_ctr[0] % 16
                    slot_ctr[0] += 1
                    E("vector", lambda e, j0=j0, ick=ick, sl=sl, sg=sg,
                      w=w: e.tensor_scalar(
                          out=bcol[:, sl:sl + 1],
                          in0=cx.sb["iota_col"][:, 0:1],
                          scalar1=sg, scalar2=sg * (j0 - (ick + w / 2.0)),
                          op0=ALU.mult, op1=ALU.add),
                      reads=["iota_col"], writes=[f"bcol:{sl}"])
                    E("scalar", lambda e, pt=pt, ps=ps, kq=kq, sl=sl, w=w:
                      e.activation(
                          out=pt[:, w * kq:w * (kq + 1)],
                          in_=ps[:, w * kq:w * (kq + 1)], func=AF.Exp,
                          scale=0.125, bias=bcol[:, sl:sl + 1]),
                      reads=[pk, f"bcol:{sl}"], writes=[ptk])
                    if st_ == "partial":
                        dlt = (j0 - ick) // 128
                        E("vector", lambda e, pt=pt, kq=kq, dlt=dlt, w=w:
                          e.tensor_tensor(
                              out=pt[:, w * kq:w * (kq + 1)],
                              in0=pt[:, w * kq:w * (kq + 1)],
                              in1=cx.sb["cmask"][:, 512 * dlt:512 * dlt + w],
                              op=ALU.mult),
                          reads=[ptk, "cmask"], writes=[ptk])

            def emit_pv(idx, h=h):
                jt, _ = live[idx]
                pb4 = idx % 4
                ptk = f"ptb:{pb4}"
                pt = ptb[:, 512 * pb4:512 * (pb4 + 1)]
                E("tensor", lambda e, pt=pt, jt=jt, h=h, fp=(idx == 0):
                  e.matmul(
                    cx.sb["pO"][0:65, :],
                    vt1[:, 1040 * jt + 65 * h:1040 * jt + 65 * (h + 1)],
                    pt, start=fp, stop=False, skip_group_check=True),
                  reads=[ptk, f"vt1:{jt}"], writes=["pO"])

            LAG = 2
            for idx in range(len(live)):
                emit_qk_exp(idx)
                if idx >= LAG:
                    emit_pv(idx - LAG)
            for idx in range(max(0, len(live) - LAG), len(live)):
                emit_pv(idx)
            if odd == 0:
                E("scalar", lambda e, pair=pair, isup=isup: e.activation(
                    out=aof[pair][0:64, 512 * isup:512 * (isup + 1)],
                    in_=cx.sb["pO"][0:64, :], func=AF.Copy),
                  reads=["pO"], writes=[f"aof{pair}:{isup}"])
            else:
                E("scalar", lambda e, isup=isup: e.activation(
                    out=tmp_o[0:64, 512 * isup:512 * (isup + 1)],
                    in_=cx.sb["pO"][0:64, :], func=AF.Copy),
                  reads=["pO"], writes=[f"tmp_o:{isup}"])
            E("vector", lambda e, isup=isup: e.tensor_copy(
                out=zrow[64:65, 512 * isup:512 * (isup + 1)],
                in_=cx.sb["pO"][64:65, :]),
              reads=["pO"], writes=[f"zrow:{isup}"])
        E("sync", lambda e: e.dma_start(out=zrow0[0:1, :], in_=zrow[64:65, :]),
          reads=[f"zrow:{u}" for u in range(4)], writes=["zrow0"], dma=True)
        E("vector", lambda e: e.reciprocal(out=zrow0[0:1, :],
                                           in_=zrow0[0:1, :]),
          reads=["zrow0"], writes=["zrow0"])
        for isup in range(4):
            E("tensor", lambda e, isup=isup: e.matmul(
                cx.sb["pbc"][0:64, :], cx.sb["onesf_row"][0:1, 0:64],
                zrow0[0:1, 512 * isup:512 * (isup + 1)], start=True,
                stop=True),
              reads=["zrow0", "onesf_row"], writes=["pbc"])
            if odd == 0:
                E("vector", lambda e, pair=pair, isup=isup: e.tensor_tensor(
                    out=aof[pair][0:64, 512 * isup:512 * (isup + 1)],
                    in0=aof[pair][0:64, 512 * isup:512 * (isup + 1)],
                    in1=cx.sb["pbc"][0:64, :], op=ALU.mult),
                  reads=[f"aof{pair}:{isup}", "pbc"],
                  writes=[f"aof{pair}:{isup}"])
            else:
                E("vector", lambda e, isup=isup: e.tensor_tensor(
                    out=tmp_o[0:64, 512 * isup:512 * (isup + 1)],
                    in0=tmp_o[0:64, 512 * isup:512 * (isup + 1)],
                    in1=cx.sb["pbc"][0:64, :], op=ALU.mult),
                  reads=[f"tmp_o:{isup}", "pbc"], writes=[f"tmp_o:{isup}"])
        if odd == 1:
            E("sync", lambda e, pair=pair: e.dma_start(
                out=aof[pair][64:128, :], in_=tmp_o[0:64, :]),
              reads=[f"tmp_o:{u}" for u in range(4)],
              writes=[f"aof{pair}:odd"], dma=True)


def emit_l1_oproj_combine(cx, dr):
    p = cx.p
    E = p.emit
    aof = [cx.sb[f"aof{c}"] for c in range(CC)]
    owb = [cx.sb[f"owb{c}"] for c in range(CC)]
    xt2 = cx.sb["xt2"]
    res1k, scr1k = cx.sb["res1k"], cx.sb["scr1k"]
    scrA, scrB = cx.sb["scr512a"], cx.sb["scr512b"]
    for cc in range(CC):
        E("gpsimd", lambda e, cc=cc: e.dma_start(
            out=owb[cc][:, :], in_=dr["ow"][128 * cc:128 * (cc + 1), :]),
          writes=[f"owb{cc}"], dma=True)
    aof_keys = [f"aof{cc}:{u}" for cc in range(CC) for u in range(4)] + \
               [f"aof{cc}:odd" for cc in range(CC)]
    for nt in range(NT):
        b = nt % 2
        E("sync", lambda e, nt=nt, b=b: e.dma_start(
            out=xt2[:, 1024 * b:1024 * (b + 1)],
            in_=dr["x1"][128 * nt:128 * (nt + 1), :]),
          reads=[f"x1d:{nt}"], writes=[f"xt2:{b}"], dma=True)
        E("scalar", lambda e, nt=nt: e.dma_start(
            out=scr1k[:, :], in_=dr["acc1"][128 * nt:128 * (nt + 1), :]),
          reads=[f"acc1d:{nt}"], writes=["scr1k"], dma=True)
        for dh in range(2):
            ps = cx.sb["pA"] if dh == 0 else cx.sb["pB"]
            pk = "pA" if dh == 0 else "pB"
            mm_chain(cx, ps[:, :], pk,
                     [(aof[cc][:, 128 * nt:128 * (nt + 1)],
                       owb[cc][:, 512 * dh:512 * (dh + 1)],
                       [f"owb{cc}"] + aof_keys) for cc in range(CC)])
            E("scalar", lambda e, dh=dh: e.activation(
                out=scrA[:, :], in_=scr1k[:, 512 * dh:512 * (dh + 1)],
                func=AF.Silu),
              reads=["scr1k"], writes=["scr512a"])
            E("vector", lambda e, ps=ps, pk=pk: e.tensor_tensor(
                out=scrB[:, :], in0=ps[:, :], in1=scrA[:, :], op=ALU.mult),
              reads=[pk, "scr512a"], writes=["scr512b"])
            E("vector", lambda e, b=b, dh=dh: e.tensor_tensor(
                out=scrB[:, :], in0=scrB[:, :],
                in1=xt2[:, 1024 * b + 512 * dh:1024 * b + 512 * (dh + 1)],
                op=ALU.add),
              reads=["scr512b", f"xt2:{b}"], writes=["scr512b"])
            E("sync", lambda e, nt=nt, dh=dh: e.dma_start(
                out=dr["y"][128 * nt:128 * (nt + 1), 512 * dh:512 * (dh + 1)],
                in_=scrB[:, :]),
              reads=["scr512b"], writes=[f"yd:{nt}:{dh}"], dma=True)



KSTOP = os.environ.get("KSTOP", "")


def build_program(layers=(0, 1), dbg=(), scratch_out=False):
    nc = bass.Bass("TRN2", target_bir_lowering=False)
    dr = {}

    def din(name, shape):
        dr[name] = nc.dram_tensor(name, list(shape), F32, kind="ExternalInput")

    def dscr(name, shape, out=False):
        dr[name] = nc.dram_tensor(
            name, list(shape), F32,
            kind="ExternalOutput" if (out or scratch_out) else "Internal")

    both = len(layers) == 2
    if 0 in layers:
        din("x", [N, C])
        din("Wq", [C, C]); din("bq", [C]); din("Wk", [C, C]); din("bk", [C])
        din("Wv", [C, C]); din("bv", [C])
        din("gamma", [C]); din("beta", [C])
        din("gate_w", [C, NE])
        din("Wg", [NE, C, NI]); din("Wu", [NE, C, NI]); din("Wd", [NE, NI, C])
        dscr("s0", [N, C])
        dr["xnt"] = nc.dram_tensor("xnt", [N, C], BF16, kind="Internal")
        for i in range(2):
            dr[f"xg{i}"] = nc.dram_tensor(f"xg{i}", [1040, 1040], BF16,
                                          kind="Internal")
        dr["g1d"] = nc.dram_tensor("g1d", [N, C], F32, kind="Internal")
        dscr("x1", [N, C], out=not both)
    if 1 in layers:
        if 0 not in layers:
            din("x1", [N, C])
        din("qw", [C, C]); din("kw", [C, C]); din("vw", [C, C])
        din("ow", [C, C])
        din("tsig", [8, C])
        din("W1", [8, C, C]); din("b1", [8, C])
        dscr("acc1", [N, C])
        dscr("y", [N, C], out=True)

    p = Prog()
    with contextlib.ExitStack() as stack:
        ec = stack.enter_context
        cx = Ctx(nc, p, ec)
        cx.dbg_names = set(dbg)
        cx.psum("ptr", (128, 2048), BF16)
        cx.psum("pbc", (128, 512), F32)
        cx.psum("pA", (128, 512), F32)
        cx.psum("pB", (128, 512), F32)
        cx.sbuf("xt2", [128, 2048], F32)
        cx.sbuf("scr1k", [128, 1024], F32)
        cx.sbuf("res1k", [128, 1024], F32)
        cx.sbuf("scr512a", [128, 512], F32)
        cx.sbuf("scr512b", [128, 512], F32)
        cx.sbuf("scr512c", [128, 512], F32)
        cx.sbuf("scr512d", [128, 512], F32)
        cx.sbuf("st8", [128, 8], F32)
        cx.sbuf("st8b", [128, 8], F32)
        cx.sbuf("mst", [128, 16], F32)
        cx.sbuf("rst", [128, 16], F32)
        cx.sbuf("mrst", [128, 16], F32)
        emit_consts(cx)

        if 0 in layers:
            cx.scope()
            cx.sbuf("aff_t", [128, 128], F32)
            cx.sbuf("aff_pack", [128, 128], F32)
            cx.sbuf("w_t", [128, 128], F32)
            cx.sbuf("th", [8, 4], F32)
            cx.sbuf("s8", [8, 128], F32)
            cx.sbuf("s8t", [128, 8], F32)
            cx.sbuf("thb", [128, 1], F32)
            cx.sbuf("cnt_col", [128, 1], F32)
            cx.sbuf("ind_sb", [128, 128], BF16)
            cx.sbuf("cgt", [8, 1], F32)
            cx.sbuf("iby16", [128, 2], F32)
            cx.sbuf("iby16i", [128, 1], I32)
            cx.sbuf("iot8", [1, 8], F32)
            cx.sbuf("i8b", [128, 8], F32)
            cx.sbuf("hi_row", [1, 8], F32)
            cx.sbuf("thb_t", [128, 8], F32)
            cx.sbuf("kmrow", [1, 1024], F32)
            cx.sbuf("kmean_b", [128, 1024], BF16)
            cx.sbuf("kv_bd", [128, 1024], BF16)
            cx.scope()
            for c in range(CC):
                cx.sbuf(f"l0xnf{c}", [128, 2048], BF16)
            cx.scope()
            cx.sbuf("l0xnb", [128, 3072], BF16)
            for c in range(CC):
                cx.sbuf(f"l0xnlo{c}", [128, 2048], BF16)
            cx.sbuf("aff_T", [8, 2048], F32)
            cx.sbuf("gate32", [128, 64], F32)
            cx.sbuf("ghi", [128, 64], BF16)
            cx.sbuf("glo", [128, 64], BF16)
            if KSTOP != "consts":
                emit_ln(cx, dr["x"], "l0", with_lo=True,
                        x_dram_xnt=dr["xnt"])
            if KSTOP not in ("consts", "ln0"):
                pump_raw = emit_l0_gate_and_bisect(cx, dr)
            else:
                pump_raw = lambda: None
            cx.close_scope()
            cnt = [0]

            def pump():
                if cnt[0] < 40:
                    cnt[0] += 1
                    pump_raw()

            if KSTOP not in ("ln", "ln0", "consts"):
                emit_layer0(cx, dr, pump)
            cx.dbg_dump("w_t", cx.sb["w_t"][:, :], (128, 128))
            cx.dbg_dump("th", cx.sb["th"][0:8, 0:4], (8, 4))
            cx.close_scope()

        if 1 in layers:
            cx.scope()
            for c in range(CC):
                cx.sbuf(f"l1xnf{c}", [128, 2048], BF16)
            cx.sbuf("l1xnb", [128, 3072], BF16)
            cx.sbuf("o_all", [128, 128], F32)
            cx.sbuf("o_T", [8, 2048], BF16)
            cx.scope()
            for c in range(CC):
                cx.sbuf(f"l1xnlo{c}", [128, 2048], BF16)
            cx.sbuf("tsgb", [8, 1024], BF16)
            cx.sbuf("sgn_f", [128, 64], BF16)
            emit_ln(cx, dr["x1"], "l1", with_lo=True)
            emit_l1_routing(cx, dr)
            cx.close_scope()
            cx.scope()
            cx.sbuf("w1b", [128, 16384], BF16)
            cx.sbuf("b1sb", [8, 1024], BF16)
            cx.sbuf("acc1sb", [128, 16384], F32)
            emit_l1_ffn(cx, dr)
            cx.close_scope()
            for c in range(CC):
                cx.sbuf(f"q_f{c}", [128, 2048], BF16, outer=True)
                cx.sbuf(f"k_f{c}", [128, 2048], BF16, outer=True)
            cx.sbuf("vt1", [128, NT * 1040], BF16, outer=True)
            emit_l1_attention(cx, dr)
            cx.close_scope()
            cx.scope()
            cx.psum("pS", (128, 512), F32)
            cx.psum("pS2", (128, 512), F32)
            cx.psum("pO", (128, 512), F32)
            for c in range(CC):
                cx.sbuf(f"aof{c}", [128, 2048], BF16)
            cx.sbuf("tmp_o", [64, 2048], BF16)
            cx.sbuf("ptb", [128, 2048], BF16)
            cx.sbuf("bcol", [128, 16], F32)
            cx.sbuf("zrow", [128, 2048], F32)
            cx.sbuf("zrow0", [1, 2048], F32)
            emit_l1_attention_core(cx, dr)
            for c in range(CC):
                cx.sbuf(f"owb{c}", [128, 1024], BF16)
            emit_l1_oproj_combine(cx, dr)
            cx.close_scope()

        p.final_wait_all("sync")
        sems = {s: ec(nc.semaphore(s)) for s in p.sem_names}
        block = ec(nc.Block())
        p.build_into(block, sems)
    return nc, dr


_NC_CACHE = {}


def _get_program(key, layers, dbg=(), scratch_out=False):
    if key not in _NC_CACHE:
        _NC_CACHE[key] = build_program(layers, dbg, scratch_out)
    return _NC_CACHE[key]


def make_in_map(inputs, b, layers=(0, 1)):
    m = {}
    if 0 in layers:
        m.update({
            "x": np.asarray(inputs["L0_attn_x"])[b],
            "Wq": inputs["L0_attn_Wq"], "bq": inputs["L0_attn_bq"],
            "Wk": inputs["L0_attn_Wk"], "bk": inputs["L0_attn_bk"],
            "Wv": inputs["L0_attn_Wv"], "bv": inputs["L0_attn_bv"],
            "gamma": inputs["L0_attn_gamma"], "beta": inputs["L0_attn_beta"],
            "gate_w": inputs["L0_ffn_gate_w"],
            "Wg": inputs["L0_ffn_Wg"], "Wu": inputs["L0_ffn_Wu"],
            "Wd": inputs["L0_ffn_Wd"],
        })
    if 1 in layers:
        if 0 not in layers:
            m["x1"] = inputs["x1"][b]
        m.update({
            "qw": inputs["L1_attn_q_w"], "kw": inputs["L1_attn_k_w"],
            "vw": inputs["L1_attn_v_w"], "ow": inputs["L1_attn_o_w"],
            "tsig": inputs["L1_ffn_tile_sigs"],
            "W1": inputs["L1_ffn_W"], "b1": inputs["L1_ffn_b"],
        })
    return {k: np.ascontiguousarray(np.asarray(v), dtype=np.float32)
            for k, v in m.items()}


def kernel(**inputs):
    nc, _ = _get_program("fused", (0, 1))
    B = 8
    in_maps = [make_in_map(inputs, b) for b in range(B)]
    res = run_bass_kernel_spmd(nc, in_maps, core_ids=list(range(B)))
    out = np.stack([res.results[b]["y"] for b in range(B)], axis=0)
    return out.astype(np.float32)


# version 16 (final)
# speedup vs baseline: 1.2396x; 1.2092x over previous
import contextlib
import math
import os
import numpy as np
import ml_dtypes
import concourse.bass as bass
import concourse.mybir as mybir
from concourse.bass_utils import run_bass_kernel_spmd
from concourse.bass import IndirectOffsetOnAxis

F32 = mybir.dt.float32
BF16 = mybir.dt.bfloat16
I32 = mybir.dt.int32
AF = mybir.ActivationFunctionType
ALU = mybir.AluOpType
AX = mybir.AxisListType

N = 2048
C = 1024
NH = 16
HD = 64
NE = 8
NI = 2048
KSEL = 1024
NT = N // 128
CC = C // 128
TWO_PI = 2.0 * math.pi
TWO_PI_HI = float(np.float32(TWO_PI))
TWO_PI_LO = float(np.float64(TWO_PI) - np.float64(np.float32(TWO_PI)))
INV_2PI = 1.0 / TWO_PI
HALF_PI = math.pi / 2.0
SLOPES = [2.0 ** (-0.5 * (i + 1)) for i in range(NH)]
SCL2 = 1.0 / 2048.0
LN1E4 = math.log(10000.0)

ENGINES = ("tensor", "vector", "scalar", "gpsimd", "sync")
DMA_QUEUES = ("sync", "scalar", "gpsimd")


class Prog:
    DMA_POOL = 8

    def __init__(self):
        self.queues = {e: [] for e in ENGINES}
        self.count = {}
        self.sem_names = []
        for e in ENGINES:
            self._add_sem(e)
        for e in DMA_QUEUES:
            for i in range(self.DMA_POOL):
                self._add_sem(f"{e}_dma{i}")
        self.dma_rr = {e: 0 for e in DMA_QUEUES}
        self.waited = {e: {} for e in ENGINES}
        self.writer = {}
        self.readers = {}

    def _add_sem(self, name):
        self.count[name] = 0
        self.sem_names.append(name)

    def _dep(self, deps, sem, cnt):
        if cnt > deps.get(sem, 0):
            deps[sem] = cnt

    def emit(self, eng, fn, reads=(), writes=(), dma=False):
        if dma:
            i = self.dma_rr[eng]
            self.dma_rr[eng] = (i + 1) % self.DMA_POOL
            sem = f"{eng}_dma{i}"
            inc = 16
        else:
            sem = eng
            inc = 1
        deps = {}
        if dma and self.count[sem] > 0:
            self._dep(deps, sem, self.count[sem])
        for k in reads:
            w = self.writer.get(k)
            if w is not None:
                self._dep(deps, *w)
        for k in writes:
            w = self.writer.get(k)
            if w is not None:
                self._dep(deps, *w)
            for r in self.readers.get(k, ()):
                self._dep(deps, *r)
        waits = []
        wt = self.waited[eng]
        for s, c in deps.items():
            if s == "tensor" and eng == "tensor":
                continue
            if wt.get(s, 0) >= c:
                continue
            waits.append((s, c))
            wt[s] = c
        self.count[sem] += inc
        cnt_after = self.count[sem]
        self.queues[eng].append((waits, fn, sem, inc))
        for k in writes:
            self.writer[k] = (sem, cnt_after)
            self.readers[k] = []
        for k in reads:
            self.readers.setdefault(k, []).append((sem, cnt_after))
        return (sem, cnt_after)

    def barrier_wait(self, eng, sems_counts):
        wt = self.waited[eng]
        waits = []
        for s, c in sems_counts.items():
            if wt.get(s, 0) >= c:
                continue
            waits.append((s, c))
            wt[s] = c
        if waits:
            self.queues[eng].append((waits, None, None, 0))

    def full_barrier(self):
        snap = {s: c for s, c in self.count.items() if c > 0}
        for e in ENGINES:
            self.barrier_wait(e, snap)

    def final_wait_all(self, eng="sync"):
        self.barrier_wait(eng, {s: c for s, c in self.count.items() if c > 0})

    def build_into(self, block, sems):
        def run_queue(engine_obj, items):
            for waits, fn, sem, inc in items:
                for (s, c) in waits:
                    engine_obj.wait_ge(sems[s], c)
                if fn is not None:
                    fn(engine_obj).then_inc(sems[sem], inc)

        @block.sync
        def _(eng):
            run_queue(eng, self.queues["sync"])

        @block.vector
        def _(eng):
            run_queue(eng, self.queues["vector"])

        @block.scalar
        def _(eng):
            run_queue(eng, self.queues["scalar"])

        @block.tensor
        def _(eng):
            run_queue(eng, self.queues["tensor"])

        @block.gpsimd
        def _(eng):
            run_queue(eng, self.queues["gpsimd"])


class Ctx:
    def __init__(self, nc, p, ec):
        self.nc = nc
        self.p = p
        self.ec = ec
        self.ec0 = ec
        self.sb = {}
        self.dbg_names = set()
        self.scopes = []

    def sbuf(self, name, shape, dtype, outer=False):
        ec = self.ec0 if outer else self.ec
        kw = {"side": "right"} if outer else {}
        t = ec(self.nc.sbuf_tensor(name, list(shape), dtype,
                                   allow_name_mangling=True, **kw))
        self.sb[name] = t
        return t

    def psum(self, name, shape=(128, 512), dtype=F32):
        t = self.ec(self.nc.psum_tensor(name, list(shape), dtype,
                                        allow_name_mangling=True))
        self.sb[name] = t
        return t

    def scope(self):
        st = contextlib.ExitStack()
        self.scopes.append((st, self.ec))
        self.ec = st.enter_context
        return st

    def close_scope(self):
        st, outer_ec = self.scopes.pop()
        self.p.full_barrier()
        st.close()
        self.ec = outer_ec

    def dbg_dump(self, name, ap, shape, dtype=F32):
        if name not in self.dbg_names:
            return
        t = self.nc.dram_tensor("dbg_" + name, list(shape), dtype,
                                kind="ExternalOutput")
        self.p.full_barrier()
        self.p.emit("sync", lambda e: e.dma_start(out=t[:, :] if len(shape) == 2 else t[:], in_=ap),
                    reads=[], writes=["dbg_" + name], dma=True)





def transpose_via_ptr(cx, items, identity="identb"):
    p = cx.p
    ptr = cx.sb["ptr"]
    flip = getattr(cx, "_ptr_flip", 0)
    i = 0
    while i < len(items):
        grp = items[i:i + 4]
        bank = flip % 2
        bkey = "ptrA" if bank == 0 else "ptrB"
        for s, (in_ap, in_keys, _, _) in enumerate(grp):
            off = 1024 * bank + 128 * s
            p.emit("tensor", lambda e, in_ap=in_ap, off=off: e.transpose(
                ptr[:, off:off + 128], in_ap, cx.sb[identity][:, :]),
                   reads=list(in_keys) + [identity], writes=[bkey])
        for s, (_, _, evac_fn, ew) in enumerate(grp):
            off = 1024 * bank + 128 * s
            p.emit("scalar", lambda e, evac_fn=evac_fn, off=off:
                   evac_fn(e, ptr[:, off:off + 128]),
                   reads=[bkey], writes=list(ew))
        flip += 1
        i += 4
    cx._ptr_flip = flip


def emit_consts(cx):
    nc, p = cx.nc, cx.p
    E = p.emit
    KC = int(os.environ.get("KCONST", "9"))
    cx.sbuf("identf", [128, 128], F32)
    cx.sbuf("identb", [128, 128], BF16)
    cx.sbuf("onesf_row", [1, 128], F32)
    cx.sbuf("onesb_row", [1, 128], BF16)
    cx.sbuf("onesb_col", [128, 1], BF16)
    cx.sbuf("iota_col", [128, 1], F32)
    cx.sbuf("halfpi", [128, 1], F32)
    cx.sbuf("pos_cols", [128, 16], F32)
    cx.sbuf("cmask", [128, 4 * 512], BF16)
    for nm in ("identf", "identb"):
        t = cx.sb[nm]
        E("gpsimd", lambda e, t=t: e.memset(t[:, :], 0.0), writes=[nm])
        E("gpsimd", lambda e, t=t: e.affine_select(
            out=t[:, :], in_=t[:, :], compare_op=ALU.not_equal, fill=1.0,
            base=0, pattern=[[-1, 128]], channel_multiplier=1),
          reads=[nm], writes=[nm])
    if KC < 2:
        return
    for nm, v in (("onesf_row", 1.0), ("onesb_row", 1.0), ("onesb_col", 1.0),
                  ("halfpi", HALF_PI)):
        t = cx.sb[nm]
        E("gpsimd", lambda e, t=t, v=v: e.memset(t[:, :], v), writes=[nm])
    if KC < 3:
        return
    E("gpsimd", lambda e: e.iota(cx.sb["iota_col"][:, :], pattern=[[0, 1]],
                                 base=0, channel_multiplier=1,
                                 allow_small_or_imprecise_dtypes=True),
      writes=["iota_col"])
    E("gpsimd", lambda e: e.iota(cx.sb["pos_cols"][:, :], pattern=[[128, 16]],
                                 base=0, channel_multiplier=1,
                                 allow_small_or_imprecise_dtypes=True),
      writes=["pos_cols"])
    if KC < 4:
        return
    E("gpsimd", lambda e: e.memset(cx.sb["cmask"][:, :], 1.0), writes=["cmask"])
    for k in range(4):
        E("gpsimd", lambda e, k=k: e.affine_select(
            out=cx.sb["cmask"][:, 512 * k:512 * (k + 1)],
            in_=cx.sb["cmask"][:, 512 * k:512 * (k + 1)],
            compare_op=ALU.is_ge, fill=0.0, base=-(128 * k),
            pattern=[[1, 512]], channel_multiplier=-1),
          reads=["cmask"], writes=["cmask"])


def bcast_row(cx, dst_name, dst_off, src_ap, width, read_keys, write_key=None):
    p = cx.p
    dst = cx.sb[dst_name]
    pb = cx.sb["pbc"]
    wk = write_key or dst_name
    for s in range(0, width, 512):
        w = min(512, width - s)
        p.emit("tensor", lambda e, s=s, w=w: e.matmul(
            pb[:, 0:w], cx.sb["onesf_row"][0:1, :],
            src_ap[0:1, s:s + w], start=True, stop=True),
               reads=list(read_keys) + ["onesf_row"], writes=["pbc"])
        p.emit("vector", lambda e, s=s, w=w: e.tensor_copy(
            out=dst[:, dst_off + s:dst_off + s + w], in_=pb[:, 0:w]),
               reads=["pbc"], writes=[wk])


def emit_ln_to_xnf(cx, x_dram, pfx, eps=1e-5):
    p = cx.p
    E = p.emit
    xt = cx.sb["xt2"]
    xnb = cx.sb["xnb2"]
    scr = cx.sb["scr1k"]
    st = cx.sb["st8"]
    mst, rst, mrst = cx.sb["mst"], cx.sb["rst"], cx.sb["mrst"]
    ptr = cx.sb["ptr"]
    xnf = [cx.sb[f"xnf{c}"] for c in range(CC)]
    mk, rk, mrk = pfx + "mst", pfx + "rst", pfx + "mrst"

    for nt in range(NT):
        b = nt % 2
        st = cx.sb["st8"] if b == 0 else cx.sb["st8b"]
        sp = "st8" if b == 0 else "st8b"
        xk = f"xt2:{b}"
        E("sync", lambda e, nt=nt, b=b: e.dma_start(
            out=xt[:, 1024 * b:1024 * (b + 1)],
            in_=x_dram[128 * nt:128 * (nt + 1), :]),
          writes=[xk], dma=True)
        xv = xt[:, 1024 * b:1024 * (b + 1)]
        E("vector", lambda e, xv=xv, st=st: e.reduce_sum(
            out=st[:, 0:1], in_=xv, axis=AX.X), reads=[xk],
          writes=[sp + ":s"])
        E("scalar", lambda e, xv=xv: e.activation(
            out=scr[:, :], in_=xv, func=AF.Square, accum_out=st[:, 1:2]),
          reads=[xk], writes=["scr1k", "st8:q"])
        E("vector", lambda e, nt=nt: e.tensor_scalar(
            out=mst[:, nt:nt + 1], in0=st[:, 0:1], scalar1=1.0 / 1024.0,
            scalar2=None, op0=ALU.mult),
          reads=["st8:s"], writes=[mk])
        E("scalar", lambda e, nt=nt: e.activation(
            out=st[:, 2:3], in_=mst[:, nt:nt + 1], func=AF.Square),
          reads=[mk], writes=["st8:m2"])
        E("vector", lambda e: e.tensor_scalar(
            out=st[:, 3:4], in0=st[:, 1:2], scalar1=1.0 / 1024.0,
            scalar2=float(eps), op0=ALU.mult, op1=ALU.add),
          reads=["st8:q"], writes=["st8:v"])
        E("vector", lambda e: e.tensor_tensor(
            out=st[:, 3:4], in0=st[:, 3:4], in1=st[:, 2:3], op=ALU.subtract),
          reads=["st8:v", "st8:m2"], writes=["st8:v"])
        E("scalar", lambda e: e.activation(
            out=st[:, 4:5], in_=st[:, 3:4], func=AF.Sqrt),
          reads=["st8:v"], writes=["st8:sd"])
        E("vector", lambda e, nt=nt: e.reciprocal(
            out=rst[:, nt:nt + 1], in_=st[:, 4:5]),
          reads=["st8:sd"], writes=[rk])
        E("vector", lambda e, nt=nt: e.tensor_tensor(
            out=mrst[:, nt:nt + 1], in0=mst[:, nt:nt + 1],
            in1=rst[:, nt:nt + 1], op=ALU.mult),
          reads=[mk, rk], writes=[mrk])
        E("vector", lambda e, xv=xv, nt=nt, b=b: e.tensor_scalar(
            out=xnb[:, 1024 * b:1024 * (b + 1)], in0=xv,
            scalar1=mst[:, nt:nt + 1], scalar2=rst[:, nt:nt + 1],
            op0=ALU.subtract, op1=ALU.mult),
          reads=[xk, mk, rk], writes=[f"xnb2:{b}"])
        for cc in range(CC):
            E("tensor", lambda e, cc=cc, b=b: e.transpose(
                ptr[:, 128 * (cc % 4):128 * (cc % 4 + 1)],
                xnb[:, 1024 * b + 128 * cc:1024 * b + 128 * (cc + 1)],
                cx.sb["identb"][:, :]),
              reads=[f"xnb2:{b}", "identb"], writes=[f"ptr:{cc % 4}"])
            if cc % 4 == 3:
                for c2 in range(cc - 3, cc + 1):
                    E("scalar", lambda e, c2=c2, nt=nt: e.activation(
                        out=xnf[c2][:, 128 * nt:128 * (nt + 1)],
                        in_=ptr[:, 128 * (c2 % 4):128 * (c2 % 4 + 1)],
                        func=AF.Copy),
                      reads=[f"ptr:{c2 % 4}"], writes=[f"xnf{c2}:{nt}"])


def mm_chain(cx, psum_ap, pk, pairs, extra_reads=()):
    p = cx.p
    n = len(pairs)
    for i, (lh, rh, rd) in enumerate(pairs):
        p.emit("tensor", lambda e, lh=lh, rh=rh, i=i: e.matmul(
            psum_ap, lh, rh, start=(i == 0), stop=(i == n - 1)),
               reads=list(rd) + list(extra_reads), writes=[pk])



def emit_ln(cx, x_dram, pfx, with_lo=False, eps=1e-5, x_dram_xnt=None):
    p = cx.p
    E = p.emit
    xt = cx.sb["xt2"]
    xnb = cx.sb[pfx + "xnb"]
    scr = cx.sb["scr1k"]
    res = cx.sb["res1k"]
    mst, rst, mrst = cx.sb["mst"], cx.sb["rst"], cx.sb["mrst"]
    ptr = cx.sb["ptr"]
    xnf = [cx.sb[f"{pfx}xnf{c}"] for c in range(CC)]
    xnlo = [cx.sb[f"{pfx}xnlo{c}"] for c in range(CC)] if with_lo else None
    mk, rk, mrk = pfx + "mst", pfx + "rst", pfx + "mrst"

    for nt in range(NT):
        b = nt % 2
        st = cx.sb["st8"] if b == 0 else cx.sb["st8b"]
        sp = "st8" if b == 0 else "st8b"
        xk = f"xt2:{b}"
        E("sync", lambda e, nt=nt, b=b: e.dma_start(
            out=xt[:, 1024 * b:1024 * (b + 1)],
            in_=x_dram[128 * nt:128 * (nt + 1), :]),
          writes=[xk], dma=True)
        xv = xt[:, 1024 * b:1024 * (b + 1)]
        E("vector", lambda e, xv=xv, st=st: e.reduce_sum(
            out=st[:, 0:1], in_=xv, axis=AX.X), reads=[xk],
          writes=[sp + ":s"])
        E("scalar", lambda e, xv=xv, st=st: e.activation(
            out=scr[:, :], in_=xv, func=AF.Square, accum_out=st[:, 1:2]),
          reads=[xk], writes=["scr1k", sp + ":q"])
        E("vector", lambda e, nt=nt, st=st: e.tensor_scalar(
            out=mst[:, nt:nt + 1], in0=st[:, 0:1], scalar1=1.0 / 1024.0,
            scalar2=None, op0=ALU.mult),
          reads=[sp + ":s"], writes=[mk])
        E("scalar", lambda e, nt=nt, st=st: e.activation(
            out=st[:, 2:3], in_=mst[:, nt:nt + 1], func=AF.Square),
          reads=[mk], writes=[sp + ":m2"])
        E("vector", lambda e, st=st: e.tensor_scalar(
            out=st[:, 3:4], in0=st[:, 1:2], scalar1=1.0 / 1024.0,
            scalar2=float(eps), op0=ALU.mult, op1=ALU.add),
          reads=[sp + ":q"], writes=[sp + ":v"])
        E("vector", lambda e, st=st: e.tensor_tensor(
            out=st[:, 3:4], in0=st[:, 3:4], in1=st[:, 2:3], op=ALU.subtract),
          reads=[sp + ":v", sp + ":m2"], writes=[sp + ":v"])
        E("scalar", lambda e, st=st: e.activation(
            out=st[:, 4:5], in_=st[:, 3:4], func=AF.Sqrt),
          reads=[sp + ":v"], writes=[sp + ":sd"])
        E("vector", lambda e, nt=nt, st=st: e.reciprocal(
            out=rst[:, nt:nt + 1], in_=st[:, 4:5]),
          reads=[sp + ":sd"], writes=[rk])
        E("vector", lambda e, nt=nt: e.tensor_tensor(
            out=mrst[:, nt:nt + 1], in0=mst[:, nt:nt + 1],
            in1=rst[:, nt:nt + 1], op=ALU.mult),
          reads=[mk, rk], writes=[mrk])
        store_xnt = (pfx == "l0")
        if with_lo:
            E("vector", lambda e, xv=xv, nt=nt: e.tensor_scalar(
                out=res[:, :], in0=xv, scalar1=mst[:, nt:nt + 1],
                scalar2=rst[:, nt:nt + 1], op0=ALU.subtract, op1=ALU.mult),
              reads=[xk, mk, rk], writes=["res1k"])
            E("vector", lambda e, b=b: e.tensor_copy(
                out=xnb[:, 1024 * b:1024 * (b + 1)], in_=res[:, :]),
              reads=["res1k"], writes=[f"{pfx}xnb:{b}"])
            E("vector", lambda e, b=b: e.tensor_tensor(
                out=xnb[:, 2048:3072],
                in0=res[:, :], in1=xnb[:, 1024 * b:1024 * (b + 1)],
                op=ALU.subtract),
              reads=["res1k", f"{pfx}xnb:{b}"], writes=[f"{pfx}xnlo:x"])
            if store_xnt:
                E("scalar", lambda e, nt=nt, b=b: e.dma_start(
                    out=x_dram_xnt[128 * nt:128 * (nt + 1), :],
                    in_=xnb[:, 1024 * b:1024 * (b + 1)]),
                  reads=[f"{pfx}xnb:{b}"], writes=[f"xntd:{nt}"], dma=True)
        else:
            E("vector", lambda e, xv=xv, nt=nt, b=b: e.tensor_scalar(
                out=xnb[:, 1024 * b:1024 * (b + 1)], in0=xv,
                scalar1=mst[:, nt:nt + 1], scalar2=rst[:, nt:nt + 1],
                op0=ALU.subtract, op1=ALU.mult),
              reads=[xk, mk, rk], writes=[f"{pfx}xnb:{b}"])
        items = []
        for cc in range(CC):
            def mk_evac(c2, nt):
                def f(e, ps):
                    return e.activation(
                        out=xnf[c2][:, 128 * nt:128 * (nt + 1)], in_=ps,
                        func=AF.Copy)
                return f
            items.append((
                xnb[:, 1024 * b + 128 * cc:1024 * b + 128 * (cc + 1)],
                [f"{pfx}xnb:{b}"], mk_evac(cc, nt), [f"{pfx}xnf{cc}:{nt}"]))
        transpose_via_ptr(cx, items)
        if with_lo:
            items = []
            for cc in range(CC):
                def mk_evac2(c2, nt):
                    def f(e, ps):
                        return e.activation(
                            out=xnlo[c2][:, 128 * nt:128 * (nt + 1)], in_=ps,
                            func=AF.Copy)
                    return f
                items.append((
                    xnb[:, 2048 + 128 * cc:2048 + 128 * (cc + 1)],
                    [f"{pfx}xnlo:x"], mk_evac2(cc, nt),
                    [f"{pfx}xnlo{cc}:{nt}"]))
            transpose_via_ptr(cx, items)



def emit_l0_gate_and_bisect(cx, dr):
    p = cx.p
    E = p.emit
    xnf = [cx.sb[f"l0xnf{c}"] for c in range(CC)]
    xnlo = [cx.sb[f"l0xnlo{c}"] for c in range(CC)]
    gate32 = cx.sb["gate32"]
    ghi, glo = cx.sb["ghi"], cx.sb["glo"]
    aff_t = cx.sb["aff_t"]
    aff_T = cx.sb["aff_T"]
    aff_pack = cx.sb["aff_pack"]
    pA, pB = cx.sb["pA"], cx.sb["pB"]
    st = cx.sb["st8"]
    scr = cx.sb["scr1k"]
    E("sync", lambda e: e.dma_start(
        out=gate32[:, :].rearrange("p (c e) -> p c e", e=8),
        in_=dr["gate_w"][:, :].rearrange("(c p) e -> p c e", p=128)),
      writes=["gate32"], dma=True)
    E("vector", lambda e: e.tensor_copy(out=ghi[:, :], in_=gate32[:, :]),
      reads=["gate32"], writes=["ghi"])
    E("vector", lambda e: e.tensor_tensor(
        out=glo[:, :], in0=gate32[:, :], in1=ghi[:, :], op=ALU.subtract),
      reads=["gate32", "ghi"], writes=["glo"])
    for nt in range(NT):
        pairs = []
        for cc in range(CC):
            pairs.append((xnf[cc][:, 128 * nt:128 * (nt + 1)],
                          ghi[:, 8 * cc:8 * (cc + 1)],
                          [f"l0xnf{cc}:{nt}", "ghi"]))
        for cc in range(CC):
            pairs.append((xnf[cc][:, 128 * nt:128 * (nt + 1)],
                          glo[:, 8 * cc:8 * (cc + 1)],
                          [f"l0xnf{cc}:{nt}", "glo"]))
        for cc in range(CC):
            pairs.append((xnlo[cc][:, 128 * nt:128 * (nt + 1)],
                          ghi[:, 8 * cc:8 * (cc + 1)],
                          [f"l0xnlo{cc}:{nt}", "ghi"]))
        mm_chain(cx, pA[:, 0:8], "pA", pairs)
        E("vector", lambda e: e.reduce_max(
            out=st[:, 0:1], in_=pA[:, 0:8], axis=AX.X, negate=True),
          reads=["pA"], writes=["st8:nm"])
        E("scalar", lambda e: e.activation(
            out=scr[:, 0:8], in_=pA[:, 0:8], func=AF.Exp,
            bias=st[:, 0:1], accum_out=st[:, 1:2]),
          reads=["pA", "st8:nm"], writes=["scr1k", "st8:es"])
        E("vector", lambda e: e.reciprocal(out=st[:, 2:3], in_=st[:, 1:2]),
          reads=["st8:es"], writes=["st8:rs"])
        E("vector", lambda e, nt=nt: e.tensor_scalar(
            out=aff_t[:, 8 * nt:8 * (nt + 1)], in0=scr[:, 0:8],
            scalar1=st[:, 2:3], scalar2=None, op0=ALU.mult),
          reads=["scr1k", "st8:rs"], writes=[f"aff_t:{nt}"])
        E("tensor", lambda e, nt=nt: e.transpose(
            pB[0:8, 0:128], aff_t[:, 8 * nt:8 * (nt + 1)],
            cx.sb["identf"][:, :]),
          reads=[f"aff_t:{nt}", "identf"], writes=["pB"])
        E("vector", lambda e, nt=nt: e.tensor_copy(
            out=aff_T[0:8, 128 * nt:128 * (nt + 1)], in_=pB[0:8, 0:128]),
          reads=["pB"], writes=[f"aff_T:{nt}"])
    E("sync", lambda e: e.dma_start(
        out=aff_pack[:, :],
        in_=aff_T[:, :].rearrange("e (t n) -> e t n", n=128)),
      reads=[f"aff_T:{nt}" for nt in range(NT)], writes=["aff_pack"], dma=True)

    th = cx.sb["th"]
    s8, s8t = cx.sb["s8"], cx.sb["s8t"]
    thb, cnt_col = cx.sb["thb"], cx.sb["cnt_col"]
    ind_sb, cgt = cx.sb["ind_sb"], cx.sb["cgt"]
    iby = cx.sb["iby16"]
    ibyi = cx.sb["iby16i"]
    iot8, i8b = cx.sb["iot8"], cx.sb["i8b"]
    E("gpsimd", lambda e: e.memset(s8t[:, :], 1.0), writes=["s8t"])
    E("gpsimd", lambda e: e.affine_select(
        out=s8t[:, :], in_=s8t[:, :], compare_op=ALU.is_ge, fill=0.0,
        base=0, pattern=[[-16, 8]], channel_multiplier=1),
      reads=["s8t"], writes=["s8t"])
    E("gpsimd", lambda e: e.affine_select(
        out=s8t[:, :], in_=s8t[:, :], compare_op=ALU.is_ge, fill=0.0,
        base=15, pattern=[[16, 8]], channel_multiplier=-1),
      reads=["s8t"], writes=["s8t"])
    E("tensor", lambda e: e.transpose(
        cx.sb["pbc"][0:8, 0:128], s8t[:, :], cx.sb["identf"][:, :]),
      reads=["s8t", "identf"], writes=["pbc"])
    E("vector", lambda e: e.tensor_copy(out=s8[0:8, :],
                                        in_=cx.sb["pbc"][0:8, 0:128]),
      reads=["pbc"], writes=["s8"])
    E("gpsimd", lambda e: e.memset(th[0:8, 0:1], 0.0), writes=["th:lo"])
    E("gpsimd", lambda e: e.memset(th[0:8, 1:2], 1.0), writes=["th:hi"])
    E("gpsimd", lambda e: e.memset(th[0:8, 2:3], 0.5), writes=["th:mid"])

    def bisect_iter():
        pbc = cx.sb["pbc"]
        E("tensor", lambda e: e.matmul(
            pbc[0:128, 0:1], s8[0:8, :], th[0:8, 2:3], start=True, stop=True),
          reads=["s8", "th:mid"], writes=["pbc"])
        E("vector", lambda e: e.tensor_copy(out=thb[:, 0:1],
                                            in_=pbc[0:128, 0:1]),
          reads=["pbc"], writes=["thb"])
        E("vector", lambda e: e.tensor_scalar(
            out=ind_sb[:, :], in0=cx.sb["aff_pack"][:, :],
            scalar1=thb[:, 0:1], scalar2=0.0, op0=ALU.is_ge, op1=ALU.add,
            accum_out=cnt_col[:, 0:1]),
          reads=["aff_pack", "thb"], writes=["ind_sb", "cnt_col"])
        E("tensor", lambda e: e.matmul(
            pbc[0:8, 0:1], s8t[:, :], cnt_col[:, 0:1], start=True, stop=True),
          reads=["s8t", "cnt_col"], writes=["pbc"])
        E("vector", lambda e: e.tensor_scalar(
            out=cgt[0:8, 0:1], in0=pbc[0:8, 0:1], scalar1=float(KSEL),
            scalar2=None, op0=ALU.is_gt),
          reads=["pbc"], writes=["cgt"])
        E("vector", lambda e: e.tensor_tensor(
            out=th[0:8, 3:4], in0=th[0:8, 2:3], in1=th[0:8, 0:1],
            op=ALU.subtract),
          reads=["th:mid", "th:lo"], writes=["th:s"])
        E("vector", lambda e: e.tensor_tensor(
            out=th[0:8, 3:4], in0=th[0:8, 3:4], in1=cgt[0:8, 0:1],
            op=ALU.mult),
          reads=["th:s", "cgt"], writes=["th:s"])
        E("vector", lambda e: e.tensor_tensor(
            out=th[0:8, 0:1], in0=th[0:8, 0:1], in1=th[0:8, 3:4], op=ALU.add),
          reads=["th:lo", "th:s"], writes=["th:lo"])
        E("vector", lambda e: e.tensor_tensor(
            out=th[0:8, 3:4], in0=th[0:8, 2:3], in1=th[0:8, 1:2],
            op=ALU.subtract),
          reads=["th:mid", "th:hi"], writes=["th:s"])
        E("vector", lambda e: e.tensor_tensor(
            out=th[0:8, 3:4], in0=th[0:8, 3:4], in1=cgt[0:8, 0:1],
            op=ALU.mult),
          reads=["th:s", "cgt"], writes=["th:s"])
        E("vector", lambda e: e.tensor_tensor(
            out=th[0:8, 1:2], in0=th[0:8, 2:3], in1=th[0:8, 3:4],
            op=ALU.subtract),
          reads=["th:mid", "th:s"], writes=["th:hi"])
        E("vector", lambda e: e.tensor_tensor(
            out=th[0:8, 3:4], in0=th[0:8, 0:1], in1=th[0:8, 1:2], op=ALU.add),
          reads=["th:lo", "th:hi"], writes=["th:s"])
        E("vector", lambda e: e.tensor_scalar(
            out=th[0:8, 2:3], in0=th[0:8, 3:4], scalar1=0.5, scalar2=None,
            op0=ALU.mult),
          reads=["th:s"], writes=["th:mid"])

    return bisect_iter


def emit_l0_gating_final(cx):
    p = cx.p
    E = p.emit
    th = cx.sb["th"]
    hi_row = cx.sb["hi_row"]
    thb_t = cx.sb["thb_t"]
    w_t = cx.sb["w_t"]
    aff_t = cx.sb["aff_t"]
    scr = cx.sb["scr1k"]
    E("tensor", lambda e: e.transpose(
        cx.sb["pbc"][0:1, 0:8], th[0:8, 1:2], cx.sb["identf"][0:8, 0:8]),
      reads=["th:hi", "identf"], writes=["pbc"])
    E("vector", lambda e: e.tensor_copy(out=hi_row[0:1, 0:8],
                                        in_=cx.sb["pbc"][0:1, 0:8]),
      reads=["pbc"], writes=["hi_row"])
    bcast_row(cx, "thb_t", 0, hi_row[0:1, 0:8], 8, ["hi_row"])
    for nt in range(NT):
        E("vector", lambda e, nt=nt: e.tensor_tensor(
            out=scr[:, 0:8], in0=aff_t[:, 8 * nt:8 * (nt + 1)],
            in1=thb_t[:, 0:8], op=ALU.is_ge),
          reads=[f"aff_t:{nt}", "thb_t"], writes=["scr1k"])
        E("vector", lambda e, nt=nt: e.tensor_tensor(
            out=w_t[:, 8 * nt:8 * (nt + 1)], in0=scr[:, 0:8],
            in1=aff_t[:, 8 * nt:8 * (nt + 1)], op=ALU.mult),
          reads=["scr1k", f"aff_t:{nt}"], writes=[f"w_t:{nt}"])


def emit_rope_tables(cx, nt, ss=0):
    p = cx.p
    E = p.emit
    sx = "" if ss == 0 else "2"
    ivb = cx.sb["ivf_b"]
    tA, tB, kf = (cx.sb["scr512a" + sx], cx.sb["scr512b" + sx],
                  cx.sb["scr512c" + sx])
    ki = cx.sb["kint" + sx]
    pos = cx.sb["pos_cols"]
    E("vector", lambda e, nt=nt: e.tensor_scalar(
        out=tA[:, :], in0=ivb[:, :], scalar1=pos[:, nt:nt + 1], scalar2=None,
        op0=ALU.mult),
      reads=["ivf_b", "pos_cols"], writes=["scr512a" + sx])
    for ph, extra in ((0, None), (1, 0.25)):
        E("vector", lambda e, extra=extra: e.tensor_scalar(
            out=ki[:, :], in0=tA[:, :], scalar1=INV_2PI, scalar2=extra,
            op0=ALU.mult, op1=(ALU.add if extra is not None else ALU.bypass)),
          reads=["scr512a" + sx], writes=["kint" + sx])
        E("vector", lambda e: e.tensor_copy(out=kf[:, :], in_=ki[:, :]),
          reads=["kint" + sx], writes=["scr512c" + sx])
        E("vector", lambda e: e.scalar_tensor_tensor(
            out=tB[:, :], in0=kf[:, :], scalar=-TWO_PI_HI, in1=tA[:, :],
            op0=ALU.mult, op1=ALU.add),
          reads=["scr512c" + sx, "scr512a" + sx], writes=["scr512b" + sx])
        E("vector", lambda e: e.scalar_tensor_tensor(
            out=tB[:, :], in0=kf[:, :], scalar=-TWO_PI_LO, in1=tB[:, :],
            op0=ALU.mult, op1=ALU.add),
          reads=["scr512c" + sx, "scr512b" + sx], writes=["scr512b" + sx])
        thr = float(np.float32(math.pi if ph == 0 else math.pi / 2))
        E("vector", lambda e, thr=thr: e.tensor_scalar(
            out=kf[:, :], in0=tB[:, :], scalar1=thr,
            scalar2=None, op0=ALU.is_gt),
          reads=["scr512b" + sx], writes=["scr512c" + sx])
        E("vector", lambda e: e.scalar_tensor_tensor(
            out=tB[:, :], in0=kf[:, :], scalar=-TWO_PI_HI, in1=tB[:, :],
            op0=ALU.mult, op1=ALU.add),
          reads=["scr512c" + sx, "scr512b" + sx], writes=["scr512b" + sx])
        if ph == 0:
            E("scalar", lambda e, nt=nt: e.activation(
                out=cx.sb["sin_all"][:, 512 * nt:512 * (nt + 1)],
                in_=tB[:, :], func=AF.Sin),
              reads=["scr512b" + sx], writes=[f"sin_all:{nt}"])
        else:
            E("scalar", lambda e, nt=nt: e.activation(
                out=cx.sb["cos_all"][:, 512 * nt:512 * (nt + 1)],
                in_=tB[:, :], func=AF.Sin, bias=cx.sb["halfpi"][:, 0:1]),
              reads=["scr512b" + sx, "halfpi"], writes=[f"cos_all:{nt}"])


def emit_rope_apply(cx, x_ap, xkey, nt, ss=0):
    p = cx.p
    E = p.emit
    sx = "" if ss == 0 else "2"
    cos = cx.sb["cos_all"][:, 512 * nt:512 * (nt + 1)]
    sin = cx.sb["sin_all"][:, 512 * nt:512 * (nt + 1)]
    ck, sk = f"cos_all:{nt}", f"sin_all:{nt}"
    xe = x_ap.rearrange("p (i two) -> p i two", two=2)[:, :, 0]
    xo = x_ap.rearrange("p (i two) -> p i two", two=2)[:, :, 1]
    a, b, c, d = (cx.sb["scr512a" + sx], cx.sb["scr512b" + sx],
                  cx.sb["scr512c" + sx], cx.sb["scr512d" + sx])
    E("vector", lambda e: e.tensor_tensor(out=a[:, :], in0=xe, in1=cos,
                                          op=ALU.mult),
      reads=[xkey, ck], writes=["scr512a" + sx])
    E("vector", lambda e: e.tensor_tensor(out=b[:, :], in0=xo, in1=sin,
                                          op=ALU.mult),
      reads=[xkey, sk], writes=["scr512b" + sx])
    E("gpsimd", lambda e: e.tensor_tensor(out=c[:, :], in0=xo, in1=cos,
                                          op=ALU.mult),
      reads=[xkey, ck], writes=["scr512c" + sx])
    E("gpsimd", lambda e: e.tensor_tensor(out=d[:, :], in0=xe, in1=sin,
                                          op=ALU.mult),
      reads=[xkey, sk], writes=["scr512d" + sx])
    E("vector", lambda e: e.tensor_tensor(out=xe, in0=a[:, :], in1=b[:, :],
                                          op=ALU.subtract),
      reads=["scr512a" + sx, "scr512b" + sx], writes=[xkey])
    E("gpsimd", lambda e: e.tensor_tensor(out=xo, in0=c[:, :], in1=d[:, :],
                                          op=ALU.add),
      reads=["scr512c" + sx, "scr512d" + sx], writes=[xkey])


def load_wT(cx, w_dram, dst, dstname):
    p = cx.p
    E = p.emit
    wtmp = cx.sb["wtmp"]
    for jc in range(CC):
        b = jc % 2
        wk = f"wtmp:{b}"
        E("gpsimd", lambda e, jc=jc, b=b, w_dram=w_dram: e.dma_start(
            out=wtmp[:, 1024 * b:1024 * (b + 1)],
            in_=w_dram[128 * jc:128 * (jc + 1), :]),
          writes=[wk], dma=True)
        items = []
        for cc in range(CC):
            def mk_evac(c2, jc):
                def f(e, ps):
                    return e.activation(
                        out=dst[c2][:, 128 * jc:128 * (jc + 1)], in_=ps,
                        func=AF.Copy)
                return f
            items.append((
                wtmp[:, 1024 * b + 128 * cc:1024 * b + 128 * (cc + 1)],
                [wk], mk_evac(cc, jc), [f"{dstname}{cc}"]))
        transpose_via_ptr(cx, items)


def emit_layer0(cx, dr, pump):
    p = cx.p
    E = p.emit
    xnf = [cx.sb[f"l0xnf{c}"] for c in range(CC)]
    cx.scope()
    qt = cx.sbuf("qt", [128, NT * 1024], BF16)
    cx.sbuf("cos_all", [128, NT * 512], BF16)
    cx.sbuf("sin_all", [128, NT * 512], BF16)
    cx.sbuf("scr512a2", [128, 512], F32)
    cx.sbuf("scr512b2", [128, 512], F32)
    cx.sbuf("scr512c2", [128, 512], F32)
    cx.sbuf("scr512d2", [128, 512], F32)

    cx.scope()
    wtq = [cx.sbuf(f"wtq{c}", [128, 1024], BF16) for c in range(CC)]
    cx.sbuf("wtmp", [128, 2048], BF16)
    brow1 = cx.sbuf("brow1", [1, 1024], F32)
    bqb = cx.sbuf("bqb", [128, 1024], BF16)
    load_wT(cx, dr["Wq"], wtq, "wtq")
    E("sync", lambda e: e.dma_start(out=brow1[0:1, :], in_=dr["bq"][:]),
      writes=["brow1"], dma=True)
    bcast_row(cx, "bqb", 0, brow1[0:1, :], 1024, ["brow1"])
    scrA, scrB = cx.sb["scr512a"], cx.sb["scr512b"]
    for nt in range(NT):
        pump(); pump()
        for jh in range(2):
            ps = cx.sb["pA"] if jh == 0 else cx.sb["pB"]
            pk = "pA" if jh == 0 else "pB"
            mm_chain(cx, ps[:, :], pk,
                     [(xnf[cc][:, 128 * nt:128 * (nt + 1)],
                       wtq[cc][:, 512 * jh:512 * (jh + 1)],
                       [f"l0xnf{cc}:{nt}", f"wtq{cc}"]) for cc in range(CC)])
            E("vector", lambda e, ps=ps, pk=pk, jh=jh: e.tensor_tensor(
                out=scrA[:, :], in0=ps[:, :],
                in1=bqb[:, 512 * jh:512 * (jh + 1)], op=ALU.add),
              reads=[pk, "bqb"], writes=["scr512a"])
            E("scalar", lambda e: e.activation(
                out=scrB[:, :], in_=scrA[:, :], func=AF.Gelu),
              reads=["scr512a"], writes=["scr512b"])
            E("vector", lambda e, nt=nt, jh=jh: e.tensor_scalar(
                out=qt[:, 1024 * nt + 512 * jh:1024 * nt + 512 * (jh + 1)],
                in0=scrB[:, :], scalar1=0.21, scalar2=None, op0=ALU.add),
              reads=["scr512b"], writes=[f"qt:{nt}"])
    cx.close_scope()

    if KSTOP == "b1a":
        cx.close_scope()
        return
    cx.scope()
    cx.psum("pD", (1, 1024), F32)
    cx.psum("pkv", (128, 512), F32)
    wtk = [cx.sbuf(f"wtk{c}", [128, 1024], BF16) for c in range(CC)]
    wtv = [cx.sbuf(f"wtv{c}", [128, 1024], BF16) for c in range(CC)]
    cx.sbuf("wtmp", [128, 2048], BF16)
    brow2 = cx.sbuf("brow2", [1, 1024], F32)
    bkb = cx.sbuf("bkb", [128, 1024], BF16)
    bvb2 = cx.sbuf("bvb2", [128, 1024], F32)
    ktb2 = cx.sbuf("ktb", [128, 2048], BF16)
    vtb2 = cx.sbuf("vtb", [128, 2048], BF16)
    cx.sbuf("kint", [128, 512], I32)
    cx.sbuf("kint2", [128, 512], I32)
    ivr = cx.sbuf("ivf_row", [1, 512], F32)
    cx.sbuf("ivf_b", [128, 512], F32)
    load_wT(cx, dr["Wk"], wtk, "wtk")
    load_wT(cx, dr["Wv"], wtv, "wtv")
    E("sync", lambda e: e.dma_start(out=brow2[0:1, :], in_=dr["bk"][:]),
      writes=["brow2:k"], dma=True)
    bcast_row(cx, "bkb", 0, brow2[0:1, :], 1024, ["brow2:k"])
    E("sync", lambda e: e.dma_start(out=brow2[0:1, :], in_=dr["bv"][:]),
      reads=["bkb"], writes=["brow2:v"], dma=True)
    bcast_row(cx, "bvb2", 0, brow2[0:1, :], 1024, ["brow2:v"])
    E("vector", lambda e: e.tensor_scalar(
        out=cx.sb["bvb2"][:, :], in0=cx.sb["bvb2"][:, :], scalar1=SCL2,
        scalar2=None, op0=ALU.mult),
      reads=["bvb2"], writes=["bvb2"])
    E("gpsimd", lambda e: e.iota(ivr[:, :], pattern=[[1, 512]], base=0,
                                 channel_multiplier=0,
                                 allow_small_or_imprecise_dtypes=True),
      writes=["ivf_row"])
    E("scalar", lambda e: e.activation(
        out=ivr[:, :], in_=ivr[:, :], func=AF.Exp, scale=-LN1E4 / 512.0),
      reads=["ivf_row"], writes=["ivf_row"])
    bcast_row(cx, "ivf_b", 0, ivr[0:1, :], 512, ["ivf_row"])

    for nt in range(NT):
        kb = nt % 2
        ktb = ktb2[:, 1024 * kb:1024 * (kb + 1)]
        vtb = vtb2[:, 1024 * kb:1024 * (kb + 1)]
        kkey, vkey = f"ktb:{kb}", f"vtb:{kb}"
        pump(); pump()
        emit_rope_tables(cx, nt, ss=kb)
        for pi, (wt, wname) in enumerate(((wtk, "wtk"), (wtv, "wtv"))):
            for jh in range(2):
                ps = cx.sb["pA"] if jh == 0 else cx.sb["pB"]
                pk = "pA" if jh == 0 else "pB"
                mm_chain(cx, ps[:, :], pk,
                         [(xnf[cc][:, 128 * nt:128 * (nt + 1)],
                           wt[cc][:, 512 * jh:512 * (jh + 1)],
                           [f"l0xnf{cc}:{nt}", f"{wname}{cc}"])
                          for cc in range(CC)])
                if pi == 0:
                    E("vector", lambda e, ps=ps, pk=pk, jh=jh: e.tensor_tensor(
                        out=scrA[:, :], in0=ps[:, :],
                        in1=bkb[:, 512 * jh:512 * (jh + 1)], op=ALU.add),
                      reads=[pk, "bkb"], writes=["scr512a"])
                    E("scalar", lambda e: e.activation(
                        out=scrB[:, :], in_=scrA[:, :], func=AF.Gelu),
                      reads=["scr512a"], writes=["scr512b"])
                    E("vector", lambda e, jh=jh, ktb=ktb: e.tensor_scalar(
                        out=ktb[:, 512 * jh:512 * (jh + 1)],
                        in0=scrB[:, :], scalar1=0.21, scalar2=None,
                        op0=ALU.add),
                      reads=["scr512b"], writes=[kkey])
                else:
                    E("vector", lambda e, ps=ps, pk=pk, jh=jh, vtb=vtb:
                      e.scalar_tensor_tensor(
                          out=vtb[:, 512 * jh:512 * (jh + 1)], in0=ps[:, :],
                          scalar=SCL2, in1=bvb2[:, 512 * jh:512 * (jh + 1)],
                          op0=ALU.mult, op1=ALU.add),
                      reads=[pk, "bvb2"], writes=[vkey])
        for half in range(2):
            E("tensor", lambda e, nt=nt, half=half, ktb=ktb: e.matmul(
                cx.sb["pD"][0:1, 512 * half:512 * (half + 1)],
                cx.sb["onesb_col"][:, 0:1],
                ktb[:, 512 * half:512 * (half + 1)],
                start=(nt == 0), stop=(nt == NT - 1), skip_group_check=True),
              reads=["onesb_col", kkey], writes=["pD"])
        emit_rope_apply(cx, ktb[:, :], kkey, nt, ss=kb)
        for h in range(NH):
            pair, odd = h // 2, h % 2
            E("tensor", lambda e, nt=nt, h=h, pair=pair, odd=odd, ktb=ktb,
              vtb=vtb: e.matmul(
                cx.sb["pkv"][64 * odd:64 * (odd + 1),
                             64 * pair:64 * (pair + 1)],
                ktb[:, 64 * h:64 * (h + 1)], vtb[:, 64 * h:64 * (h + 1)],
                start=(nt == 0 and h <= 1),
                stop=(nt == NT - 1 and h >= NH - 2),
                skip_group_check=True, tile_position=(0, 64 * odd)),
              reads=[kkey, vkey], writes=["pkv"])
    kmrow = cx.sb["kmrow"]
    E("vector", lambda e: e.tensor_scalar(
        out=kmrow[0:1, :], in0=cx.sb["pD"][0:1, 0:1024], scalar1=1.0 / 2048.0,
        scalar2=None, op0=ALU.mult),
      reads=["pD"], writes=["kmrow"])
    bcast_row(cx, "kmean_b", 0, kmrow[0:1, 0:1024], 1024, ["kmrow"])
    kvbd = cx.sb["kv_bd"]
    E("gpsimd", lambda e: e.memset(kvbd[:, :], 0.0), writes=["kv_bd"])
    E("scalar", lambda e: e.activation(
        out=kvbd[0:64, :].rearrange("p (c x) -> p c x", x=128)[:, :, 0:64],
        in_=cx.sb["pkv"][0:64, :].rearrange("p (c x) -> p c x", x=64),
        func=AF.Copy),
      reads=["pkv", "kv_bd"], writes=["kv_bd"])
    E("scalar", lambda e: e.activation(
        out=kvbd[64:128, :].rearrange("p (c x) -> p c x", x=128)[:, :, 64:128],
        in_=cx.sb["pkv"][64:128, :].rearrange("p (c x) -> p c x", x=64),
        func=AF.Copy),
      reads=["pkv", "kv_bd"], writes=["kv_bd"])
    cx.close_scope()

    if KSTOP == "b1b":
        cx.close_scope()
        return
    cx.scope()
    cx.psum("pout", (128, 1024), F32)
    qrT = cx.sbuf("qrT", [128, 2048], BF16)
    z_all = cx.sbuf("z_all", [128, 256], F32)
    gbrow = cx.sbuf("gbrow", [1, 1024], F32)
    gammab = cx.sbuf("gammab", [128, 1024], F32)
    betab = cx.sbuf("betab", [128, 1024], F32)
    E("sync", lambda e: e.dma_start(out=gbrow[0:1, :], in_=dr["gamma"][:]),
      writes=["gbrow:g"], dma=True)
    bcast_row(cx, "gammab", 0, gbrow[0:1, :], 1024, ["gbrow:g"])
    E("sync", lambda e: e.dma_start(out=gbrow[0:1, :], in_=dr["beta"][:]),
      reads=["gammab"], writes=["gbrow:b"], dma=True)
    bcast_row(cx, "betab", 0, gbrow[0:1, :], 1024, ["gbrow:b"])
    kv_bd = cx.sb["kv_bd"]
    kmean_b = cx.sb["kmean_b"]
    xt2 = cx.sb["xt2"]
    st = cx.sb["st8"]
    resA, scrA1 = cx.sb["res1k"], cx.sb["scr1k"]
    resB = cx.sbuf("res1kB", [128, 1024], F32)
    scrB1 = cx.sbuf("scr1kB", [128, 1024], F32)
    zscr = cx.sbuf("zscr", [128, 1024], F32)
    zTs = cx.sbuf("zTs", [16, 128], F32)
    zbS = cx.sbuf("zbS", [128, 1024], F32)
    cx.sbuf("i16f", [16, 1024], F32)
    E("gpsimd", lambda e: e.memset(cx.sb["i16f"][:, :], 1.0), writes=["i16f"])
    E("gpsimd", lambda e: e.affine_select(
        out=cx.sb["i16f"][:, :], in_=cx.sb["i16f"][:, :],
        compare_op=ALU.is_ge, fill=0.0,
        base=0, pattern=[[1, 1024]], channel_multiplier=-64),
      reads=["i16f"], writes=["i16f"])
    E("gpsimd", lambda e: e.affine_select(
        out=cx.sb["i16f"][:, :], in_=cx.sb["i16f"][:, :],
        compare_op=ALU.is_ge, fill=0.0,
        base=63, pattern=[[-1, 1024]], channel_multiplier=64),
      reads=["i16f"], writes=["i16f"])
    def b2_stageA(nt):
        b = nt % 2
        scr = scrA1 if b == 0 else scrB1
        sk = "scr1k" if b == 0 else "scr1kB"
        xk = f"xt2:{b}"
        E("sync", lambda e, nt=nt, b=b: e.dma_start(
            out=xt2[:, 1024 * b:1024 * (b + 1)],
            in_=dr["x"][128 * nt:128 * (nt + 1), :]),
          writes=[xk], dma=True)
        E("gpsimd", lambda e, nt=nt: e.tensor_tensor(
            out=zscr[:, :], in0=qt[:, 1024 * nt:1024 * (nt + 1)],
            in1=kmean_b[:, :], op=ALU.mult),
          reads=[f"qt:{nt}", "kmean_b"], writes=["zscr"])
        E("vector", lambda e, nt=nt: e.reduce_sum(
            out=z_all[:, 16 * nt:16 * (nt + 1)],
            in_=zscr[:, :].rearrange("p (h d) -> p h d", d=64), axis=AX.X),
          reads=["zscr"], writes=[f"z_all:{nt}"])
        E("vector", lambda e, nt=nt: e.tensor_scalar(
            out=z_all[:, 16 * nt:16 * (nt + 1)],
            in0=z_all[:, 16 * nt:16 * (nt + 1)], scalar1=1e-6, scalar2=None,
            op0=ALU.add),
          reads=[f"z_all:{nt}"], writes=[f"z_all:{nt}"])
        E("vector", lambda e, nt=nt: e.reciprocal(
            out=z_all[:, 16 * nt:16 * (nt + 1)],
            in_=z_all[:, 16 * nt:16 * (nt + 1)]),
          reads=[f"z_all:{nt}"], writes=[f"z_all:{nt}"])
        emit_rope_apply(cx, qt[:, 1024 * nt:1024 * (nt + 1)], f"qt:{nt}", nt,
                        ss=b)
        items = []
        for cc in range(CC):
            def mk_evac(c2, b2_=b):
                def f(e, ps):
                    return e.activation(
                        out=qrT[:, 1024 * b2_ + 128 * c2:
                                1024 * b2_ + 128 * (c2 + 1)], in_=ps,
                        func=AF.Copy)
                return f
            items.append((
                qt[:, 1024 * nt + 128 * cc:1024 * nt + 128 * (cc + 1)],
                [f"qt:{nt}"], mk_evac(cc), [f"qrT:{b}"]))
        transpose_via_ptr(cx, items)

    def b2_stageB(nt):
        b = nt % 2
        res = resA if b == 0 else resB
        scr = scrA1 if b == 0 else scrB1
        rk = "res1k" if b == 0 else "res1kB"
        sk = "scr1k" if b == 0 else "scr1kB"
        xk = f"xt2:{b}"
        for pr in range(8):
            E("tensor", lambda e, pr=pr, b=b: e.matmul(
                cx.sb["pout"][:, 128 * pr:128 * (pr + 1)],
                qrT[:, 1024 * b + 128 * pr:1024 * b + 128 * (pr + 1)],
                kv_bd[:, 128 * pr:128 * (pr + 1)],
                start=True, stop=True, skip_group_check=True),
              reads=[f"qrT:{b}", "kv_bd"], writes=["pout"])
        E("tensor", lambda e, nt=nt: e.transpose(
            cx.sb["pbc"][0:16, 0:128], z_all[:, 16 * nt:16 * (nt + 1)],
            cx.sb["identf"][:, :]),
          reads=[f"z_all:{nt}", "identf"], writes=["pbc"])
        E("vector", lambda e: e.tensor_copy(out=zTs[:, :],
                                            in_=cx.sb["pbc"][0:16, 0:128]),
          reads=["pbc"], writes=["zTs"])
        for dh in range(2):
            E("tensor", lambda e, dh=dh: e.matmul(
                cx.sb["pA" if dh == 0 else "pB"][:, :], zTs[0:16, :],
                cx.sb["i16f"][0:16, 512 * dh:512 * (dh + 1)],
                start=True, stop=True),
              reads=["zTs", "i16f"], writes=["pA" if dh == 0 else "pB"])
        E("vector", lambda e, nt=nt, b=b, scr=scr: e.tensor_scalar(
            out=scr[:, :], in0=xt2[:, 1024 * b:1024 * (b + 1)],
            scalar1=cx.sb["rst"][:, nt:nt + 1],
            scalar2=cx.sb["mrst"][:, nt:nt + 1],
            op0=ALU.mult, op1=ALU.subtract),
          reads=[xk, "l0rst", "l0mrst"], writes=[sk])
        for dh in range(2):
            E("vector", lambda e, dh=dh: e.tensor_copy(
                out=zbS[:, 512 * dh:512 * (dh + 1)],
                in_=cx.sb["pA" if dh == 0 else "pB"][:, :]),
              reads=["pA" if dh == 0 else "pB"], writes=["zbS"])
        E("vector", lambda e, res=res: e.tensor_tensor(
            out=res[:, :], in0=cx.sb["pout"][:, :], in1=zbS[:, :],
            op=ALU.mult),
          reads=["pout", "zbS"], writes=[rk])
        E("vector", lambda e, res=res, scr=scr: e.tensor_tensor(
            out=res[:, :], in0=res[:, :], in1=scr[:, :], op=ALU.add),
          reads=[rk, sk], writes=[rk])
        E("vector", lambda e, res=res: e.reduce_sum(
            out=st[:, 0:1], in_=res[:, :], axis=AX.X),
          reads=[rk], writes=["st8:s"])
        E("scalar", lambda e, res=res, scr=scr: e.activation(
            out=scr[:, :], in_=res[:, :], func=AF.Square,
            accum_out=st[:, 1:2]),
          reads=[rk], writes=[sk, "st8:q"])
        E("vector", lambda e: e.tensor_scalar(
            out=st[:, 2:3], in0=st[:, 0:1], scalar1=1.0 / 1024.0,
            scalar2=None, op0=ALU.mult),
          reads=["st8:s"], writes=["st8:m"])
        E("scalar", lambda e: e.activation(out=st[:, 3:4], in_=st[:, 2:3],
                                           func=AF.Square),
          reads=["st8:m"], writes=["st8:m2"])
        E("vector", lambda e: e.tensor_scalar(
            out=st[:, 4:5], in0=st[:, 1:2], scalar1=1.0 / 1024.0,
            scalar2=1e-12, op0=ALU.mult, op1=ALU.add),
          reads=["st8:q"], writes=["st8:v"])
        E("vector", lambda e: e.tensor_tensor(
            out=st[:, 4:5], in0=st[:, 4:5], in1=st[:, 3:4], op=ALU.subtract),
          reads=["st8:v", "st8:m2"], writes=["st8:v"])
        E("scalar", lambda e: e.activation(out=st[:, 5:6], in_=st[:, 4:5],
                                           func=AF.Sqrt),
          reads=["st8:v"], writes=["st8:sd"])
        E("vector", lambda e: e.reciprocal(out=st[:, 6:7], in_=st[:, 5:6]),
          reads=["st8:sd"], writes=["st8:r"])
        E("vector", lambda e: e.tensor_tensor(
            out=st[:, 7:8], in0=st[:, 2:3], in1=st[:, 6:7], op=ALU.mult),
          reads=["st8:m", "st8:r"], writes=["st8:mr"])
        E("vector", lambda e, res=res, scr=scr: e.tensor_scalar(
            out=scr[:, :], in0=res[:, :], scalar1=st[:, 6:7],
            scalar2=st[:, 7:8], op0=ALU.mult, op1=ALU.subtract),
          reads=[rk, "st8:r", "st8:mr"], writes=[sk])
        E("vector", lambda e, res=res, scr=scr: e.tensor_tensor(
            out=res[:, :], in0=scr[:, :], in1=gammab[:, :], op=ALU.mult),
          reads=[sk, "gammab"], writes=[rk])
        E("vector", lambda e, res=res, scr=scr: e.tensor_tensor(
            out=scr[:, :], in0=res[:, :], in1=betab[:, :], op=ALU.add),
          reads=[rk, "betab"], writes=[sk])
        E("scalar", lambda e, nt=nt, scr=scr: e.dma_start(
            out=dr["s0"][128 * nt:128 * (nt + 1), :], in_=scr[:, :]),
          reads=[sk], writes=[f"s0d:{nt}"], dma=True)

    b2_stageA(0)
    b2_stageA(1)
    for nt in range(NT):
        b2_stageB(nt)
        if nt + 2 < NT:
            b2_stageA(nt + 2)
    cx.close_scope()
    cx.close_scope()

    if KSTOP == "b2":
        return
    cx.close_scope()

    for _ in range(40):
        pump()
    emit_l0_gating_final(cx)
    BIG = 1024.0
    slot_tf = cx.sbuf("slot_tf", [128, 128], F32)
    slot_ti = cx.sbuf("slot_ti", [128, 128], I32)
    cx.scope()
    cx.psum("pE", (128, 512), F32)
    cx.psum("pF", (128, 512), F32)
    cx.scope()
    mpk = cx.sbuf("mpk", [128, 128], F32)
    ltr = cx.sbuf("ltr", [128, 128], BF16)
    bsm = cx.sbuf("bsm", [128, 128], BF16)
    cntb = cx.sbuf("cntb", [128, 1], BF16)
    offs = cx.sbuf("offs", [128, 1], F32)
    th = cx.sb["th"]
    s8, s8t = cx.sb["s8"], cx.sb["s8t"]
    thb, cnt_col, ind_sb = cx.sb["thb"], cx.sb["cnt_col"], cx.sb["ind_sb"]
    pbc = cx.sb["pbc"]
    E("tensor", lambda e: e.matmul(
        pbc[0:128, 0:1], s8[0:8, :], th[0:8, 1:2], start=True, stop=True),
      reads=["s8", "th:hi"], writes=["pbc"])
    E("vector", lambda e: e.tensor_copy(out=thb[:, 0:1], in_=pbc[0:128, 0:1]),
      reads=["pbc"], writes=["thb"])
    E("vector", lambda e: e.tensor_scalar(
        out=ind_sb[:, :], in0=cx.sb["aff_pack"][:, :], scalar1=thb[:, 0:1],
        scalar2=0.0, op0=ALU.is_ge, op1=ALU.add, accum_out=cnt_col[:, 0:1]),
      reads=["aff_pack", "thb"], writes=["ind_f", "cnt_col"])
    E("vector", lambda e: e.tensor_copy(out=mpk[:, :], in_=ind_sb[:, :]),
      reads=["ind_f"], writes=["mpk"])
    E("vector", lambda e: e.tensor_copy(out=cntb[:, 0:1], in_=cnt_col[:, 0:1]),
      reads=["cnt_col"], writes=["cntb"])
    E("gpsimd", lambda e: e.memset(ltr[:, :], 1.0), writes=["ltr"])
    E("gpsimd", lambda e: e.affine_select(
        out=ltr[:, :], in_=ltr[:, :], compare_op=ALU.is_ge, fill=0.0,
        base=-1, pattern=[[1, 128]], channel_multiplier=-1),
      reads=["ltr"], writes=["ltr"])
    s8b = cx.sbuf("s8b", [8, 128], BF16)
    E("vector", lambda e: e.tensor_copy(out=s8b[0:8, :], in_=s8[0:8, :]),
      reads=["s8"], writes=["s8b"])
    E("tensor", lambda e: e.matmul(
        pbc[0:128, 0:128], s8b[0:8, :], s8b[0:8, :], start=True, stop=True),
      reads=["s8b"], writes=["pbc"])
    E("vector", lambda e: e.tensor_copy(out=bsm[:, :], in_=pbc[0:128, 0:128]),
      reads=["pbc"], writes=["bsm"])
    E("vector", lambda e: e.tensor_tensor(
        out=ltr[:, :], in0=ltr[:, :], in1=bsm[:, :], op=ALU.mult),
      reads=["ltr", "bsm"], writes=["ltr"])
    E("tensor", lambda e: e.matmul(
        pbc[0:128, 0:1], ltr[:, :], cntb[:, 0:1], start=True, stop=True),
      reads=["ltr", "cntb"], writes=["pbc"])
    E("vector", lambda e: e.tensor_copy(out=offs[:, 0:1], in_=pbc[0:128, 0:1]),
      reads=["pbc"], writes=["offs"])
    ev8i = cx.sbuf("ev8i", [8, 1], I32)
    ev8 = cx.sbuf("ev8", [8, 1], F32)
    bigcol = cx.sbuf("bigcol", [128, 1], F32)
    offsb = cx.sbuf("offsb", [128, 1], F32)
    E("gpsimd", lambda e: e.iota(ev8i[:, :], pattern=[[0, 1]], base=0,
                                 channel_multiplier=1),
      writes=["ev8i"])
    E("vector", lambda e: e.tensor_copy(out=ev8[:, :], in_=ev8i[:, :]),
      reads=["ev8i"], writes=["ev8"])
    E("tensor", lambda e: e.matmul(
        pbc[0:128, 4:5], s8[0:8, :], ev8[0:8, 0:1], start=True, stop=True),
      reads=["s8", "ev8"], writes=["pbc"])
    E("vector", lambda e: e.scalar_tensor_tensor(
        out=bigcol[:, 0:1], in0=pbc[0:128, 4:5], scalar=-16.0,
        in1=cx.sb["iota_col"][:, 0:1], op0=ALU.mult, op1=ALU.add),
      reads=["pbc", "iota_col"], writes=["ntcol"])
    E("vector", lambda e: e.tensor_tensor(
        out=offsb[:, 0:1], in0=offs[:, 0:1], in1=bigcol[:, 0:1],
        op=ALU.subtract),
      reads=["offs", "ntcol"], writes=["offsb"])
    E("vector", lambda e: e.tensor_scalar(
        out=offsb[:, 0:1], in0=offsb[:, 0:1], scalar1=-1.0 - BIG,
        scalar2=None, op0=ALU.add),
      reads=["offsb"], writes=["offsb"])
    E("vector", lambda e: e.tensor_scalar(
        out=bigcol[:, 0:1], in0=bigcol[:, 0:1], scalar1=BIG,
        scalar2=None, op0=ALU.add),
      reads=["ntcol"], writes=["bigcol"])
    E("vector", lambda e: e.tensor_tensor_scan(
        out=mpk[:, :], data0=mpk[:, :], data1=mpk[:, :], initial=0.0,
        op0=ALU.add, op1=ALU.bypass),
      reads=["mpk"], writes=["mpk2"])
    E("vector", lambda e: e.tensor_scalar(
        out=mpk[:, :], in0=mpk[:, :], scalar1=offsb[:, 0:1],
        scalar2=None, op0=ALU.add),
      reads=["mpk2", "offsb"], writes=["mpk3"])
    E("vector", lambda e: e.tensor_tensor(
        out=mpk[:, :], in0=mpk[:, :],
        in1=ind_sb[:, :], op=ALU.mult),
      reads=["mpk3", "ind_f"], writes=["mpk4"])
    E("vector", lambda e: e.tensor_scalar(
        out=mpk[:, :], in0=mpk[:, :], scalar1=bigcol[:, 0:1],
        scalar2=None, op0=ALU.add),
      reads=["mpk4", "bigcol"], writes=["mpk5"])
    E("tensor", lambda e: e.transpose(
        pbc[0:128, 0:128], mpk[:, :], cx.sb["identf"][:, :]),
      reads=["mpk5", "identf"], writes=["pbc"])
    E("vector", lambda e: e.tensor_copy(out=slot_tf[:, :],
                                        in_=pbc[0:128, 0:128]),
      reads=["pbc"], writes=["slot_tf"])
    E("vector", lambda e: e.tensor_copy(out=slot_ti[:, :], in_=slot_tf[:, :]),
      reads=["slot_tf"], writes=["slot_ti"])
    cx.close_scope()

    zb128 = cx.sbuf("zb128", [128, 1040], BF16)
    z128f = cx.sbuf("z128f", [128, 1024], F32)
    E("gpsimd", lambda e: e.memset(zb128[:, :], 0.0), writes=["zb128"])
    E("gpsimd", lambda e: e.memset(z128f[:, :], 0.0), writes=["z128f"])
    for nt_z in range(NT):
        E("sync", lambda e, nt_z=nt_z: e.dma_start(
            out=dr["g1d"][128 * nt_z:128 * (nt_z + 1), :], in_=z128f[:, :]),
          reads=["z128f"], writes=[f"g1z:{nt_z}"], dma=True)
    xg_keys = {xi: [f"xg:{xi}:{nt}" for nt in range(NT)] for xi in range(2)}
    for xi in range(2):
        for r0 in range(0, 1040, 128):
            rn = min(128, 1040 - r0)
            E("scalar", lambda e, xi=xi, r0=r0, rn=rn: e.dma_start(
                out=dr[f"xg{xi}"][r0:r0 + rn, :], in_=zb128[0:rn, :]),
              reads=["zb128"], writes=xg_keys[xi], dma=True)
    meta4 = cx.sbuf("meta4", [128, 512], F32)
    wb_b = cx.sbuf("wb_b", [128, 128], BF16)
    wlo_t = cx.sbuf("wlo_t", [128, 128], F32)
    iota8i = cx.sbuf("iota8i", [128, 8], I32)
    iota8 = cx.sbuf("iota8", [128, 8], F32)
    w_t = cx.sb["w_t"]
    wtk = [f"w_t:{nt}" for nt in range(NT)]
    E("vector", lambda e: e.tensor_copy(out=wb_b[:, :], in_=w_t[:, :]),
      reads=wtk, writes=["wb_b"])
    E("vector", lambda e: e.tensor_tensor(
        out=wlo_t[:, :], in0=w_t[:, :], in1=wb_b[:, :], op=ALU.subtract),
      reads=wtk + ["wb_b"], writes=["wlo_t"])
    E("gpsimd", lambda e: e.iota(iota8i[:, :], pattern=[[0, 8]], base=0,
                                 channel_multiplier=1),
      writes=["iota8i"])
    E("vector", lambda e: e.tensor_copy(out=iota8[:, :], in_=iota8i[:, :]),
      reads=["iota8i"], writes=["iota8"])
    m4v = meta4[:, :].rearrange("p (e nt four) -> p e nt four", e=8, four=4)
    for e_i in range(NE):
        wv = w_t[:, :].rearrange("p (nt e) -> p nt e", e=8)[:, :, e_i]
        lv = wlo_t[:, :].rearrange("p (nt e) -> p nt e", e=8)[:, :, e_i]
        E("vector", lambda e, e_i=e_i, wv=wv: e.tensor_copy(
            out=m4v[:, e_i, :, 0], in_=wv),
          reads=wtk, writes=[f"m4w:{e_i}"])
        E("vector", lambda e, e_i=e_i, lv=lv: e.tensor_copy(
            out=m4v[:, e_i, :, 1], in_=lv),
          reads=["wlo_t"], writes=[f"m4l:{e_i}"])
    for nt in range(NT):
        E("vector", lambda e, nt=nt: e.tensor_scalar(
            out=m4v[:, :, nt, 2], in0=iota8[:, :], scalar1=0.0,
            scalar2=float(nt // 2), op0=ALU.mult, op1=ALU.add),
          reads=["iota8"], writes=[f"m4h:{nt}"])
        E("vector", lambda e, nt=nt: e.tensor_scalar(
            out=m4v[:, :, nt, 3], in0=iota8[:, :],
            scalar1=float(128 * (nt & 1)), scalar2=None, op0=ALU.add),
          reads=["iota8"], writes=[f"m4o:{nt}"])
    xgT = [cx.sbuf(f"xgT{c}", [128, 1040], BF16) for c in range(CC)]
    mcolb = cx.sbuf("mcolb", [128, 64], BF16)
    mcol = cx.sbuf("mcol", [128, 48], F32)
    wcol2 = cx.sbuf("wcol2", [128, 16], F32)
    idxi2 = cx.sbuf("idxi2", [128, 16], I32)
    xbuf = cx.sbuf("xbuf", [128, 4160], BF16)
    h_all = cx.sbuf("h_all", [128, 16 * 1024], BF16)
    wgu = cx.sbuf("wgu", [128, 16384], BF16)
    wd2 = cx.sbuf("wd2", [128, 32768], BF16)
    w_t = cx.sb["w_t"]
    scrA = cx.sb["scr512a"]
    def load_wd2(e_i):
        wdb = e_i % 2
        for k4 in range(4):
            E("gpsimd", lambda eng, e_i=e_i, k4=k4, wdb=wdb: eng.dma_start(
                out=wd2[:, 16384 * wdb + 4096 * k4:
                        16384 * wdb + 4096 * (k4 + 1)].rearrange(
                    "p (c j) -> p c j", j=1024),
                in_=dr["Wd"][e_i, 512 * k4:512 * (k4 + 1), :].rearrange(
                    "(c p) j -> p c j", p=128)),
              writes=[f"wd2:{wdb}:{ic}" for ic in range(4 * k4, 4 * k4 + 4)],
              dma=True)

    def prep_scat(e_i):
        xgb = e_i % 2
        for nt in range(NT):
            xb = nt % 4
            E("sync", lambda eng, nt=nt, xb=xb: eng.dma_start(
                out=xbuf[:, 1040 * xb:1040 * xb + 1024],
                in_=dr["xnt"][128 * nt:128 * (nt + 1), :]),
              writes=[f"xbuf:{xb}"], dma=True)
            E("vector", lambda eng, e_i=e_i, nt=nt, xb=xb: eng.tensor_copy(
                out=xbuf[:, 1040 * xb + 1024:1040 * xb + 1028],
                in_=meta4[:, 4 * (16 * e_i + nt):4 * (16 * e_i + nt) + 4]),
              reads=[f"m4w:{e_i}", f"m4l:{e_i}", f"m4h:{nt}", f"m4o:{nt}"],
              writes=[f"xbufm:{xb}"])
            E("gpsimd", lambda eng, e_i=e_i, nt=nt, xb=xb, xgb=xgb:
              eng.indirect_dma_start(
                  out=dr[f"xg{xgb}"][:, :],
                  out_offset=IndirectOffsetOnAxis(
                      slot_ti[:, 16 * e_i + nt:16 * e_i + nt + 1], 0),
                  in_=xbuf[:, 1040 * xb:1040 * (xb + 1)], in_offset=None),
              reads=[f"xbuf:{xb}", f"xbufm:{xb}", "slot_ti"],
              writes=[f"xg:{xgb}:{nt}"], dma=True)

    def prep_xgt(e_i):
        xgb = e_i % 2
        for cc in range(CC):
            E("scalar",
              lambda eng, cc=cc, xgb=xgb: eng.dma_start(
                out=xgT[cc][:, :],
                in_=dr[f"xg{xgb}"][:, 128 * cc:128 * (cc + 1)],
                transpose=True),
              reads=xg_keys[xgb], writes=[f"xgT{cc}"], dma=True)
        E("scalar", lambda eng, xgb=xgb: eng.dma_start(
            out=mcolb[:, 32 * xgb:32 * (xgb + 1)].rearrange(
                "p (g c) -> p g c", c=4),
            in_=dr[f"xg{xgb}"][0:1024, 1024:1028].rearrange(
                "(g p) c -> p g c", p=128)),
          reads=xg_keys[xgb], writes=[f"xmT:{xgb}"], dma=True)
        for zg in range(8):
            E("scalar", lambda eng, xgb=xgb, zg=zg: eng.dma_start(
                out=dr[f"xg{xgb}"][128 * zg:128 * (zg + 1), 1024:1040],
                in_=zb128[:, 0:16]),
              reads=["zb128", f"xmT:{xgb}"], writes=xg_keys[xgb], dma=True)
        E("scalar", lambda eng, xgb=xgb: eng.activation(
            out=mcol[:, 0:32], in_=mcolb[:, 32 * xgb:32 * (xgb + 1)],
            func=AF.Copy),
          reads=[f"xmT:{xgb}"], writes=["mcol"])
        mv = mcol[:, 0:32].rearrange("p (g four) -> p g four", four=4)
        E("vector", lambda eng, xgb=xgb: eng.tensor_tensor(
            out=wcol2[:, 8 * xgb:8 * (xgb + 1)], in0=mv[:, :, 0],
            in1=mv[:, :, 1], op=ALU.add),
          reads=["mcol"], writes=[f"wcol:{xgb}"])
        E("vector", lambda eng: eng.tensor_scalar(
            out=mcol[:, 40:48], in0=mv[:, :, 2], scalar1=256.0,
            scalar2=None, op0=ALU.mult),
          reads=["mcol"], writes=["mcolx"])
        E("vector", lambda eng: eng.tensor_tensor(
            out=mcol[:, 40:48], in0=mcol[:, 40:48], in1=mv[:, :, 3],
            op=ALU.add),
          reads=["mcolx", "mcol"], writes=["mcolx"])
        E("vector", lambda eng, xgb=xgb: eng.tensor_copy(
            out=idxi2[:, 8 * xgb:8 * (xgb + 1)], in_=mcol[:, 40:48]),
          reads=["mcolx"], writes=[f"idxi:{xgb}"])

    def wgu_load(e_i, ig):
        gb = ig % 2
        E("gpsimd", lambda eng, e_i=e_i, ig=ig, gb=gb: eng.dma_start(
            out=wgu[:, 8192 * gb:8192 * gb + 4096].rearrange(
                "p (c i) -> p c i", i=512),
            in_=dr["Wg"][e_i, :, 512 * ig:512 * (ig + 1)].rearrange(
                "(c p) i -> p c i", p=128)),
          writes=[f"wgu:g{gb}"], dma=True)
        E("gpsimd", lambda eng, e_i=e_i, ig=ig, gb=gb: eng.dma_start(
            out=wgu[:, 8192 * gb + 4096:8192 * gb + 8192].rearrange(
                "p (c i) -> p c i", i=512),
            in_=dr["Wu"][e_i, :, 512 * ig:512 * (ig + 1)].rearrange(
                "(c p) i -> p c i", p=128)),
          writes=[f"wgu:u{gb}"], dma=True)

    def phase1(e_i, skip01=False):
        for ig in range(4):
            gb = ig % 2
            if ig >= 2 or not skip01:
                wgu_load(e_i, ig)
            for sub in range(4):
                ic = 4 * ig + sub
                for ks in range(2):
                    par = (2 * sub + ks) % 2
                    pg, pgk = (("pA", "pA") if par == 0 else ("pE", "pE"))
                    pu, puk = (("pB", "pB") if par == 0 else ("pF", "pF"))
                    sA, sAk = (("scr512a", "scr512a") if par == 0
                               else ("scr512b", "scr512b"))
                    mm_chain(cx, cx.sb[pg][:, :], pgk,
                             [(wgu[:, 8192 * gb + 512 * cc + 128 * sub:
                                   8192 * gb + 512 * cc + 128 * (sub + 1)],
                               xgT[cc][:, 512 * ks:512 * (ks + 1)],
                               [f"wgu:g{gb}", f"xgT{cc}"])
                              for cc in range(CC)])
                    mm_chain(cx, cx.sb[pu][:, :], puk,
                             [(wgu[:, 8192 * gb + 4096 + 512 * cc + 128 * sub:
                                   8192 * gb + 4096 + 512 * cc + 128 * (sub + 1)],
                               xgT[cc][:, 512 * ks:512 * (ks + 1)],
                               [f"wgu:u{gb}", f"xgT{cc}"])
                              for cc in range(CC)])
                    E("scalar", lambda eng, pg=pg, sA=sA: eng.activation(
                        out=cx.sb[sA][:, :], in_=cx.sb[pg][:, :],
                        func=AF.Silu),
                      reads=[pgk], writes=[sAk])
                    E("vector", lambda eng, ic=ic, ks=ks, pu=pu, sA=sA:
                      eng.tensor_tensor(
                        out=h_all[:, 1024 * ic + 512 * ks:
                                  1024 * ic + 512 * (ks + 1)],
                        in0=cx.sb[sA][:, :], in1=cx.sb[pu][:, :],
                        op=ALU.mult),
                      reads=[sAk, puk], writes=[f"h:{ic}:{ks}"])

    def phase2(e_i):
        wdb = e_i % 2
        xgb = e_i % 2
        for kt in range(8):
            cb = kt % 2
            ctb = cx.sb["res1k"] if cb == 0 else cx.sb["scr1k"]
            ck = "res1k" if cb == 0 else "scr1k"
            for dh in range(2):
                ps = cx.sb["pE"] if dh == 0 else cx.sb["pF"]
                pk = "pE" if dh == 0 else "pF"
                mm_chain(cx, ps[:, :], pk,
                         [(h_all[:, 1024 * ic + 128 * kt:
                                 1024 * ic + 128 * (kt + 1)],
                           wd2[:, 16384 * wdb + 1024 * ic + 512 * dh:
                               16384 * wdb + 1024 * ic + 512 * (dh + 1)],
                           [f"h:{ic}:{kt // 4}", f"wd2:{wdb}:{ic}"])
                          for ic in range(16)])
                E("vector", lambda eng, ps=ps, pk=pk, dh=dh, ctb=ctb,
                  kt=kt, xgb=xgb: eng.tensor_scalar(
                      out=ctb[:, 512 * dh:512 * (dh + 1)], in0=ps[:, :],
                      scalar1=wcol2[:, 8 * xgb + kt:8 * xgb + kt + 1],
                      scalar2=None, op0=ALU.mult),
                  reads=[pk, f"wcol:{xgb}"], writes=[ck])
            prev = ([f"g1z:{nt_z}" for nt_z in range(NT)] if e_i == 0
                    else [f"g1w:{e_i - 1}:{k}" for k in range(8)])
            E("gpsimd", lambda eng, e_i=e_i, kt=kt, ctb=ctb, xgb=xgb:
              eng.indirect_dma_start(
                  out=dr["g1d"][:, :],
                  out_offset=IndirectOffsetOnAxis(
                      idxi2[:, 8 * xgb + kt:8 * xgb + kt + 1], 0),
                  in_=ctb[:, :], in_offset=None, compute_op=ALU.add),
              reads=[ck, f"idxi:{xgb}"] + prev,
              writes=[f"g1w:{e_i}:{kt}"], dma=True)

    for xb in range(4):
        E("gpsimd", lambda eng, xb=xb: eng.memset(
            xbuf[:, 1040 * xb + 1028:1040 * (xb + 1)], 0.0),
          writes=[f"xbufm:{xb}"])
    load_wd2(0)
    load_wd2(1)
    prep_scat(0)
    prep_scat(1)
    for e_i in range(NE):
        prep_xgt(e_i)
        phase1(e_i, skip01=(e_i > 0))
        if e_i + 1 < NE:
            wgu_load(e_i + 1, 0)
            wgu_load(e_i + 1, 1)
        phase2(e_i)
        if e_i + 2 < NE:
            prep_scat(e_i + 2)
            load_wd2(e_i + 2)
    cx.close_scope()

    if KSTOP == "c":
        return
    cx.scope()
    stg2 = cx.sbuf("stg2", [128, 4096], F32)
    acc2 = cx.sbuf("acc2", [128, 4096], F32)
    xt4 = cx.sbuf("xt4", [128, 4096], F32)
    g1done = [f"g1w:{NE - 1}:{k}" for k in range(8)]
    for nt in range(NT):
        b = nt % 4
        acc = acc2[:, 1024 * b:1024 * (b + 1)]
        ak = f"acc2:{b}"
        E("sync", lambda e, nt=nt, b=b: e.dma_start(
            out=xt4[:, 1024 * b:1024 * (b + 1)],
            in_=dr["x"][128 * nt:128 * (nt + 1), :]),
          writes=[f"xt4:{b}"], dma=True)
        E("scalar", lambda e, nt=nt, b=b: e.dma_start(
            out=stg2[:, 1024 * b:1024 * (b + 1)],
            in_=dr["s0"][128 * nt:128 * (nt + 1), :]),
          reads=[f"s0d:{nt}"], writes=[f"stg2:{b}"], dma=True)
        E("sync", lambda e, nt=nt, acc=acc: e.dma_start(
            out=acc[:, :], in_=dr["g1d"][128 * nt:128 * (nt + 1), :]),
          reads=g1done, writes=[ak], dma=True)
        E("scalar", lambda e, acc=acc: e.activation(
            out=acc[:, :], in_=acc[:, :], func=AF.Silu),
          reads=[ak], writes=[ak])
        E("vector", lambda e, acc=acc, b=b: e.tensor_tensor(
            out=acc[:, :], in0=acc[:, :],
            in1=stg2[:, 1024 * b:1024 * (b + 1)], op=ALU.mult),
          reads=[ak, f"stg2:{b}"], writes=[ak])
        E("vector", lambda e, acc=acc, b=b: e.tensor_tensor(
            out=acc[:, :], in0=acc[:, :],
            in1=xt4[:, 1024 * b:1024 * (b + 1)], op=ALU.add),
          reads=[ak, f"xt4:{b}"], writes=[ak])
        E("sync", lambda e, acc=acc, nt=nt: e.dma_start(
            out=dr["x1"][128 * nt:128 * (nt + 1), :], in_=acc[:, :]),
          reads=[ak], writes=[f"x1d:{nt}"], dma=True)
    cx.close_scope()


def emit_l1_routing(cx, dr):
    p = cx.p
    E = p.emit
    xnf = [cx.sb[f"l1xnf{c}"] for c in range(CC)]
    xnlo = [cx.sb[f"l1xnlo{c}"] for c in range(CC)]
    tsgb = cx.sb["tsgb"]
    sgn_f = cx.sb["sgn_f"]
    o_all = cx.sb["o_all"]
    o_T = cx.sb["o_T"]
    st = cx.sb["st8"]
    E("gpsimd", lambda e: e.dma_start(out=tsgb[:, :], in_=dr["tsig"][:, :]),
      writes=["tsgb"], dma=True)
    E("scalar", lambda e: e.activation(out=tsgb[:, :], in_=tsgb[:, :],
                                       func=AF.Sign),
      reads=["tsgb"], writes=["tsgb"])
    ptr = cx.sb["ptr"]
    for cc in range(CC):
        bank = cc % 2
        bkey = "ptrA" if bank == 0 else "ptrB"
        E("tensor", lambda e, cc=cc, bank=bank: e.transpose(
            ptr[:, 1024 * bank:1024 * bank + 8],
            tsgb[0:8, 128 * cc:128 * (cc + 1)], cx.sb["identb"][0:8, 0:8]),
          reads=["tsgb", "identb"], writes=[bkey])
        E("scalar", lambda e, cc=cc, bank=bank: e.activation(
            out=sgn_f[:, 8 * cc:8 * (cc + 1)],
            in_=ptr[:, 1024 * bank:1024 * bank + 8],
            func=AF.Copy),
          reads=[bkey], writes=["sgn_f"])
    pA = cx.sb["pA"]
    for nt in range(NT):
        pairs = []
        for cc in range(CC):
            pairs.append((xnf[cc][:, 128 * nt:128 * (nt + 1)],
                          sgn_f[:, 8 * cc:8 * (cc + 1)],
                          [f"l1xnf{cc}:{nt}", "sgn_f"]))
        for cc in range(CC):
            pairs.append((xnlo[cc][:, 128 * nt:128 * (nt + 1)],
                          sgn_f[:, 8 * cc:8 * (cc + 1)],
                          [f"l1xnlo{cc}:{nt}", "sgn_f"]))
        mm_chain(cx, pA[:, 0:8], "pA", pairs)
        E("vector", lambda e: e.reduce_max(out=st[:, 0:1], in_=pA[:, 0:8],
                                           axis=AX.X),
          reads=["pA"], writes=["st8:mx"])
        E("vector", lambda e, nt=nt: e.tensor_scalar(
            out=o_all[:, 8 * nt:8 * (nt + 1)], in0=pA[:, 0:8],
            scalar1=st[:, 0:1], scalar2=None, op0=ALU.is_ge),
          reads=["pA", "st8:mx"], writes=[f"o_all:{nt}"])
        E("tensor", lambda e, nt=nt: e.transpose(
            cx.sb["pbc"][0:8, 0:128],
            o_all[:, 8 * nt:8 * (nt + 1)], cx.sb["identf"][:, :]),
          reads=[f"o_all:{nt}", "identf"], writes=["pbc"])
        E("scalar", lambda e, nt=nt: e.activation(
            out=o_T[0:8, 128 * nt:128 * (nt + 1)],
            in_=cx.sb["pbc"][0:8, 0:128],
            func=AF.Copy),
          reads=["pbc"], writes=[f"o_T:{nt}"])


def emit_l1_ffn(cx, dr):
    p = cx.p
    E = p.emit
    xnf = [cx.sb[f"l1xnf{c}"] for c in range(CC)]
    w1b = cx.sb["w1b"]
    b1sb = cx.sb["b1sb"]
    acc1sb = cx.sb["acc1sb"]
    o_all = cx.sb["o_all"]
    o_T = cx.sb["o_T"]
    res1k = cx.sb["res1k"]
    E("gpsimd", lambda e: e.dma_start(out=b1sb[:, :], in_=dr["b1"][:, :]),
      writes=["b1sb"], dma=True)
    for t in range(8):
        wb = t % 2
        for cc in range(CC):
            E("gpsimd", lambda e, t=t, cc=cc, wb=wb: e.dma_start(
                out=w1b[:, 8192 * wb + 1024 * cc:8192 * wb + 1024 * (cc + 1)],
                in_=dr["W1"][t, 128 * cc:128 * (cc + 1), :]),
              writes=[f"w1b:{wb}:{cc}"], dma=True)
        for nt in range(NT):
            cb = nt % 2
            for dh in range(2):
                ps = cx.sb["pA"] if dh == 0 else cx.sb["pB"]
                pk = "pA" if dh == 0 else "pB"
                mm_chain(cx, ps[:, :], pk,
                         [(xnf[cc][:, 128 * nt:128 * (nt + 1)],
                           w1b[:, 8192 * wb + 1024 * cc + 512 * dh:
                               8192 * wb + 1024 * cc + 512 * (dh + 1)],
                           [f"l1xnf{cc}:{nt}", f"w1b:{wb}:{cc}"])
                          for cc in range(CC)])
                if t == 0:
                    E("vector", lambda e, ps=ps, pk=pk, nt=nt, dh=dh, t=t:
                      e.tensor_scalar(
                          out=acc1sb[:, 1024 * nt + 512 * dh:
                                     1024 * nt + 512 * (dh + 1)],
                          in0=ps[:, :],
                          scalar1=o_all[:, 8 * nt + t:8 * nt + t + 1],
                          scalar2=None, op0=ALU.mult),
                      reads=[pk, f"o_all:{nt}"], writes=[f"a1s:{nt}"])
                else:
                    E("vector", lambda e, ps=ps, pk=pk, nt=nt, dh=dh, t=t:
                      e.scalar_tensor_tensor(
                          out=acc1sb[:, 1024 * nt + 512 * dh:
                                     1024 * nt + 512 * (dh + 1)],
                          in0=ps[:, :],
                          scalar=o_all[:, 8 * nt + t:8 * nt + t + 1],
                          in1=acc1sb[:, 1024 * nt + 512 * dh:
                                     1024 * nt + 512 * (dh + 1)],
                          op0=ALU.mult, op1=ALU.add),
                      reads=[pk, f"o_all:{nt}", f"a1s:{nt}"],
                      writes=[f"a1s:{nt}"])
    for nt in range(NT):
        for dh in range(2):
            ps = cx.sb["pA"] if dh == 0 else cx.sb["pB"]
            pk = "pA" if dh == 0 else "pB"
            E("tensor", lambda e, nt=nt, dh=dh, ps=ps: e.matmul(
                ps[:, :], o_T[0:8, 128 * nt:128 * (nt + 1)],
                b1sb[0:8, 512 * dh:512 * (dh + 1)], start=True, stop=True),
              reads=[f"o_T:{nt}", "b1sb"], writes=[pk])
            E("vector", lambda e, ps=ps, pk=pk, nt=nt, dh=dh: e.tensor_tensor(
                out=acc1sb[:, 1024 * nt + 512 * dh:1024 * nt + 512 * (dh + 1)],
                in0=acc1sb[:, 1024 * nt + 512 * dh:1024 * nt + 512 * (dh + 1)],
                in1=ps[:, :], op=ALU.add),
              reads=[pk, f"a1s:{nt}"], writes=[f"a1s:{nt}"])
        E("sync", lambda e, nt=nt: e.dma_start(
            out=dr["acc1"][128 * nt:128 * (nt + 1), :],
            in_=acc1sb[:, 1024 * nt:1024 * (nt + 1)]),
          reads=[f"a1s:{nt}"], writes=[f"acc1d:{nt}"], dma=True)


def emit_l1_attention(cx, dr):
    p = cx.p
    E = p.emit
    xnf = [cx.sb[f"l1xnf{c}"] for c in range(CC)]
    q_f = [cx.sb[f"q_f{c}"] for c in range(CC)]
    k_f = [cx.sb[f"k_f{c}"] for c in range(CC)]
    vt1 = cx.sb["vt1"]

    cx.scope()
    qwb = [cx.sbuf(f"qwb{c}", [128, 1024], BF16) for c in range(CC)]
    kwb = [cx.sbuf(f"kwb{c}", [128, 1024], BF16) for c in range(CC)]
    for cc in range(CC):
        E("gpsimd", lambda e, cc=cc: e.dma_start(
            out=qwb[cc][:, :], in_=dr["qw"][128 * cc:128 * (cc + 1), :]),
          writes=[f"qwb{cc}"], dma=True)
        E("gpsimd", lambda e, cc=cc: e.dma_start(
            out=kwb[cc][:, :], in_=dr["kw"][128 * cc:128 * (cc + 1), :]),
          writes=[f"kwb{cc}"], dma=True)
    for jc in range(CC):
        for ns in range(4):
            for which in ("q", "k"):
                wsrc = qwb if which == "q" else kwb
                dst = q_f if which == "q" else k_f
                dn = "q_f" if which == "q" else "k_f"
                wn = "qwb" if which == "q" else "kwb"
                ps = cx.sb["pA"] if which == "q" else cx.sb["pB"]
                pk = "pA" if which == "q" else "pB"
                mm_chain(cx, ps[:, :], pk,
                         [(wsrc[cc][:, 128 * jc:128 * (jc + 1)],
                           xnf[cc][:, 512 * ns:512 * (ns + 1)],
                           [f"{wn}{cc}"] +
                           [f"l1xnf{cc}:{4 * ns + u}" for u in range(4)])
                          for cc in range(CC)])
                E("scalar", lambda e, ps=ps, pk=pk, jc=jc, ns=ns, dst=dst,
                  dn=dn: e.activation(
                      out=dst[jc][:, 512 * ns:512 * (ns + 1)], in_=ps[:, :],
                      func=AF.Copy),
                  reads=[pk], writes=[f"{dn}{jc}"])
    cx.close_scope()
    cx.scope()
    vwb = [cx.sbuf(f"vwb{c}", [128, 1024], BF16) for c in range(CC)]
    for cc in range(CC):
        E("gpsimd", lambda e, cc=cc: e.dma_start(
            out=vwb[cc][:, :], in_=dr["vw"][128 * cc:128 * (cc + 1), :]),
          writes=[f"vwb{cc}"], dma=True)
    for nt in range(NT):
        for jh in range(2):
            ps = cx.sb["pA"] if jh == 0 else cx.sb["pB"]
            pk = "pA" if jh == 0 else "pB"
            mm_chain(cx, ps[:, :], pk,
                     [(xnf[cc][:, 128 * nt:128 * (nt + 1)],
                       vwb[cc][:, 512 * jh:512 * (jh + 1)],
                       [f"vwb{cc}", f"l1xnf{cc}:{nt}"]) for cc in range(CC)])
            E("vector", lambda e, ps=ps, pk=pk, nt=nt, jh=jh: e.tensor_copy(
                out=vt1[:, 1040 * nt + 520 * jh:1040 * nt + 520 * (jh + 1)]
                .rearrange("p (h x) -> p h x", x=65)[:, :, 0:64],
                in_=ps[:, :].rearrange("p (h x) -> p h x", x=64)),
              reads=[pk], writes=[f"vt1:{nt}"])
        E("vector", lambda e, nt=nt: e.memset(
            vt1[:, 1040 * nt:1040 * (nt + 1)]
            .rearrange("p (h x) -> p h x", x=65)[:, :, 64:65], 1.0),
          reads=[], writes=[f"vt1:{nt}"])
    cx.close_scope()


def emit_l1_attention_core(cx, dr):
    p = cx.p
    E = p.emit
    q_f = [cx.sb[f"q_f{c}"] for c in range(CC)]
    k_f = [cx.sb[f"k_f{c}"] for c in range(CC)]
    vt1 = cx.sb["vt1"]
    aof = [cx.sb[f"aof{c}"] for c in range(CC)]
    tmp_o = cx.sb["tmp_o"]
    ptb = cx.sb["ptb"]
    bcol = cx.sb["bcol"]
    slot_ctr = [0]
    zrow = cx.sb["zrow"]
    zrow0 = cx.sb["zrow0"]

    def wchunk(h):
        return 128 if h < 2 else (256 if h == 2 else 512)

    for h in range(NH):
        sg = SLOPES[h]
        pair, odd = h // 2, h % 2
        w = wchunk(h)
        skip_d = 80.0 / sg
        for isup in range(4):
            i0 = 512 * isup
            jmax = min(NT, (i0 + 512) // 128)
            live = []
            for jt in range(jmax):
                j0 = 128 * jt
                chunks = []
                any_live = False
                for kq in range(512 // w):
                    ick = i0 + w * kq
                    if j0 - ick >= w:
                        st_ = "invalid"
                    elif ick - (j0 + 127) > skip_d:
                        st_ = "skip"
                    elif j0 - ick >= 0:
                        st_ = "partial"
                    else:
                        st_ = "full"
                    chunks.append((kq, ick, st_))
                    if st_ in ("partial", "full"):
                        any_live = True
                if any_live:
                    live.append((jt, chunks))

            def emit_qk_exp(idx, h=h, pair=pair, odd=odd, sg=sg, w=w,
                            isup=isup):
                jt, chunks = live[idx]
                j0 = 128 * jt
                pb = idx % 2
                if odd == 0:
                    ps = cx.sb["pS"] if pb == 0 else cx.sb["pA"]
                    pk = "pS" if pb == 0 else "pA"
                else:
                    ps = cx.sb["pS2"] if pb == 0 else cx.sb["pB"]
                    pk = "pS2" if pb == 0 else "pB"
                pb4 = idx % 4
                ptk = f"ptb:{pb4}"
                pt = ptb[:, 512 * pb4:512 * (pb4 + 1)]
                E("tensor", lambda e, ps=ps, jt=jt, pair=pair, odd=odd,
                  isup=isup: e.matmul(
                      ps[:, :],
                      k_f[pair][64 * odd:64 * (odd + 1),
                                128 * jt:128 * (jt + 1)],
                      q_f[pair][64 * odd:64 * (odd + 1),
                                512 * isup:512 * (isup + 1)],
                      start=True, stop=True),
                  reads=[f"k_f{pair}", f"q_f{pair}"], writes=[pk])
                for kq, ick, st_ in chunks:
                    if st_ in ("invalid", "skip"):
                        E("vector", lambda e, pt=pt, kq=kq, w=w: e.memset(
                            pt[:, w * kq:w * (kq + 1)], 0.0),
                          reads=[], writes=[ptk])
                        continue
                    sl = slot_ctr[0] % 16
                    slot_ctr[0] += 1
                    E("vector", lambda e, j0=j0, ick=ick, sl=sl, sg=sg,
                      w=w: e.tensor_scalar(
                          out=bcol[:, sl:sl + 1],
                          in0=cx.sb["iota_col"][:, 0:1],
                          scalar1=sg, scalar2=sg * (j0 - (ick + w / 2.0)),
                          op0=ALU.mult, op1=ALU.add),
                      reads=["iota_col"], writes=[f"bcol:{sl}"])
                    E("scalar", lambda e, pt=pt, ps=ps, kq=kq, sl=sl, w=w:
                      e.activation(
                          out=pt[:, w * kq:w * (kq + 1)],
                          in_=ps[:, w * kq:w * (kq + 1)], func=AF.Exp,
                          scale=0.125, bias=bcol[:, sl:sl + 1]),
                      reads=[pk, f"bcol:{sl}"], writes=[ptk])
                    if st_ == "partial":
                        dlt = (j0 - ick) // 128
                        E("vector", lambda e, pt=pt, kq=kq, dlt=dlt, w=w:
                          e.tensor_tensor(
                              out=pt[:, w * kq:w * (kq + 1)],
                              in0=pt[:, w * kq:w * (kq + 1)],
                              in1=cx.sb["cmask"][:, 512 * dlt:512 * dlt + w],
                              op=ALU.mult),
                          reads=[ptk, "cmask"], writes=[ptk])

            def emit_pv(idx, h=h):
                jt, _ = live[idx]
                pb4 = idx % 4
                ptk = f"ptb:{pb4}"
                pt = ptb[:, 512 * pb4:512 * (pb4 + 1)]
                E("tensor", lambda e, pt=pt, jt=jt, h=h, fp=(idx == 0):
                  e.matmul(
                    cx.sb["pO"][0:65, :],
                    vt1[:, 1040 * jt + 65 * h:1040 * jt + 65 * (h + 1)],
                    pt, start=fp, stop=False, skip_group_check=True),
                  reads=[ptk, f"vt1:{jt}"], writes=["pO"])

            LAG = 2
            for idx in range(len(live)):
                emit_qk_exp(idx)
                if idx >= LAG:
                    emit_pv(idx - LAG)
            for idx in range(max(0, len(live) - LAG), len(live)):
                emit_pv(idx)
            if odd == 0:
                E("scalar", lambda e, pair=pair, isup=isup: e.activation(
                    out=aof[pair][0:64, 512 * isup:512 * (isup + 1)],
                    in_=cx.sb["pO"][0:64, :], func=AF.Copy),
                  reads=["pO"], writes=[f"aof{pair}:{isup}"])
            else:
                E("scalar", lambda e, isup=isup: e.activation(
                    out=tmp_o[0:64, 512 * isup:512 * (isup + 1)],
                    in_=cx.sb["pO"][0:64, :], func=AF.Copy),
                  reads=["pO"], writes=[f"tmp_o:{isup}"])
            E("vector", lambda e, isup=isup: e.tensor_copy(
                out=zrow[64:65, 512 * isup:512 * (isup + 1)],
                in_=cx.sb["pO"][64:65, :]),
              reads=["pO"], writes=[f"zrow:{isup}"])
        E("sync", lambda e: e.dma_start(out=zrow0[0:1, :], in_=zrow[64:65, :]),
          reads=[f"zrow:{u}" for u in range(4)], writes=["zrow0"], dma=True)
        E("vector", lambda e: e.reciprocal(out=zrow0[0:1, :],
                                           in_=zrow0[0:1, :]),
          reads=["zrow0"], writes=["zrow0"])
        for isup in range(4):
            E("tensor", lambda e, isup=isup: e.matmul(
                cx.sb["pbc"][0:64, :], cx.sb["onesf_row"][0:1, 0:64],
                zrow0[0:1, 512 * isup:512 * (isup + 1)], start=True,
                stop=True),
              reads=["zrow0", "onesf_row"], writes=["pbc"])
            if odd == 0:
                E("vector", lambda e, pair=pair, isup=isup: e.tensor_tensor(
                    out=aof[pair][0:64, 512 * isup:512 * (isup + 1)],
                    in0=aof[pair][0:64, 512 * isup:512 * (isup + 1)],
                    in1=cx.sb["pbc"][0:64, :], op=ALU.mult),
                  reads=[f"aof{pair}:{isup}", "pbc"],
                  writes=[f"aof{pair}:{isup}"])
            else:
                E("vector", lambda e, isup=isup: e.tensor_tensor(
                    out=tmp_o[0:64, 512 * isup:512 * (isup + 1)],
                    in0=tmp_o[0:64, 512 * isup:512 * (isup + 1)],
                    in1=cx.sb["pbc"][0:64, :], op=ALU.mult),
                  reads=[f"tmp_o:{isup}", "pbc"], writes=[f"tmp_o:{isup}"])
        if odd == 1:
            E("sync", lambda e, pair=pair: e.dma_start(
                out=aof[pair][64:128, :], in_=tmp_o[0:64, :]),
              reads=[f"tmp_o:{u}" for u in range(4)],
              writes=[f"aof{pair}:odd"], dma=True)


def emit_l1_oproj_combine(cx, dr):
    p = cx.p
    E = p.emit
    aof = [cx.sb[f"aof{c}"] for c in range(CC)]
    owb = [cx.sb[f"owb{c}"] for c in range(CC)]
    xt2 = cx.sb["xt2"]
    res1k, scr1k = cx.sb["res1k"], cx.sb["scr1k"]
    scrA, scrB = cx.sb["scr512a"], cx.sb["scr512b"]
    for cc in range(CC):
        E("gpsimd", lambda e, cc=cc: e.dma_start(
            out=owb[cc][:, :], in_=dr["ow"][128 * cc:128 * (cc + 1), :]),
          writes=[f"owb{cc}"], dma=True)
    aof_keys = [f"aof{cc}:{u}" for cc in range(CC) for u in range(4)] + \
               [f"aof{cc}:odd" for cc in range(CC)]
    for nt in range(NT):
        b = nt % 2
        E("sync", lambda e, nt=nt, b=b: e.dma_start(
            out=xt2[:, 1024 * b:1024 * (b + 1)],
            in_=dr["x1"][128 * nt:128 * (nt + 1), :]),
          reads=[f"x1d:{nt}"], writes=[f"xt2:{b}"], dma=True)
        E("scalar", lambda e, nt=nt: e.dma_start(
            out=scr1k[:, :], in_=dr["acc1"][128 * nt:128 * (nt + 1), :]),
          reads=[f"acc1d:{nt}"], writes=["scr1k"], dma=True)
        for dh in range(2):
            ps = cx.sb["pA"] if dh == 0 else cx.sb["pB"]
            pk = "pA" if dh == 0 else "pB"
            mm_chain(cx, ps[:, :], pk,
                     [(aof[cc][:, 128 * nt:128 * (nt + 1)],
                       owb[cc][:, 512 * dh:512 * (dh + 1)],
                       [f"owb{cc}"] + aof_keys) for cc in range(CC)])
            E("scalar", lambda e, dh=dh: e.activation(
                out=scrA[:, :], in_=scr1k[:, 512 * dh:512 * (dh + 1)],
                func=AF.Silu),
              reads=["scr1k"], writes=["scr512a"])
            E("vector", lambda e, ps=ps, pk=pk: e.tensor_tensor(
                out=scrB[:, :], in0=ps[:, :], in1=scrA[:, :], op=ALU.mult),
              reads=[pk, "scr512a"], writes=["scr512b"])
            E("vector", lambda e, b=b, dh=dh: e.tensor_tensor(
                out=scrB[:, :], in0=scrB[:, :],
                in1=xt2[:, 1024 * b + 512 * dh:1024 * b + 512 * (dh + 1)],
                op=ALU.add),
              reads=["scr512b", f"xt2:{b}"], writes=["scr512b"])
            E("sync", lambda e, nt=nt, dh=dh: e.dma_start(
                out=dr["y"][128 * nt:128 * (nt + 1), 512 * dh:512 * (dh + 1)],
                in_=scrB[:, :]),
              reads=["scr512b"], writes=[f"yd:{nt}:{dh}"], dma=True)



KSTOP = os.environ.get("KSTOP", "")


def build_program(layers=(0, 1), dbg=(), scratch_out=False):
    nc = bass.Bass("TRN2", target_bir_lowering=False)
    dr = {}

    def din(name, shape):
        dr[name] = nc.dram_tensor(name, list(shape), F32, kind="ExternalInput")

    def dscr(name, shape, out=False):
        dr[name] = nc.dram_tensor(
            name, list(shape), F32,
            kind="ExternalOutput" if (out or scratch_out) else "Internal")

    both = len(layers) == 2
    if 0 in layers:
        din("x", [N, C])
        din("Wq", [C, C]); din("bq", [C]); din("Wk", [C, C]); din("bk", [C])
        din("Wv", [C, C]); din("bv", [C])
        din("gamma", [C]); din("beta", [C])
        din("gate_w", [C, NE])
        din("Wg", [NE, C, NI]); din("Wu", [NE, C, NI]); din("Wd", [NE, NI, C])
        dscr("s0", [N, C])
        dr["xnt"] = nc.dram_tensor("xnt", [N, C], BF16, kind="Internal")
        for i in range(2):
            dr[f"xg{i}"] = nc.dram_tensor(f"xg{i}", [1040, 1040], BF16,
                                          kind="Internal")
        dr["g1d"] = nc.dram_tensor("g1d", [N, C], F32, kind="Internal")
        dscr("x1", [N, C], out=not both)
    if 1 in layers:
        if 0 not in layers:
            din("x1", [N, C])
        din("qw", [C, C]); din("kw", [C, C]); din("vw", [C, C])
        din("ow", [C, C])
        din("tsig", [8, C])
        din("W1", [8, C, C]); din("b1", [8, C])
        dscr("acc1", [N, C])
        dscr("y", [N, C], out=True)

    p = Prog()
    with contextlib.ExitStack() as stack:
        ec = stack.enter_context
        cx = Ctx(nc, p, ec)
        cx.dbg_names = set(dbg)
        cx.psum("ptr", (128, 2048), BF16)
        cx.psum("pbc", (128, 512), F32)
        cx.psum("pA", (128, 512), F32)
        cx.psum("pB", (128, 512), F32)
        cx.sbuf("xt2", [128, 2048], F32)
        cx.sbuf("scr1k", [128, 1024], F32)
        cx.sbuf("res1k", [128, 1024], F32)
        cx.sbuf("scr512a", [128, 512], F32)
        cx.sbuf("scr512b", [128, 512], F32)
        cx.sbuf("scr512c", [128, 512], F32)
        cx.sbuf("scr512d", [128, 512], F32)
        cx.sbuf("st8", [128, 8], F32)
        cx.sbuf("st8b", [128, 8], F32)
        cx.sbuf("mst", [128, 16], F32)
        cx.sbuf("rst", [128, 16], F32)
        cx.sbuf("mrst", [128, 16], F32)
        emit_consts(cx)

        if 0 in layers:
            cx.scope()
            cx.sbuf("aff_t", [128, 128], F32)
            cx.sbuf("aff_pack", [128, 128], F32)
            cx.sbuf("w_t", [128, 128], F32)
            cx.sbuf("th", [8, 4], F32)
            cx.sbuf("s8", [8, 128], F32)
            cx.sbuf("s8t", [128, 8], F32)
            cx.sbuf("thb", [128, 1], F32)
            cx.sbuf("cnt_col", [128, 1], F32)
            cx.sbuf("ind_sb", [128, 128], BF16)
            cx.sbuf("cgt", [8, 1], F32)
            cx.sbuf("iby16", [128, 2], F32)
            cx.sbuf("iby16i", [128, 1], I32)
            cx.sbuf("iot8", [1, 8], F32)
            cx.sbuf("i8b", [128, 8], F32)
            cx.sbuf("hi_row", [1, 8], F32)
            cx.sbuf("thb_t", [128, 8], F32)
            cx.sbuf("kmrow", [1, 1024], F32)
            cx.sbuf("kmean_b", [128, 1024], BF16)
            cx.sbuf("kv_bd", [128, 1024], BF16)
            cx.scope()
            for c in range(CC):
                cx.sbuf(f"l0xnf{c}", [128, 2048], BF16)
            cx.scope()
            cx.sbuf("l0xnb", [128, 3072], BF16)
            for c in range(CC):
                cx.sbuf(f"l0xnlo{c}", [128, 2048], BF16)
            cx.sbuf("aff_T", [8, 2048], F32)
            cx.sbuf("gate32", [128, 64], F32)
            cx.sbuf("ghi", [128, 64], BF16)
            cx.sbuf("glo", [128, 64], BF16)
            if KSTOP != "consts":
                emit_ln(cx, dr["x"], "l0", with_lo=True,
                        x_dram_xnt=dr["xnt"])
            if KSTOP not in ("consts", "ln0"):
                pump_raw = emit_l0_gate_and_bisect(cx, dr)
            else:
                pump_raw = lambda: None
            cx.close_scope()
            cnt = [0]

            def pump():
                if cnt[0] < 40:
                    cnt[0] += 1
                    pump_raw()

            if KSTOP not in ("ln", "ln0", "consts"):
                emit_layer0(cx, dr, pump)
            cx.dbg_dump("w_t", cx.sb["w_t"][:, :], (128, 128))
            cx.dbg_dump("th", cx.sb["th"][0:8, 0:4], (8, 4))
            cx.close_scope()

        if 1 in layers:
            cx.scope()
            for c in range(CC):
                cx.sbuf(f"l1xnf{c}", [128, 2048], BF16)
            cx.sbuf("l1xnb", [128, 3072], BF16)
            cx.sbuf("o_all", [128, 128], F32)
            cx.sbuf("o_T", [8, 2048], BF16)
            cx.scope()
            for c in range(CC):
                cx.sbuf(f"l1xnlo{c}", [128, 2048], BF16)
            cx.sbuf("tsgb", [8, 1024], BF16)
            cx.sbuf("sgn_f", [128, 64], BF16)
            emit_ln(cx, dr["x1"], "l1", with_lo=True)
            emit_l1_routing(cx, dr)
            cx.close_scope()
            cx.scope()
            cx.sbuf("w1b", [128, 16384], BF16)
            cx.sbuf("b1sb", [8, 1024], BF16)
            cx.sbuf("acc1sb", [128, 16384], F32)
            emit_l1_ffn(cx, dr)
            cx.close_scope()
            for c in range(CC):
                cx.sbuf(f"q_f{c}", [128, 2048], BF16, outer=True)
                cx.sbuf(f"k_f{c}", [128, 2048], BF16, outer=True)
            cx.sbuf("vt1", [128, NT * 1040], BF16, outer=True)
            emit_l1_attention(cx, dr)
            cx.close_scope()
            cx.scope()
            cx.psum("pS", (128, 512), F32)
            cx.psum("pS2", (128, 512), F32)
            cx.psum("pO", (128, 512), F32)
            for c in range(CC):
                cx.sbuf(f"aof{c}", [128, 2048], BF16)
            cx.sbuf("tmp_o", [64, 2048], BF16)
            cx.sbuf("ptb", [128, 2048], BF16)
            cx.sbuf("bcol", [128, 16], F32)
            cx.sbuf("zrow", [128, 2048], F32)
            cx.sbuf("zrow0", [1, 2048], F32)
            emit_l1_attention_core(cx, dr)
            for c in range(CC):
                cx.sbuf(f"owb{c}", [128, 1024], BF16)
            emit_l1_oproj_combine(cx, dr)
            cx.close_scope()

        p.final_wait_all("sync")
        sems = {s: ec(nc.semaphore(s)) for s in p.sem_names}
        block = ec(nc.Block())
        p.build_into(block, sems)
    return nc, dr


_NC_CACHE = {}


def _get_program(key, layers, dbg=(), scratch_out=False):
    if key not in _NC_CACHE:
        _NC_CACHE[key] = build_program(layers, dbg, scratch_out)
    return _NC_CACHE[key]


def make_in_map(inputs, b, layers=(0, 1)):
    m = {}
    if 0 in layers:
        m.update({
            "x": np.asarray(inputs["L0_attn_x"])[b],
            "Wq": inputs["L0_attn_Wq"], "bq": inputs["L0_attn_bq"],
            "Wk": inputs["L0_attn_Wk"], "bk": inputs["L0_attn_bk"],
            "Wv": inputs["L0_attn_Wv"], "bv": inputs["L0_attn_bv"],
            "gamma": inputs["L0_attn_gamma"], "beta": inputs["L0_attn_beta"],
            "gate_w": inputs["L0_ffn_gate_w"],
            "Wg": inputs["L0_ffn_Wg"], "Wu": inputs["L0_ffn_Wu"],
            "Wd": inputs["L0_ffn_Wd"],
        })
    if 1 in layers:
        if 0 not in layers:
            m["x1"] = inputs["x1"][b]
        m.update({
            "qw": inputs["L1_attn_q_w"], "kw": inputs["L1_attn_k_w"],
            "vw": inputs["L1_attn_v_w"], "ow": inputs["L1_attn_o_w"],
            "tsig": inputs["L1_ffn_tile_sigs"],
            "W1": inputs["L1_ffn_W"], "b1": inputs["L1_ffn_b"],
        })
    return {k: np.ascontiguousarray(np.asarray(v), dtype=np.float32)
            for k, v in m.items()}


def kernel(**inputs):
    nc, _ = _get_program("fused", (0, 1))
    B = 8
    in_maps = [make_in_map(inputs, b) for b in range(B)]
    res = run_bass_kernel_spmd(nc, in_maps, core_ids=list(range(B)))
    out = np.stack([res.results[b]["y"] for b in range(B)], axis=0)
    return out.astype(np.float32)
